# version 09
# baseline (speedup 1.0000x reference)
import numpy as np
import ml_dtypes
import concourse.bass as bass
import concourse.mybir as mybir
from concourse.bass_utils import run_bass_kernel_spmd

F32 = mybir.dt.float32
BF16 = mybir.dt.bfloat16
AF = mybir.ActivationFunctionType
ALU = mybir.AluOpType
AX = mybir.AxisListType

H = 4096
DSSM = 8192
NH = 64
P = 128
NST = 128
KCONV = 4
EPS = 1e-5
CONV_DIM = DSSM + 2 * NST
PROJ_DIM = DSSM + CONV_DIM + NH
SEQ = 8192
BATCH = 2
TPC = 2048
Q = 128
NCH = TPC // Q
HALO = KCONV - 1
NEG = -1e30

ZT = 64
XBCT = 66



class Buf:
    __slots__ = ("last_w", "readers", "name")

    def __init__(self, name=""):
        self.last_w = None
        self.readers = []
        self.name = name


class _Op:
    __slots__ = ("fn", "stream", "ordinal", "deps", "inc", "tag")

    def __init__(self, fn, stream, ordinal, deps, tag):
        self.fn = fn
        self.stream = stream
        self.ordinal = ordinal
        self.deps = deps
        self.inc = False
        self.tag = tag


_STREAMS = {
    "pe": ("tensor", False),
    "act": ("scalar", False),
    "dve": ("vector", False),
    "pool": ("gpsimd", False),
    "sp_d": ("sync", True),
    "act_d": ("scalar", True),
    "pool_d": ("gpsimd", True),
}
_BODIES = ("tensor", "scalar", "vector", "gpsimd", "sync")


class Em:
    def __init__(self, nc):
        self.nc = nc
        self.ops = {s: [] for s in _STREAMS}
        self.body_ops = {b: [] for b in _BODIES}
        self.n = {s: 0 for s in _STREAMS}
        self.floor = {}

    def fence(self):
        self.floor = {s: self.n[s] - 1 for s in _STREAMS if self.n[s] > 0}

    def setup_consts(self, st, values):
        pool = st.enter_context(self.nc.sbuf_tensor("constpool", [128, len(values)], F32))
        b = Buf("consts")
        for i, v in enumerate(values):
            ap = pool.ap()[:, i:i + 1]
            self.dve(lambda e, ap=ap, v=v: e.memset(ap, v), writes=[b])
            self.nc.const_aps.register(F32, float(v), ap)
        self.fence()

    def op(self, stream, fn, reads=(), writes=(), tag=""):
        deps = dict(self.floor)
        for b in reads:
            if b.last_w is not None:
                s, o = b.last_w
                if deps.get(s, -1) < o:
                    deps[s] = o
        for b in writes:
            if b.last_w is not None:
                s, o = b.last_w
                if deps.get(s, -1) < o:
                    deps[s] = o
            for s, o in b.readers:
                if deps.get(s, -1) < o:
                    deps[s] = o
        ordn = self.n[stream]
        self.n[stream] += 1
        dl = []
        for s, o in deps.items():
            if s == stream and o >= ordn:
                continue
            if s == stream and o == ordn - 1 and False:
                continue
            if s == "pe" and stream == "pe":
                continue
            dl.append((s, o))
        o_ = _Op(fn, stream, ordn, dl, tag)
        self.ops[stream].append(o_)
        self.body_ops[_STREAMS[stream][0]].append(o_)
        for b in reads:
            b.readers.append((stream, ordn))
        for b in writes:
            b.last_w = (stream, ordn)
            b.readers = []
        return o_

    def pe(self, fn, reads=(), writes=(), tag=""):
        return self.op("pe", fn, reads, writes, tag)

    def act(self, fn, reads=(), writes=(), tag=""):
        return self.op("act", fn, reads, writes, tag)

    def dve(self, fn, reads=(), writes=(), tag=""):
        return self.op("dve", fn, reads, writes, tag)

    def pool(self, fn, reads=(), writes=(), tag=""):
        return self.op("pool", fn, reads, writes, tag)

    def dma(self, queue, out_ap, in_ap, reads=(), writes=(), tag="", **kw):
        return self.op(queue, lambda e: e.dma_start(out_ap, in_ap, **kw),
                       reads, writes, tag)

    def finalize_and_emit(self, final_body="sync"):
        for s, lst in self.ops.items():
            if not lst:
                continue
            if _STREAMS[s][1]:
                for o in lst:
                    o.inc = True
            lst[-1].inc = True
        for s, lst in self.ops.items():
            for o in lst:
                for ds, do in o.deps:
                    self.ops[ds][do].inc = True
        self.val_after = {}
        self.final_val = {}
        for s, lst in self.ops.items():
            step = 16 if _STREAMS[s][1] else 1
            c = 0
            va = []
            for o in lst:
                if o.inc:
                    c += step
                va.append(c)
            self.val_after[s] = va
            self.final_val[s] = c
        nc = self.nc
        import contextlib
        stack = contextlib.ExitStack()
        sems = {s: stack.enter_context(nc.semaphore(f"sem_{s}")) for s in _STREAMS}
        block = stack.enter_context(nc.Block())

        def mk_body(body):
            ops = self.body_ops[body]

            def run(eng):
                watermark = {}
                for o in ops:
                    for ds, do in o.deps:
                        v = self.val_after[ds][do]
                        if v <= 0 or watermark.get(ds, 0) >= v:
                            continue
                        eng.wait_ge(sems[ds], v)
                        watermark[ds] = v
                    inst = o.fn(eng)
                    if o.inc:
                        inst.then_inc(sems[o.stream],
                                      16 if _STREAMS[o.stream][1] else 1)
                if body == final_body:
                    for s in _STREAMS:
                        if s != "sp_d" and _STREAMS[s][0] == body:
                            continue
                        v = self.final_val[s]
                        if v > 0 and watermark.get(s, 0) < v:
                            eng.wait_ge(sems[s], v)
            return run

        for body in _BODIES:
            if not self.body_ops[body] and body != final_body:
                continue
            getattr(block, body)(mk_body(body))
        stack.close()



def host_constants():
    c = {}
    c["ident_bf16"] = np.eye(128, dtype=ml_dtypes.bfloat16)
    c["ident_f32"] = np.eye(128, dtype=np.float32)
    c["ones_f32"] = np.ones((128, 128), dtype=np.float32)
    c["ones_col_bf16"] = np.ones((128, 1), dtype=ml_dtypes.bfloat16)
    t = np.arange(128)
    c["maskW"] = np.where(t[None, :] <= t[:, None], 0.0, NEG).astype(np.float32)
    e = np.zeros((NCH, TPC), dtype=np.float32)
    for k in range(NCH):
        e[k, k * Q:(k + 1) * Q] = 1.0
    c["eye16rep"] = e
    return c


def host_prep(inputs):
    x = np.asarray(inputs["L0_attn_hidden_states"], np.float32)
    shared = dict(host_constants())
    shared["W_in"] = np.ascontiguousarray(inputs["L0_attn_W_in"], dtype=np.float32)
    shared["conv_w"] = np.ascontiguousarray(inputs["L0_attn_conv_w"], dtype=np.float32)
    shared["dt_bias_col"] = np.ascontiguousarray(
        np.asarray(inputs["L0_attn_dt_bias"], np.float32).reshape(NH, 1))
    shared["A_log_col"] = np.ascontiguousarray(
        np.asarray(inputs["L0_attn_A_log"], np.float32).reshape(NH, 1))
    shared["D_rep"] = np.ascontiguousarray(
        np.broadcast_to(np.asarray(inputs["L0_attn_D_param"], np.float32), (128, NH)))
    shared["nw_col"] = np.ascontiguousarray(
        np.asarray(inputs["L0_attn_norm_weight"], np.float32).reshape(DSSM // 128, 128).T)
    shared["W_out"] = np.ascontiguousarray(inputs["L0_attn_W_out"], dtype=np.float32)
    shared["w12T"] = np.ascontiguousarray(np.asarray(inputs["L0_ffn_w12"], np.float32).T)
    shared["b12_col"] = np.ascontiguousarray(
        np.asarray(inputs["L0_ffn_b12"], np.float32).reshape(DSSM // 128, 128).T)
    shared["w3T"] = np.ascontiguousarray(np.asarray(inputs["L0_ffn_w3"], np.float32).T)
    shared["b3_col"] = np.ascontiguousarray(
        np.asarray(inputs["L0_ffn_b3"], np.float32).reshape(H // 128, 128).T)
    percore = []
    for core in range(8):
        b, q = core // 4, core % 4
        d = {}
        xt = np.zeros((HALO + TPC, H), np.float32)
        s0 = q * TPC
        xt[HALO:] = x[b, s0:s0 + TPC]
        if q > 0:
            xt[:HALO] = x[b, s0 - HALO:s0]
        d["xt"] = np.ascontiguousarray(xt)
        rc = np.zeros((128, 8), np.float32)
        for r in range(4):
            if r < q:
                rc[:, r] = 1.0
                rc[:, 4 + r] = 1.0
        d["relay_cols"] = rc
        percore.append(d)
    return shared, percore



def build_phase_a(em, dram, st):
    nc = em.nc
    sb = lambda name, shape, dt: st.enter_context(nc.sbuf_tensor(name, shape, dt))

    ident = sb("identA", [128, 128], BF16)
    b_ident = Buf("ident")
    em.dma("sp_d", ident.ap(), dram["ident_bf16"], writes=[b_ident])
    identf = sb("identfA", [128, 128], F32)
    b_identf = Buf("identf")
    em.dma("sp_d", identf.ap(), dram["ident_f32"], writes=[b_identf])

    x1T = sb("x1T", [128, 32, HALO + TPC], BF16)
    b_x1T = [Buf(f"x1T{i}") for i in range(17)]

    ntile = 17
    import contextlib
    with contextlib.ExitStack() as st1:
        sb1 = lambda name, shape, dt: st1.enter_context(nc.sbuf_tensor(name, shape, dt))
        ps1 = lambda name, shape, dt: st1.enter_context(nc.psum_tensor(name, shape, dt))
        xa = [sb1(f"xa{i}", [128, H], F32) for i in range(2)]
        b_xa = [Buf(f"xa{i}") for i in range(2)]
        x1t = [sb1(f"x1t{i}", [128, H], BF16) for i in range(2)]
        b_x1t = [Buf(f"x1t{i}") for i in range(2)]
        sq = sb1("sq", [128, H], BF16)
        b_sq = Buf("sq")
        stats = sb1("statsA", [128, 8], F32)
        b_st = Buf("stats")
        tps = ps1("tpsA", [128, 2048], BF16)
        b_tps = Buf("tps")
        for it in range(ntile):
            r0 = it * 128
            nr = min(128, HALO + TPC - r0)
            pa = it % 2
            xa_t, b_xa_t = xa[pa], b_xa[pa]
            x1_t, b_x1_t = x1t[pa], b_x1t[pa]
            em.dma("sp_d", xa_t.ap()[0:nr, :], dram["xt"][r0:r0 + nr, :],
                   writes=[b_xa_t], tag=f"ldxa{it}")
            em.dve(lambda e, xa_t=xa_t, nr=nr: e.tensor_reduce(
                stats.ap()[0:nr, 0:1], xa_t.ap()[0:nr, :], AX.X, ALU.add),
                reads=[b_xa_t], writes=[b_st])
            em.act(lambda e, xa_t=xa_t, nr=nr: e.activation(
                sq.ap()[0:nr, :], xa_t.ap()[0:nr, :], AF.Square,
                accum_out=stats.ap()[0:nr, 1:2]),
                reads=[b_xa_t], writes=[b_sq, b_st])
            em.dve(lambda e, nr=nr: e.tensor_scalar(
                stats.ap()[0:nr, 2:3], stats.ap()[0:nr, 0:1], 1.0 / H, None, ALU.mult),
                reads=[b_st], writes=[b_st])
            em.dve(lambda e, nr=nr: e.tensor_scalar(
                stats.ap()[0:nr, 3:4], stats.ap()[0:nr, 1:2], 1.0 / H, None, ALU.mult),
                reads=[b_st], writes=[b_st])
            em.dve(lambda e, nr=nr: e.tensor_mul(
                stats.ap()[0:nr, 5:6], stats.ap()[0:nr, 2:3], stats.ap()[0:nr, 2:3]),
                reads=[b_st], writes=[b_st])
            em.dve(lambda e, nr=nr: e.tensor_sub(
                stats.ap()[0:nr, 3:4], stats.ap()[0:nr, 3:4], stats.ap()[0:nr, 5:6]),
                reads=[b_st], writes=[b_st])
            em.act(lambda e, nr=nr: e.activation(
                stats.ap()[0:nr, 5:6], stats.ap()[0:nr, 3:4], AF.Ln, bias=EPS),
                reads=[b_st], writes=[b_st])
            em.act(lambda e, nr=nr: e.activation(
                stats.ap()[0:nr, 4:5], stats.ap()[0:nr, 5:6], AF.Exp, scale=-0.5),
                reads=[b_st], writes=[b_st])
            em.dve(lambda e, xa_t=xa_t, x1_t=x1_t, nr=nr: e.tensor_scalar(
                x1_t.ap()[0:nr, :], xa_t.ap()[0:nr, :],
                stats.ap()[0:nr, 2:3], stats.ap()[0:nr, 4:5],
                ALU.subtract, ALU.mult),
                reads=[b_xa_t, b_st], writes=[b_x1_t])
            for g in range(2):
                for j in range(16):
                    dblk = g * 16 + j
                    em.pe(lambda e, x1_t=x1_t, dblk=dblk, j=j, nr=nr: e.transpose(
                        tps.ap()[:, j * 128:j * 128 + nr],
                        x1_t.ap()[0:nr, dblk * 128:(dblk + 1) * 128],
                        ident.ap()[0:nr, 0:nr]),
                        reads=[b_x1_t, b_ident], writes=[b_tps])
                em.dve(lambda e, g=g, r0=r0, nr=nr: e.tensor_copy(
                    x1T.ap()[:, g * 16:(g + 1) * 16, r0:r0 + nr],
                    tps.ap().rearrange("p (j t) -> p j t", j=16)[:, :, 0:nr]),
                    reads=[b_tps], writes=[b_x1T[it]], tag=f"x1Tev{it}_{g}")
        em.fence()

    dt_scope = contextlib.ExitStack()
    with dt_scope as st2:
        sb2 = lambda name, shape, dt: st2.enter_context(nc.sbuf_tensor(name, shape, dt))
        ps2 = lambda name, shape, dt: st2.enter_context(nc.psum_tensor(name, shape, dt))
        wdt = sb2("wdt", [128, 32, NH], BF16)
        b_wdt = Buf("wdt")
        src = dram["W_in"].rearrange("(i p) c -> p i c", p=128)[:, :, PROJ_DIM - NH:]
        for piece in range(4):
            em.dma("pool_d", wdt.ap()[:, piece * 8:(piece + 1) * 8, :],
                   src[:, piece * 8:(piece + 1) * 8, :], writes=[b_wdt])
        dtps = ps2("dtps", [128, NH], F32)
        b_dtps = Buf("dtps")
        dtsb = sb2("dtsb", [128, NH], F32)
        b_dtsb = Buf("dtsb")
        dtp2 = ps2("dtp2", [NH, 128], F32)
        b_dtp2 = Buf("dtp2")
        dtraw = sb2("dtraw", [NH, TPC], F32)
        b_dtraw = Buf("dtraw")
        for ch in range(NCH):
            t0 = HALO + ch * Q
            for d in range(32):
                em.pe(lambda e, d=d, t0=t0: e.matmul(
                    dtps.ap(), x1T.ap()[:, d, t0:t0 + Q], wdt.ap()[:, d, :],
                    start=(d == 0), stop=(d == 31)),
                    reads=[b_x1T[t0 // 128], b_x1T[t0 // 128 + 1], b_wdt],
                    writes=[b_dtps])
            em.dve(lambda e: e.tensor_copy(dtsb.ap(), dtps.ap()),
                   reads=[b_dtps], writes=[b_dtsb])
            em.pe(lambda e: e.transpose(dtp2.ap(), dtsb.ap(), identf.ap()),
                  reads=[b_dtsb, b_identf], writes=[b_dtp2])
            em.dve(lambda e, ch=ch: e.tensor_copy(
                dtraw.ap()[:, ch * Q:(ch + 1) * Q], dtp2.ap()),
                reads=[b_dtp2], writes=[b_dtraw])
        dtb = sb2("dtb", [NH, 1], F32)
        b_dtb = Buf("dtb")
        em.dma("sp_d", dtb.ap(), dram["dt_bias_col"], writes=[b_dtb])
        t_abs = sb2("t_abs", [NH, TPC], F32)
        b_abs = Buf("t_abs")
        dt_soft = sb2("dt_softA", [NH, TPC], F32)
        b_dts = Buf("dt_soft")
        em.act(lambda e: e.activation(t_abs.ap(), dtraw.ap(), AF.Abs, bias=dtb.ap()),
               reads=[b_dtraw, b_dtb], writes=[b_abs])
        em.act(lambda e: e.activation(t_abs.ap(), t_abs.ap(), AF.Exp, scale=-1.0),
               reads=[b_abs], writes=[b_abs])
        em.act(lambda e: e.activation(t_abs.ap(), t_abs.ap(), AF.Ln, bias=1.0),
               reads=[b_abs], writes=[b_abs])
        em.act(lambda e: e.activation(dt_soft.ap(), dtraw.ap(), AF.Relu, bias=dtb.ap()),
               reads=[b_dtraw, b_dtb], writes=[b_dts])
        em.dve(lambda e: e.tensor_add(dt_soft.ap(), dt_soft.ap(), t_abs.ap()),
               reads=[b_abs, b_dts], writes=[b_dts])
        em.dma("sp_d", dram["dt_soft_hbm"], dt_soft.ap(), reads=[b_dts],
               tag="st_dtsoft")
        em.fence()

    cw = sb("cwA", [128, XBCT, KCONV], F32)
    b_cw = Buf("cw")
    em.dma("sp_d", cw.ap(), dram["conv_w"].rearrange("(i p) k -> p i k", p=128),
           writes=[b_cw])

    wpan = [sb(f"wpanA{i}", [128, 32, 128], BF16) for i in range(2)]
    b_wpan = [Buf(f"wpan{i}") for i in range(2)]

    def load_wpanel(c, pa):
        src = dram["W_in"].rearrange("(i p) c -> p i c", p=128)[:, :, c * 128:(c + 1) * 128]
        for piece in range(4):
            em.dma("pool_d", wpan[pa].ap()[:, piece * 8:(piece + 1) * 8, :],
                   src[:, piece * 8:(piece + 1) * 8, :],
                   writes=[b_wpan[pa]], tag=f"ldw{c}_{piece}")

    with contextlib.ExitStack() as st4:
        sb4 = lambda name, shape, dt: st4.enter_context(nc.sbuf_tensor(name, shape, dt))
        ps4 = lambda name, shape, dt: st4.enter_context(nc.psum_tensor(name, shape, dt))
        xps = ps4("xps", [128, 2052], F32)
        b_xps = Buf("xps")
        xbc = [sb4(f"xbc{i}", [128, HALO + 2048], BF16) for i in range(2)]
        b_xbc_lo = [Buf(f"xbcl{i}") for i in range(2)]
        b_xbc_hi = [Buf(f"xbch{i}") for i in range(2)]
        cv = [sb4(f"cv{i}", [128, 2048], BF16) for i in range(4)]
        b_cv = [Buf(f"cv{i}") for i in range(4)]
        xout = [sb4(f"xout{i}", [128, 2048], BF16) for i in range(2)]
        b_xout = [Buf(f"xout{i}") for i in range(2)]
        offs = [0, 512, 1024, 1536, 2048, 2051]
        for ci in range(XBCT):
            c = ZT + ci
            pa = c % 2
            load_wpanel(c, pa)
            for d in range(32):
                for m in range(5):
                    em.pe(lambda e, pa=pa, d=d, m=m: e.matmul(
                        xps.ap()[:, offs[m]:offs[m + 1]],
                        wpan[pa].ap()[:, d, :],
                        x1T.ap()[:, d, offs[m]:offs[m + 1]],
                        start=(d == 0), stop=(d == 31)),
                        reads=[b_wpan[pa]] + b_x1T, writes=[b_xps])
            em.dve(lambda e, pa=pa: e.tensor_copy(
                xbc[pa].ap()[:, 0:1024], xps.ap()[:, 0:1024]),
                reads=[b_xps], writes=[b_xbc_lo[pa]])
            em.act(lambda e, pa=pa: e.activation(
                xbc[pa].ap()[:, 1024:2051], xps.ap()[:, 1024:2051], AF.Copy),
                reads=[b_xps], writes=[b_xbc_hi[pa]])
            if "xbcdbg_hbm" in dram:
                em.dma("act_d", dram["xbcdbg_hbm"][ci * 128:(ci + 1) * 128, :],
                       xbc[pa].ap(), reads=[b_xbc_lo[pa], b_xbc_hi[pa]])
            rbc = [b_xbc_lo[pa], b_xbc_hi[pa], b_cw]
            em.act(lambda e, pa=pa, ci=ci: e.activation(
                cv[0].ap(), xbc[pa].ap()[:, 0:2048], AF.Identity,
                scale=cw.ap()[:, ci, 0:1]),
                reads=rbc, writes=[b_cv[0]])
            em.act(lambda e, pa=pa, ci=ci: e.activation(
                cv[1].ap(), xbc[pa].ap()[:, 1:2049], AF.Identity,
                scale=cw.ap()[:, ci, 1:2]),
                reads=rbc, writes=[b_cv[1]])
            em.dve(lambda e, pa=pa, ci=ci: e.tensor_scalar(
                cv[2].ap(), xbc[pa].ap()[:, 2:2050], cw.ap()[:, ci, 2:3],
                None, ALU.mult),
                reads=rbc, writes=[b_cv[2]])
            em.dve(lambda e, pa=pa, ci=ci: e.tensor_scalar(
                cv[3].ap(), xbc[pa].ap()[:, 3:2051], cw.ap()[:, ci, 3:4],
                None, ALU.mult),
                reads=rbc, writes=[b_cv[3]])
            em.dve(lambda e: e.tensor_add(cv[0].ap(), cv[0].ap(), cv[1].ap()),
                   reads=[b_cv[0], b_cv[1]], writes=[b_cv[0]])
            em.dve(lambda e: e.tensor_add(cv[2].ap(), cv[2].ap(), cv[3].ap()),
                   reads=[b_cv[2], b_cv[3]], writes=[b_cv[2]])
            em.dve(lambda e: e.tensor_add(cv[0].ap(), cv[0].ap(), cv[2].ap()),
                   reads=[b_cv[0], b_cv[2]], writes=[b_cv[0]])
            em.act(lambda e, pa=pa: e.activation(xout[pa].ap(), cv[0].ap(), AF.Silu),
                   reads=[b_cv[0]], writes=[b_xout[pa]])
            em.dma("sp_d", dram["xssm_hbm"][ci * 128:(ci + 1) * 128, :],
                   xout[pa].ap(), reads=[b_xout[pa]], tag=f"stxs{ci}")
        em.fence()


    with contextlib.ExitStack() as st3:
        sb3 = lambda name, shape, dt: st3.enter_context(nc.sbuf_tensor(name, shape, dt))
        ps3 = lambda name, shape, dt: st3.enter_context(nc.psum_tensor(name, shape, dt))
        zps = [ps3(f"zps{i}", [128, 2048], F32) for i in range(2)]
        b_zps = [Buf(f"zps{i}") for i in range(2)]
        zsb = [sb3(f"zsb{i}", [128, 2048], BF16) for i in range(2)]
        b_zsb_lo = [Buf(f"zsbl{i}") for i in range(2)]
        b_zsb_hi = [Buf(f"zsbh{i}") for i in range(2)]
        for c in range(ZT):
            pa = c % 2
            load_wpanel(c, pa)
            for d in range(32):
                for m in range(4):
                    em.pe(lambda e, pa=pa, d=d, m=m: e.matmul(
                        zps[pa].ap()[:, m * 512:(m + 1) * 512],
                        wpan[pa].ap()[:, d, :],
                        x1T.ap()[:, d, HALO + m * 512:HALO + (m + 1) * 512],
                        start=(d == 0), stop=(d == 31)),
                        reads=[b_wpan[pa]] + b_x1T, writes=[b_zps[pa]])
            em.dve(lambda e, pa=pa: e.tensor_copy(
                zsb[pa].ap()[:, 0:1024], zps[pa].ap()[:, 0:1024]),
                reads=[b_zps[pa]], writes=[b_zsb_lo[pa]])
            em.act(lambda e, pa=pa: e.activation(
                zsb[pa].ap()[:, 1024:2048], zps[pa].ap()[:, 1024:2048], AF.Copy),
                reads=[b_zps[pa]], writes=[b_zsb_hi[pa]])
            em.dma("sp_d", dram["z_hbm"][c * 128:(c + 1) * 128, :], zsb[pa].ap(),
                   reads=[b_zsb_lo[pa], b_zsb_hi[pa]], tag=f"stz{c}")
        em.fence()


_DT_NP2MY = {np.dtype(np.float32): F32, np.dtype(ml_dtypes.bfloat16): BF16}
BF = ml_dtypes.bfloat16

IO_A = dict(
    ins=[("xt", (HALO + TPC, H), np.float32),
         ("W_in", (H, PROJ_DIM), np.float32),
         ("conv_w", (CONV_DIM, KCONV), np.float32),
         ("dt_bias_col", (NH, 1), np.float32),
         ("ident_bf16", (128, 128), BF),
         ("ident_f32", (128, 128), np.float32)],
    outs=[("z_hbm", (DSSM, TPC), BF),
          ("xssm_hbm", (CONV_DIM, TPC), BF),
          ("dt_soft_hbm", (NH, TPC), np.float32)])


def build_scan(em, dram, st, summary_only):
    nc = em.nc
    import contextlib
    sb = lambda name, shape, dt: st.enter_context(nc.sbuf_tensor(name, shape, dt))
    ps = lambda name, shape, dt: st.enter_context(nc.psum_tensor(name, shape, dt))
    tag = "B1" if summary_only else "B2"

    identf = sb("identfB", [128, 128], F32)
    b_identf = Buf("identf")
    em.dma("sp_d", identf.ap(), dram["ident_f32"], writes=[b_identf])
    maskW = sb("maskWB", [128, 128], F32)
    b_mask = Buf("maskW")
    em.dma("sp_d", maskW.ap(), dram["maskW"], writes=[b_mask])

    negall = sb("negallB", [128, NCH, NH], F32)
    b_negall = Buf("negall")
    biasall = sb("biasallB", [128, NCH, NH], F32)
    b_biasall = Buf("biasall")
    dwall = sb("dwallB", [128, NCH, NH], BF16)
    b_dwall = Buf("dwall")
    decCall = sb("decCallB", [128, NCH, NH], F32)
    b_decCall = Buf("decCall")
    eTall = sb("eTallB", [128, NCH, NH], F32)
    b_eTall = Buf("eTall")

    with contextlib.ExitStack() as stp:
        sbp = lambda name, shape, dt: stp.enter_context(nc.sbuf_tensor(name, shape, dt))
        psP = stp.enter_context(nc.psum_tensor("psPB", [128, 128], F32))
        b_psP = Buf("psP")
        dts = sbp("dtsB", [NH, TPC], F32)
        b_dts = Buf("dts")
        em.dma("sp_d", dts.ap(), dram["dt_soft_hbm"], writes=[b_dts])
        acol = sbp("acolB", [NH, 2], F32)
        b_acol = Buf("acol")
        em.dma("sp_d", acol.ap()[:, 0:1], dram["A_log_col"], writes=[b_acol])
        em.act(lambda e: e.activation(acol.ap()[:, 1:2], acol.ap()[:, 0:1], AF.Exp,
                                      scale=1.0), reads=[b_acol], writes=[b_acol])
        em.dve(lambda e: e.tensor_scalar(acol.ap()[:, 1:2], acol.ap()[:, 1:2], -1.0,
                                         None, ALU.mult), reads=[b_acol],
               writes=[b_acol])
        av = sbp("avB", [NH, TPC], F32)
        b_av = Buf("av")
        em.dve(lambda e: e.tensor_scalar(av.ap(), dts.ap(), acol.ap()[:, 1:2], None,
                                         ALU.mult), reads=[b_dts, b_acol],
               writes=[b_av])
        ones64 = sbp("ones64B", [NH, Q], F32)
        b_ones64 = Buf("ones64")
        em.dve(lambda e: e.memset(ones64.ap(), 1.0), writes=[b_ones64])
        clocal = sbp("clocalB", [NH, TPC], F32)
        b_cl = Buf("clocal")
        for ch in range(NCH):
            em.dve(lambda e, ch=ch: e.tensor_tensor_scan(
                clocal.ap()[:, ch * Q:(ch + 1) * Q], ones64.ap(),
                av.ap()[:, ch * Q:(ch + 1) * Q], 0.0, ALU.mult, ALU.add),
                reads=[b_av, b_ones64], writes=[b_cl])
        totsb = sbp("totsbB", [NH, NCH], F32)
        b_tot = Buf("totsb")
        em.dve(lambda e: e.tensor_copy(totsb.ap(), clocal.ap()[:, Q - 1::Q]),
               reads=[b_cl], writes=[b_tot])
        totT = sbp("totTB", [NCH, NH], F32)
        b_totT = Buf("totT")
        decT = sbp("decTB", [NCH, NH], F32)
        b_decT = Buf("decT")
        em.pe(lambda e: e.transpose(psP.ap()[0:NCH, 0:NH], totsb.ap(),
                                    identf.ap()[0:NH, 0:NH]),
              reads=[b_tot, b_identf], writes=[b_psP])
        em.dve(lambda e: e.tensor_copy(totT.ap(), psP.ap()[0:NCH, 0:NH]),
               reads=[b_psP], writes=[b_totT])
        em.act(lambda e: e.activation(decT.ap(), totT.ap(), AF.Exp),
               reads=[b_totT], writes=[b_decT])
        eye16 = sbp("eye16B", [NCH, TPC], F32)
        b_eye16 = Buf("eye16")
        em.dma("sp_d", eye16.ap(), dram["eye16rep"], writes=[b_eye16])
        if summary_only:
            dtot = sbp("dtotB", [NH, 2], F32)
            b_dtot = Buf("dtot")
            em.dve(lambda e: e.tensor_reduce(dtot.ap()[:, 0:1], totsb.ap(), AX.X,
                                             ALU.add), reads=[b_tot],
                   writes=[b_dtot])
            em.act(lambda e: e.activation(dtot.ap()[:, 1:2], dtot.ap()[:, 0:1],
                                          AF.Exp), reads=[b_dtot], writes=[b_dtot])
            if "cc_D_in" in dram:
                drow_o = sbp("drowoB", [1, NH], F32)
                b_drow_o = Buf("drow_o")
                em.pe(lambda e: e.transpose(psP.ap()[0:1, 0:NH], dtot.ap()[:, 1:2],
                                            identf.ap()[0:NH, 0:NH]),
                      reads=[b_dtot, b_identf], writes=[b_psP])
                em.dve(lambda e: e.tensor_copy(drow_o.ap(), psP.ap()[0:1, 0:NH]),
                       reads=[b_psP], writes=[b_drow_o])
                em.dma("sp_d", dram["cc_D_in"], drow_o.ap(), reads=[b_drow_o])
            else:
                em.dma("sp_d", dram["Dt_col"], dtot.ap()[:, 1:2], reads=[b_dtot])
        wtmp = sbp("wtmpB", [128, 2 * NH], F32)
        b_wtmp = Buf("wtmp")
        dtf = sbp("dtfB", [128, NH], F32)
        b_dtf = Buf("dtf")
        lnt = sbp("lntB", [128, NH], F32)
        b_lnt = Buf("lnt")
        for ch in range(NCH):
            csl = slice(ch * Q, (ch + 1) * Q)
            em.pe(lambda e, csl=csl: e.transpose(
                psP.ap()[:, 0:NH], clocal.ap()[:, csl], identf.ap()[0:NH, 0:NH]),
                reads=[b_cl, b_identf], writes=[b_psP])
            em.dve(lambda e, ch=ch: e.tensor_scalar(
                negall.ap()[:, ch, :], psP.ap()[:, 0:NH], -1.0, None, ALU.mult),
                reads=[b_psP], writes=[b_negall])
            em.pe(lambda e, csl=csl: e.transpose(
                psP.ap()[:, 0:NH], dts.ap()[:, csl], identf.ap()[0:NH, 0:NH]),
                reads=[b_dts, b_identf, b_negall], writes=[b_psP])
            em.dve(lambda e: e.tensor_copy(dtf.ap(), psP.ap()[:, 0:NH]),
                   reads=[b_psP], writes=[b_dtf])
            em.act(lambda e: e.activation(lnt.ap(), dtf.ap(), AF.Ln),
                   reads=[b_dtf], writes=[b_lnt])
            em.dve(lambda e, ch=ch: e.tensor_add(
                biasall.ap()[:, ch, :], negall.ap()[:, ch, :], lnt.ap()),
                reads=[b_negall, b_lnt], writes=[b_biasall])
            em.act(lambda e, ch=ch: e.activation(
                eTall.ap()[:, ch, :], negall.ap()[:, ch, :], AF.Exp, scale=-1.0),
                reads=[b_negall], writes=[b_eTall])
            em.pe(lambda e, csl=csl: e.matmul(
                psP.ap()[:, 0:NH], eye16.ap()[:, csl], totT.ap(),
                start=True, stop=True),
                reads=[b_eye16, b_totT, b_lnt], writes=[b_psP])
            em.dve(lambda e, ch=ch: e.tensor_add(
                wtmp.ap()[:, 0:NH], psP.ap()[:, 0:NH], negall.ap()[:, ch, :]),
                reads=[b_psP, b_negall], writes=[b_wtmp])
            em.act(lambda e: e.activation(wtmp.ap()[:, NH:2 * NH],
                                          wtmp.ap()[:, 0:NH], AF.Exp),
                   reads=[b_wtmp], writes=[b_wtmp])
            em.pool(lambda e, ch=ch: e.tensor_tensor(
                dwall.ap()[:, ch, :], dtf.ap(),
                wtmp.ap()[:, NH:2 * NH], ALU.mult),
                reads=[b_wtmp, b_dtf], writes=[b_dwall])
            em.pe(lambda e, csl=csl: e.matmul(
                psP.ap()[:, 0:NH], eye16.ap()[:, csl], decT.ap(),
                start=True, stop=True),
                reads=[b_eye16, b_decT, b_wtmp], writes=[b_psP])
            em.dve(lambda e, ch=ch: e.tensor_copy(
                decCall.ap()[:, ch, :], psP.ap()[:, 0:NH]),
                reads=[b_psP], writes=[b_decCall])
        em.fence()

    Bt = sb("BtB", [128, TPC], BF16)
    b_Bt = Buf("Bt")
    em.dma("sp_d", Bt.ap(), dram["xssm_hbm"][DSSM:DSSM + NST, :], writes=[b_Bt])
    Ct = sb("CtB", [128, TPC], BF16)
    b_Ct = Buf("Ct")
    em.dma("sp_d", Ct.ap(), dram["xssm_hbm"][DSSM + NST:, :], writes=[b_Ct])

    hst = sb("hstB", [128, DSSM], F32)
    b_h = [Buf(f"h{g}") for g in range(8)]
    if summary_only:
        psS_ = [ps(f"psSB{i}", [128, 1024], F32) for i in range(2)]
        b_psS_ = [Buf(f"psS{i}") for i in range(2)]
    else:
        psS_ = [ps("psSB", [128, 1024], F32)] * 2
        b_psS_ = [Buf("psS")] * 2
    psS, b_psS = psS_[0], b_psS_[0]
    if summary_only:
        em.dve(lambda e: e.memset(hst.ap(), 0.0), writes=b_h)
        hbf, b_hbf = None, None
    else:
        hbf = sb("hbfB", [128, DSSM], BF16)
        b_hbf = [Buf(f"hbf{g}") for g in range(8)]
        onesf = sb("onesfB", [1, 128], F32)
        b_onesf = Buf("onesf")
        em.dve(lambda e: e.memset(onesf.ap(), 1.0), writes=[b_onesf])
        drow = sb("drowB", [1, 3 * NH], F32)
        b_drow = Buf("drow")
        with contextlib.ExitStack() as sti:
            stage = sti.enter_context(nc.sbuf_tensor("stageB", [128, DSSM], F32))
            b_stage = Buf("stage")
            em.dma("sp_d", stage.ap(), dram["S_in0"], writes=[b_stage])
            em.dve(lambda e: e.tensor_copy(hst.ap(), stage.ap()),
                   reads=[b_stage], writes=b_h)
            for j in (1, 2):
                em.dma("sp_d", drow.ap()[:, (j - 1) * NH:j * NH],
                       dram[f"Dt_row{j}"], writes=[b_drow])
                em.pe(lambda e, j=j: e.matmul(
                    psS.ap()[:, 0:NH], onesf.ap(), drow.ap()[:, (j - 1) * NH:j * NH],
                    start=True, stop=True),
                    reads=[b_drow, b_onesf], writes=[b_psS])
                em.dve(lambda e: e.tensor_tensor(
                    hst.ap().rearrange("p (h q) -> p h q", q=P),
                    hst.ap().rearrange("p (h q) -> p h q", q=P),
                    psS.ap()[:, 0:NH].unsqueeze(2).broadcast_to([128, NH, P]),
                    ALU.mult),
                    reads=[b_psS] + b_h, writes=b_h)
                em.dma("sp_d", stage.ap(), dram[f"S_in{j}"], writes=[b_stage])
                em.dve(lambda e: e.tensor_add(hst.ap(), hst.ap(), stage.ap()),
                       reads=[b_stage] + b_h, writes=b_h)
            for g in range(8):
                em.act(lambda e, g=g: e.activation(
                    hbf.ap()[:, g * 1024:(g + 1) * 1024],
                    hst.ap()[:, g * 1024:(g + 1) * 1024], AF.Copy),
                    reads=[b_h[g]], writes=[b_hbf[g]])
            em.fence()

    xT = [sb(f"xTB{i}", [128, CONV_DIM], BF16) for i in range(2)]
    b_xT = [Buf(f"xT{i}") for i in range(2)]
    dtx2 = sb("dtx2B", [128, DSSM], BF16)
    b_dtx2 = Buf("dtx2")
    htmp = sb("htmpB", [128, 1024], F32)
    b_htmp = Buf("htmp")

    if not summary_only:
        ddiag = sb("ddiagB", [128, NH, 128], BF16)
        b_ddiag = Buf("ddiag")
        identb2 = sb("identb2B", [128, 128], BF16)
        b_identb2 = Buf("identb2")
        em.dma("sp_d", identb2.ap(), dram["ident_bf16"], writes=[b_identb2])
        drepf = sb("drepfB", [128, NH], F32)
        b_drepf = Buf("drepf")
        em.dma("sp_d", drepf.ap(), dram["D_rep"], writes=[b_drepf])
        for hh in range(NH):
            em.dve(lambda e, hh=hh: e.tensor_scalar(
                ddiag.ap()[:, hh, :], identb2.ap(), drepf.ap()[:, hh:hh + 1],
                None, ALU.mult),
                reads=[b_identb2, b_drepf], writes=[b_ddiag])
        Wsb = [sb("WsbB0", [128, 1024], F32)] * 2
        b_Wsb = [Buf("Wsb")] * 2
        MtA = [sb(f"MtAB{i}", [128, 8, 1024], BF16) for i in range(2)]
        b_MtA = [[Buf(f"MtA{i}_{g}") for g in range(8)] for i in range(2)]
        Gtb2 = [sb(f"GtbB{i}", [128, 128], BF16) for i in range(2)]
        b_Gtb2 = [Buf(f"Gtb{i}") for i in range(2)]
        ysb = sb("ysbB", [128, 1024], F32)
        b_ysb = Buf("ysb")
        yo2 = [sb(f"yo2B{i}", [128, 1024], BF16) for i in range(2)]
        b_yo2 = [Buf(f"yo2{i}") for i in range(2)]
        psE = ps("psEB", [128, 1024], F32)
        b_psE = Buf("psE")
        psI = ps("psIB", [128, 1024], F32)
        b_psI = Buf("psI")
        psN = ps("psNB", [128, 1024], F32)
        b_psN = Buf("psN")

    def b2_gt(ch):
        cpp = ch % 2
        tc = slice(ch * Q, (ch + 1) * Q)
        em.pe(lambda e, tc=tc: e.matmul(
            psE.ap()[:, 0:128], Bt.ap()[:, tc], Ct.ap()[:, tc],
            start=True, stop=True),
            reads=[b_Bt, b_Ct], writes=[b_psE])
        em.dve(lambda e, cpp=cpp: e.tensor_copy(Gtb2[cpp].ap(), psE.ap()[:, 0:128]),
               reads=[b_psE], writes=[b_Gtb2[cpp]])

    def b2_prod_W(ch, g):
        pp = g % 2
        em.pool(lambda e, g=g, ch=ch, pp=pp: e.tensor_tensor(
            Wsb[pp].ap().rearrange("p (h s) -> p h s", s=Q),
            maskW.ap().unsqueeze(1).broadcast_to([128, 8, Q]),
            negall.ap()[:, ch, g * 8:(g + 1) * 8].unsqueeze(2)
                .broadcast_to([128, 8, Q]),
            ALU.subtract),
            reads=[b_negall, b_mask], writes=[b_Wsb[pp]])

    def b2_prod_tp(ch, g):
        pp = g % 2
        for j in range(8):
            em.pe(lambda e, j=j, pp=pp: e.transpose(
                psE.ap()[:, j * Q:(j + 1) * Q],
                Wsb[pp].ap()[:, j * Q:(j + 1) * Q], identf.ap()),
                reads=[b_Wsb[pp], b_identf], writes=[b_psE])

    def b2_prod_exp(ch, g):
        cpp = ch % 2
        for j in range(8):
            em.act(lambda e, g=g, j=j, ch=ch, cpp=cpp: e.activation(
                MtA[cpp].ap()[:, g, j * Q:(j + 1) * Q],
                psE.ap()[:, j * Q:(j + 1) * Q], AF.Exp,
                bias=biasall.ap()[:, ch, g * 8 + j:g * 8 + j + 1]),
                reads=[b_psE, b_biasall], writes=[b_MtA[cpp][g]])
        em.pool(lambda e, g=g, cpp=cpp: e.tensor_tensor(
            MtA[cpp].ap()[:, g, :].rearrange("p (h s) -> p h s", s=Q),
            MtA[cpp].ap()[:, g, :].rearrange("p (h s) -> p h s", s=Q),
            Gtb2[cpp].ap().unsqueeze(1).broadcast_to([128, 8, Q]),
            ALU.mult),
            reads=[b_Gtb2[cpp]], writes=[b_MtA[cpp][g]])

    def b2_dtx(ch):
        pa = ch % 2
        em.dve(lambda e, pa=pa, ch=ch: e.tensor_tensor(
            dtx2.ap().rearrange("p (h q) -> p h q", q=P),
            xT[pa].ap()[:, 0:DSSM].rearrange("p (h q) -> p h q", q=P),
            dwall.ap()[:, ch, :].unsqueeze(2).broadcast_to([128, NH, P]),
            ALU.mult),
            reads=[b_xT[pa], b_dwall], writes=[b_dtx2])

    def b2_cons_mm(ch, g):
        cpp = ch % 2
        pa = ch % 2
        tc = slice(ch * Q, (ch + 1) * Q)
        for m in range(2):
            em.pe(lambda e, g=g, m=m, tc=tc: e.matmul(
                psN.ap()[:, m * 512:(m + 1) * 512],
                Ct.ap()[:, tc],
                hbf.ap()[:, g * 1024 + m * 512:g * 1024 + (m + 1) * 512],
                start=True, stop=True),
                reads=[b_Ct, b_hbf[g]], writes=[b_psN])
        for j in range(8):
            em.pe(lambda e, g=g, j=j, cpp=cpp, pa=pa: e.matmul(
                psI.ap()[:, j * Q:(j + 1) * Q],
                MtA[cpp].ap()[:, g, j * Q:(j + 1) * Q],
                xT[pa].ap()[:, (g * 8 + j) * P:(g * 8 + j + 1) * P],
                start=True, stop=False),
                reads=[b_MtA[cpp][g], b_xT[pa]], writes=[b_psI])
            em.pe(lambda e, g=g, j=j, pa=pa: e.matmul(
                psI.ap()[:, j * Q:(j + 1) * Q],
                ddiag.ap()[:, g * 8 + j, :],
                xT[pa].ap()[:, (g * 8 + j) * P:(g * 8 + j + 1) * P],
                start=False, stop=True),
                reads=[b_ddiag, b_xT[pa]], writes=[b_psI])
        for m in range(2):
            em.pe(lambda e, g=g, m=m, pa=pa: e.matmul(
                psS_[0].ap()[:, m * 512:(m + 1) * 512],
                xT[pa].ap()[:, DSSM:DSSM + NST],
                dtx2.ap()[:, g * 1024 + m * 512:g * 1024 + (m + 1) * 512],
                start=True, stop=True),
                reads=[b_xT[pa], b_dtx2], writes=[b_psS_[0]])

    def b2_cons_ev(ch, g):
        hsl = slice(g * 1024, (g + 1) * 1024)
        for j in range(8):
            em.act(lambda e, g=g, j=j, ch=ch: e.activation(
                ysb.ap()[:, j * P:(j + 1) * P],
                psN.ap()[:, j * P:(j + 1) * P], AF.Identity,
                scale=eTall.ap()[:, ch, g * 8 + j:g * 8 + j + 1]),
                reads=[b_psN, b_eTall], writes=[b_ysb])
        gy = g % 2
        em.dve(lambda e, gy=gy: e.tensor_add(
            yo2[gy].ap(), ysb.ap(), psI.ap()),
            reads=[b_ysb, b_psI], writes=[b_yo2[gy]])
        em.dma("sp_d", dram["y_hbm"][ch * Q:(ch + 1) * Q, hsl], yo2[gy].ap(),
               reads=[b_yo2[gy]], tag=f"sty{ch}_{g}")
        em.dve(lambda e, g=g, ch=ch, hsl=hsl: e.tensor_tensor(
            htmp.ap().rearrange("p (h q) -> p h q", q=P),
            hst.ap()[:, hsl].rearrange("p (h q) -> p h q", q=P),
            decCall.ap()[:, ch, g * 8:(g + 1) * 8].unsqueeze(2)
                   .broadcast_to([128, 8, P]),
            ALU.mult),
            reads=[b_h[g], b_decCall], writes=[b_htmp])
        em.dve(lambda e, hsl=hsl: e.tensor_add(
            hst.ap()[:, hsl], htmp.ap(), psS_[0].ap()),
            reads=[b_htmp, b_psS_[0]], writes=[b_h[g]])
        em.act(lambda e, hsl=hsl: e.activation(
            hbf.ap()[:, hsl], hst.ap()[:, hsl], AF.Copy),
            reads=[b_h[g]], writes=[b_hbf[g]])

    if summary_only:
        for ch in range(NCH):
            tc = slice(ch * Q, (ch + 1) * Q)
            pa = ch % 2
            em.dma("sp_d", xT[pa].ap(), dram["xssm_hbm"][:, tc],
                   writes=[b_xT[pa]], tag=f"{tag}xT{ch}", transpose=True)
            em.dve(lambda e, pa=pa, ch=ch: e.tensor_tensor(
                dtx2.ap().rearrange("p (h q) -> p h q", q=P),
                xT[pa].ap()[:, 0:DSSM].rearrange("p (h q) -> p h q", q=P),
                dwall.ap()[:, ch, :].unsqueeze(2).broadcast_to([128, NH, P]),
                ALU.mult),
                reads=[b_xT[pa], b_dwall], writes=[b_dtx2])
            for g in range(8):
                hsl = slice(g * 1024, (g + 1) * 1024)
                ppS = g % 2
                for m in range(2):
                    em.pe(lambda e, g=g, m=m, pa=pa, ppS=ppS: e.matmul(
                        psS_[ppS].ap()[:, m * 512:(m + 1) * 512],
                        xT[pa].ap()[:, DSSM:DSSM + NST],
                        dtx2.ap()[:, g * 1024 + m * 512:g * 1024 + (m + 1) * 512],
                        start=True, stop=True),
                        reads=[b_xT[pa], b_dtx2], writes=[b_psS_[ppS]])
                em.dve(lambda e, g=g, ch=ch, hsl=hsl: e.tensor_tensor(
                    htmp.ap().rearrange("p (h q) -> p h q", q=P),
                    hst.ap()[:, hsl].rearrange("p (h q) -> p h q", q=P),
                    decCall.ap()[:, ch, g * 8:(g + 1) * 8].unsqueeze(2)
                           .broadcast_to([128, 8, P]),
                    ALU.mult),
                    reads=[b_h[g], b_decCall], writes=[b_htmp])
                em.dve(lambda e, hsl=hsl, ppS=ppS: e.tensor_add(
                    hst.ap()[:, hsl], htmp.ap(), psS_[ppS].ap()),
                    reads=[b_htmp, b_psS_[ppS]], writes=[b_h[g]])
    else:
        for ch in range(NCH + 1):
            if ch < NCH:
                tc = slice(ch * Q, (ch + 1) * Q)
                pa = ch % 2
                em.dma("sp_d", xT[pa].ap(), dram["xssm_hbm"][:, tc],
                       writes=[b_xT[pa]], tag=f"{tag}xT{ch}", transpose=True)
                b2_gt(ch)
            if ch >= 1:
                b2_dtx(ch - 1)
            for g in range(8):
                if ch < NCH:
                    b2_prod_W(ch, g)
                if ch >= 1:
                    b2_cons_mm(ch - 1, g)
                if ch < NCH:
                    b2_prod_tp(ch, g)
                    b2_prod_exp(ch, g)
                if ch >= 1:
                    b2_cons_ev(ch - 1, g)
    if summary_only:
        tgt = dram["cc_S_in"] if "cc_S_in" in dram else dram["S_own"]
        em.dma("sp_d", tgt, hst.ap(), reads=b_h, tag="stS")
    em.fence()



def build_c1(em, dram, st):
    nc = em.nc
    import contextlib
    sb = lambda name, shape, dt: st.enter_context(nc.sbuf_tensor(name, shape, dt))
    ps = lambda name, shape, dt: st.enter_context(nc.psum_tensor(name, shape, dt))

    nw = sb("nwC", [128, NH], F32)
    b_nw = Buf("nw")
    em.dma("sp_d", nw.ap(), dram["nw_col"], writes=[b_nw])
    onesb = sb("onesbC", [128, 1], BF16)
    b_onesb = Buf("onesb")
    em.dma("sp_d", onesb.ap(), dram["ones_col_bf16"], writes=[b_onesb])
    onesf = sb("onesfC", [1, 128], F32)
    b_onesf = Buf("onesf")
    em.dve(lambda e: e.memset(onesf.ap(), 1.0), writes=[b_onesf])

    ygnw = sb("ygnwC", [128, NH, 1024], BF16)
    b_yg = [Buf(f"yg{j}") for j in range(NH)]
    zt = [sb(f"ztC{i}", [128, 1024], BF16) for i in range(2)]
    b_zt = [Buf(f"zt{i}") for i in range(2)]
    zs = [sb(f"zsC{i}", [128, 1024], BF16) for i in range(2)]
    b_zs = [Buf(f"zs{i}") for i in range(2)]
    sq = [sb(f"sqC{i}", [128, 1024], BF16) for i in range(2)]
    b_sq = [Buf(f"sq{i}") for i in range(2)]
    psV = ps("psVC", [1, 1024], F32)
    b_psV = Buf("psV")
    psR = ps("psRC", [128, 1024], F32)
    b_psR = Buf("psR")
    rmsrow = sb("rmsrowC", [1, 1024], F32)
    b_rms = Buf("rmsrow")
    invrep = sb("invrepC", [128, 1024], F32)
    b_invrep = Buf("invrep")
    wpan = [sb(f"wpanC{i}", [128, NH, 128], BF16) for i in range(2)]
    b_wpan = [Buf(f"wpanC{i}") for i in range(2)]
    psA = [ps(f"psAC{i}", [128, 1024], F32) for i in range(2)]
    b_psA = [Buf(f"psA{i}") for i in range(2)]
    att = [sb(f"attC{i}", [128, 1024], BF16) for i in range(2)]
    b_att = [Buf(f"att{i}") for i in range(2)]

    for hf in range(2):
        tsl = slice(hf * 1024, (hf + 1) * 1024)
        for j in range(NH):
            pa = j % 2
            em.dma("sp_d", ygnw.ap()[:, j, :], dram["y_hbm"][tsl, j * 128:(j + 1) * 128],
                   writes=[b_yg[j]], transpose=True, tag=f"ldy{hf}_{j}")
            em.dma("act_d", zt[pa].ap(), dram["z_hbm"][j * 128:(j + 1) * 128, tsl],
                   writes=[b_zt[pa]], tag=f"ldz{hf}_{j}")
            em.act(lambda e, pa=pa: e.activation(zs[pa].ap(), zt[pa].ap(), AF.Silu),
                   reads=[b_zt[pa]], writes=[b_zs[pa]])
            em.dve(lambda e, j=j, pa=pa: e.tensor_tensor(
                ygnw.ap()[:, j, :], ygnw.ap()[:, j, :], zs[pa].ap(), ALU.mult),
                reads=[b_zs[pa], b_yg[j]], writes=[b_yg[j]])
            em.act(lambda e, j=j, pa=pa: e.activation(
                sq[pa].ap(), ygnw.ap()[:, j, :], AF.Square),
                reads=[b_yg[j]], writes=[b_sq[pa]])
            for m in range(2):
                em.pe(lambda e, pa=pa, m=m, j=j: e.matmul(
                    psV.ap()[:, m * 512:(m + 1) * 512], onesb.ap(),
                    sq[pa].ap()[:, m * 512:(m + 1) * 512],
                    start=(j == 0), stop=(j == NH - 1)),
                    reads=[b_sq[pa], b_onesb], writes=[b_psV])
            em.dve(lambda e, j=j: e.tensor_scalar(
                ygnw.ap()[:, j, :], ygnw.ap()[:, j, :], nw.ap()[:, j:j + 1],
                None, ALU.mult),
                reads=[b_nw, b_yg[j]], writes=[b_yg[j]])
        em.act(lambda e: e.activation(rmsrow.ap(), psV.ap(), AF.Ln,
                                      bias=EPS, scale=1.0 / DSSM),
               reads=[b_psV], writes=[b_rms])
        em.act(lambda e: e.activation(rmsrow.ap(), rmsrow.ap(), AF.Exp, scale=-0.5),
               reads=[b_rms], writes=[b_rms])
        for m in range(2):
            em.pe(lambda e, m=m: e.matmul(
                psR.ap()[:, m * 512:(m + 1) * 512], onesf.ap(),
                rmsrow.ap()[:, m * 512:(m + 1) * 512], start=True, stop=True),
                reads=[b_rms, b_onesf], writes=[b_psR])
        em.dve(lambda e: e.tensor_copy(invrep.ap(), psR.ap()),
               reads=[b_psR], writes=[b_invrep])
        for d in range(32):
            pa = d % 2
            src = dram["W_out"].rearrange("(i p) c -> p i c", p=128)[
                :, :, d * 128:(d + 1) * 128]
            for piece in range(8):
                em.dma("pool_d", wpan[pa].ap()[:, piece * 8:(piece + 1) * 8, :],
                       src[:, piece * 8:(piece + 1) * 8, :],
                       writes=[b_wpan[pa]], tag=f"ldWo{hf}_{d}_{piece}")
            for j in range(NH):
                for m in range(2):
                    em.pe(lambda e, pa=pa, j=j, m=m: e.matmul(
                        psA[pa].ap()[:, m * 512:(m + 1) * 512],
                        wpan[pa].ap()[:, j, :],
                        ygnw.ap()[:, j, m * 512:(m + 1) * 512],
                        start=(j == 0), stop=(j == NH - 1)),
                        reads=[b_wpan[pa], b_yg[j]], writes=[b_psA[pa]])
            em.dve(lambda e, pa=pa: e.tensor_tensor(
                att[pa].ap(), psA[pa].ap(), invrep.ap(), ALU.mult),
                reads=[b_psA[pa], b_invrep], writes=[b_att[pa]])
            em.dma("sp_d", dram["attn_hbm"][d * 128:(d + 1) * 128, tsl],
                   att[pa].ap(), reads=[b_att[pa]], tag=f"stat{hf}_{d}")
        em.fence()



def build_res_ln(em, dram, st, which):
    nc = em.nc
    sb = lambda name, shape, dt: st.enter_context(nc.sbuf_tensor(name + which, shape, dt))
    ps = lambda name, shape, dt: st.enter_context(nc.psum_tensor(name + which, shape, dt))
    src_ch = dram["attn_hbm"] if which == "c2" else dram["ffn_hbm"]

    ident = sb("identR", [128, 128], BF16)
    b_ident = Buf("ident")
    em.dma("sp_d", ident.ap(), dram["ident_bf16"], writes=[b_ident])

    at = [sb(f"atR{i}", [128, H], BF16) for i in range(2)]
    b_at = [Buf(f"at{i}") for i in range(2)]
    af = [sb(f"afR{i}", [128, H], F32) for i in range(2)]
    b_af = [Buf(f"af{i}") for i in range(2)]
    xm = [sb(f"xmR{i}", [128, H], F32) for i in range(2)]
    b_xm = [Buf(f"xm{i}") for i in range(2)]
    res = [sb(f"resR{i}", [128, H], F32) for i in range(2)]
    b_res = [Buf(f"res{i}") for i in range(2)]
    sq = [sb(f"sqR{i}", [128, H], BF16) for i in range(2)]
    b_sq = [Buf(f"sq{i}") for i in range(2)]
    stats = [sb(f"statsR{i}", [128, 8], F32) for i in range(2)]
    b_st_ = [Buf(f"stats{i}") for i in range(2)]
    if which == "c2":
        x2t = [sb(f"x2tR{i}", [128, H], BF16) for i in range(2)]
        b_x2t = [Buf(f"x2t{i}") for i in range(2)]
        tps = ps("tpsR", [128, 2048], BF16)
        b_tps = Buf("tps")
        evb = sb("evbR", [128, 16, 128], BF16)
        b_evb = Buf("evb")

    for it in range(16):
        r0 = it * 128
        pa = it % 2
        col = lambda i, pa=pa: stats[pa].ap()[:, i:i + 1]
        b_st = b_st_[pa]
        sq_t, b_sq_t = sq[pa], b_sq[pa]
        em.dma("sp_d", at[pa].ap(), src_ch[:, r0:r0 + 128], writes=[b_at[pa]],
               transpose=True, tag=f"ldat{it}")
        if which == "c2":
            em.dma("act_d", res[pa].ap(), dram["xt"][HALO + r0:HALO + r0 + 128, :],
                   writes=[b_res[pa]], tag=f"ldx{it}")
        else:
            em.dma("act_d", res[pa].ap(), dram["xmid_hbm"][r0:r0 + 128, :],
                   writes=[b_res[pa]], tag=f"ldxm{it}")
        em.dve(lambda e, pa=pa: e.tensor_copy(af[pa].ap(), at[pa].ap()),
               reads=[b_at[pa]], writes=[b_af[pa]])
        em.dve(lambda e, pa=pa, col=col: e.tensor_reduce(
            col(0), af[pa].ap(), AX.X, ALU.add),
            reads=[b_af[pa]], writes=[b_st])
        em.act(lambda e, pa=pa, sq_t=sq_t, col=col: e.activation(
            sq_t.ap(), at[pa].ap(), AF.Square, accum_out=col(1)),
            reads=[b_at[pa]], writes=[b_sq_t, b_st])
        em.dve(lambda e, col=col: e.tensor_scalar(col(2), col(0), 1.0 / H, None, ALU.mult),
               reads=[b_st], writes=[b_st])
        em.dve(lambda e, col=col: e.tensor_scalar(col(3), col(1), 1.0 / H, None, ALU.mult),
               reads=[b_st], writes=[b_st])
        em.dve(lambda e, col=col: e.tensor_mul(col(5), col(2), col(2)),
               reads=[b_st], writes=[b_st])
        em.dve(lambda e, col=col: e.tensor_sub(col(3), col(3), col(5)),
               reads=[b_st], writes=[b_st])
        em.act(lambda e, col=col: e.activation(col(5), col(3), AF.Ln, bias=EPS),
               reads=[b_st], writes=[b_st])
        em.act(lambda e, col=col: e.activation(col(4), col(5), AF.Exp, scale=-0.5),
               reads=[b_st], writes=[b_st])
        em.dve(lambda e, pa=pa, col=col: e.tensor_scalar(
            af[pa].ap(), af[pa].ap(), col(2), col(4), ALU.subtract, ALU.mult),
            reads=[b_st, b_af[pa]], writes=[b_af[pa]])
        if which == "c5":
            em.dve(lambda e, pa=pa: e.tensor_add(
                xm[pa].ap(), af[pa].ap(), res[pa].ap()),
                reads=[b_af[pa], b_res[pa]], writes=[b_xm[pa]])
            em.dma("sp_d", dram["out_hbm"][r0:r0 + 128, :], xm[pa].ap(),
                   reads=[b_xm[pa]], tag=f"stout{it}")
            continue
        em.op("dve", lambda e, pa=pa, col=col: e.scalar_tensor_tensor(
            xm[pa].ap(), af[pa].ap(), 0.0, res[pa].ap(), ALU.add, ALU.add,
            accum_out=col(6)),
            reads=[b_af[pa], b_res[pa]], writes=[b_xm[pa], b_st])
        em.dma("sp_d", dram["xmid_hbm"][r0:r0 + 128, :], xm[pa].ap(),
               reads=[b_xm[pa]], tag=f"stxm{it}")
        em.act(lambda e, pa=pa, sq_t=sq_t, col=col: e.activation(
            sq_t.ap(), xm[pa].ap(), AF.Square, accum_out=col(7)),
            reads=[b_xm[pa]], writes=[b_sq_t, b_st])
        em.dve(lambda e, col=col: e.tensor_scalar(col(2), col(6), 1.0 / H, None, ALU.mult),
               reads=[b_st], writes=[b_st])
        em.dve(lambda e, col=col: e.tensor_scalar(col(3), col(7), 1.0 / H, None, ALU.mult),
               reads=[b_st], writes=[b_st])
        em.dve(lambda e, col=col: e.tensor_mul(col(5), col(2), col(2)),
               reads=[b_st], writes=[b_st])
        em.dve(lambda e, col=col: e.tensor_sub(col(3), col(3), col(5)),
               reads=[b_st], writes=[b_st])
        em.act(lambda e, col=col: e.activation(col(5), col(3), AF.Ln, bias=EPS),
               reads=[b_st], writes=[b_st])
        em.act(lambda e, col=col: e.activation(col(4), col(5), AF.Exp, scale=-0.5),
               reads=[b_st], writes=[b_st])
        em.dve(lambda e, pa=pa, col=col: e.tensor_scalar(
            x2t[pa].ap(), xm[pa].ap(), col(2), col(4), ALU.subtract, ALU.mult),
            reads=[b_st, b_xm[pa]], writes=[b_x2t[pa]])
        for g in range(2):
            for jj in range(16):
                dblk = g * 16 + jj
                em.pe(lambda e, pa=pa, dblk=dblk, jj=jj: e.transpose(
                    tps.ap()[:, jj * 128:(jj + 1) * 128],
                    x2t[pa].ap()[:, dblk * 128:(dblk + 1) * 128], ident.ap()),
                    reads=[b_x2t[pa], b_ident], writes=[b_tps])
            em.dve(lambda e: e.tensor_copy(
                evb.ap(), tps.ap().rearrange("p (j t) -> p j t", j=16)),
                reads=[b_tps], writes=[b_evb])
            em.dma("sp_d",
                   dram["x2T_hbm"].rearrange("(i p) t -> p i t", p=128)[
                       :, g * 16:(g + 1) * 16, r0:r0 + 128],
                   evb.ap(), reads=[b_evb], tag=f"stx2T{it}_{g}")
    em.fence()


def build_c2(em, dram, st):
    build_res_ln(em, dram, st, "c2")


def build_c5(em, dram, st):
    build_res_ln(em, dram, st, "c5")



def build_c34(em, dram, st):
    nc = em.nc
    import contextlib
    sb = lambda name, shape, dt: st.enter_context(nc.sbuf_tensor(name, shape, dt))

    b12 = sb("b12F", [128, NH], F32)
    b_b12 = Buf("b12")
    em.dma("sp_d", b12.ap(), dram["b12_col"], writes=[b_b12])
    b3 = sb("b3F", [128, 32], F32)
    b_b3 = Buf("b3")
    em.dma("sp_d", b3.ap(), dram["b3_col"], writes=[b_b3])

    x2h = sb("x2hF", [128, 32, 1024], BF16)
    b_x2h = Buf("x2h")
    x4h = sb("x4hF", [128, 32, 1024], BF16)
    b_x4 = [Buf(f"x4_{k}") for k in range(32)]
    wp = [sb(f"wpF{i}", [128, 32, 128], BF16) for i in range(4)]
    b_wp = [Buf(f"wpF{i}") for i in range(4)]
    s1t = [sb(f"s1tF{i}", [128, 1024], BF16) for i in range(2)]
    b_s1 = [Buf(f"s1t{i}") for i in range(2)]
    v2t = [sb(f"v2tF{i}", [128, 1024], BF16) for i in range(2)]
    b_v2 = [Buf(f"v2t{i}") for i in range(2)]
    fout = [sb(f"foutF{i}", [128, 1024], BF16) for i in range(2)]
    b_fout = [Buf(f"fout{i}") for i in range(2)]

    for hf in range(2):
        tsl = slice(hf * 1024, (hf + 1) * 1024)
        em.dma("sp_d", x2h.ap(),
               dram["x2T_hbm"].rearrange("(i p) t -> p i t", p=128)[:, :, tsl],
               writes=[b_x2h], tag=f"ldx2h{hf}")
        import contextlib as _ct
        with _ct.ExitStack() as st1:
            ps1 = [st1.enter_context(nc.psum_tensor(f"ps1F{hf}{i}", [128, 1024], F32))
                   for i in range(2)]
            b_ps1_ = [Buf(f"ps1{i}") for i in range(2)]
            ps2 = [st1.enter_context(nc.psum_tensor(f"ps2F{hf}{i}", [128, 1024], F32))
                   for i in range(2)]
            b_ps2_ = [Buf(f"ps2{i}") for i in range(2)]
            for mi in range(32):
                pa = mi % 2
                ps1_, ps2_ = ps1[pa], ps2[pa]
                b_ps1, b_ps2 = b_ps1_[pa], b_ps2_[pa]
                src1 = dram["w12T"].rearrange("(i p) c -> p i c", p=128)[
                    :, :, mi * 128:(mi + 1) * 128]
                src2 = dram["w12T"].rearrange("(i p) c -> p i c", p=128)[
                    :, :, (mi + 32) * 128:(mi + 33) * 128]
                for piece in range(4):
                    pp = slice(piece * 8, (piece + 1) * 8)
                    em.dma("pool_d", wp[pa].ap()[:, pp, :], src1[:, pp, :],
                           writes=[b_wp[pa]], tag=f"ldw12a{hf}_{mi}_{piece}")
                    em.dma("pool_d", wp[2 + pa].ap()[:, pp, :], src2[:, pp, :],
                           writes=[b_wp[2 + pa]], tag=f"ldw12b{hf}_{mi}_{piece}")
                for k in range(32):
                    for m in range(2):
                        em.pe(lambda e, pa=pa, k=k, m=m, ps1_=ps1_: e.matmul(
                            ps1_.ap()[:, m * 512:(m + 1) * 512],
                            wp[pa].ap()[:, k, :],
                            x2h.ap()[:, k, m * 512:(m + 1) * 512],
                            start=(k == 0), stop=(k == 31)),
                            reads=[b_wp[pa], b_x2h], writes=[b_ps1])
                        em.pe(lambda e, pa=pa, k=k, m=m, ps2_=ps2_: e.matmul(
                            ps2_.ap()[:, m * 512:(m + 1) * 512],
                            wp[2 + pa].ap()[:, k, :],
                            x2h.ap()[:, k, m * 512:(m + 1) * 512],
                            start=(k == 0), stop=(k == 31)),
                            reads=[b_wp[2 + pa], b_x2h], writes=[b_ps2])
                em.act(lambda e, mi=mi, ps1_=ps1_, pa=pa: e.activation(
                    s1t[pa].ap(), ps1_.ap(), AF.Silu, bias=b12.ap()[:, mi:mi + 1]),
                    reads=[b_ps1, b_b12], writes=[b_s1[pa]])
                em.act(lambda e, mi=mi, ps2_=ps2_, pa=pa: e.activation(
                    v2t[pa].ap(), ps2_.ap(), AF.Identity,
                    bias=b12.ap()[:, mi + 32:mi + 33]),
                    reads=[b_ps2, b_b12], writes=[b_v2[pa]])
                em.dve(lambda e, mi=mi, pa=pa: e.tensor_tensor(
                    x4h.ap()[:, mi, :], s1t[pa].ap(), v2t[pa].ap(), ALU.mult),
                    reads=[b_s1[pa], b_v2[pa]], writes=[b_x4[mi]])
            em.fence()
        with _ct.ExitStack() as st2:
            psF = [st2.enter_context(nc.psum_tensor(f"psF{hf}_{i}", [128, 1024], F32))
                   for i in range(2)]
            b_psF = [Buf(f"psF{i}") for i in range(2)]
            for ni in range(32):
                pa = ni % 2
                src = dram["w3T"].rearrange("(i p) c -> p i c", p=128)[
                    :, :, ni * 128:(ni + 1) * 128]
                for piece in range(4):
                    pp = slice(piece * 8, (piece + 1) * 8)
                    em.dma("pool_d", wp[pa].ap()[:, pp, :], src[:, pp, :],
                           writes=[b_wp[pa]], tag=f"ldw3{hf}_{ni}_{piece}")
                for k in range(32):
                    for m in range(2):
                        em.pe(lambda e, pa=pa, k=k, m=m: e.matmul(
                            psF[pa].ap()[:, m * 512:(m + 1) * 512],
                            wp[pa].ap()[:, k, :],
                            x4h.ap()[:, k, m * 512:(m + 1) * 512],
                            start=(k == 0), stop=(k == 31)),
                            reads=[b_wp[pa], b_x4[k]], writes=[b_psF[pa]])
                em.act(lambda e, pa=pa, ni=ni: e.activation(
                    fout[pa].ap(), psF[pa].ap(), AF.Identity,
                    bias=b3.ap()[:, ni:ni + 1]),
                    reads=[b_psF[pa], b_b3], writes=[b_fout[pa]])
                em.dma("sp_d", dram["ffn_hbm"][ni * 128:(ni + 1) * 128, tsl],
                       fout[pa].ap(), reads=[b_fout[pa]], tag=f"stf{hf}_{ni}")
            em.fence()


def build_scan_b1(em, dram, st):
    build_scan(em, dram, st, True)


def build_scan_b2(em, dram, st):
    build_scan(em, dram, st, False)


IO_B1 = dict(
    ins=[("xssm_hbm", (CONV_DIM, TPC), BF),
         ("dt_soft_hbm", (NH, TPC), np.float32),
         ("A_log_col", (NH, 1), np.float32),
         ("ident_f32", (128, 128), np.float32),
         ("eye16rep", (NCH, TPC), np.float32),
         ("maskW", (128, 128), np.float32)],
    outs=[("S_own", (128, DSSM), np.float32),
          ("Dt_col", (NH, 1), np.float32)])

IO_B2 = dict(
    ins=[("xssm_hbm", (CONV_DIM, TPC), BF),
         ("dt_soft_hbm", (NH, TPC), np.float32),
         ("A_log_col", (NH, 1), np.float32),
         ("ident_f32", (128, 128), np.float32),
         ("eye16rep", (NCH, TPC), np.float32),
         ("maskW", (128, 128), np.float32),
         ("D_rep", (128, NH), np.float32),
         ("S_in0", (128, DSSM), np.float32),
         ("S_in1", (128, DSSM), np.float32),
         ("S_in2", (128, DSSM), np.float32),
         ("Dt_row1", (1, NH), np.float32),
         ("Dt_row2", (1, NH), np.float32)],
    outs=[("y_hbm", (TPC, DSSM), BF)])


IO_C1 = dict(
    ins=[("y_hbm", (TPC, DSSM), BF),
         ("z_hbm", (DSSM, TPC), BF),
         ("nw_col", (128, NH), np.float32),
         ("W_out", (DSSM, H), np.float32),
         ("ones_col_bf16", (128, 1), BF)],
    outs=[("attn_hbm", (H, TPC), BF)])

IO_C2 = dict(
    ins=[("attn_hbm", (H, TPC), BF),
         ("xt", (HALO + TPC, H), np.float32),
         ("ident_bf16", (128, 128), BF)],
    outs=[("xmid_hbm", (TPC, H), np.float32),
          ("x2T_hbm", (H, TPC), BF)])

IO_C34 = dict(
    ins=[("x2T_hbm", (H, TPC), BF),
         ("w12T", (H, DSSM), np.float32),
         ("b12_col", (128, NH), np.float32),
         ("w3T", (H, H), np.float32),
         ("b3_col", (128, 32), np.float32)],
    outs=[("ffn_hbm", (H, TPC), BF)])

IO_C5 = dict(
    ins=[("ffn_hbm", (H, TPC), BF),
         ("xmid_hbm", (TPC, H), np.float32),
         ("ident_bf16", (128, 128), BF)],
    outs=[("out_hbm", (TPC, H), np.float32)])


def build_program(io, build_fns):
    nc = bass.Bass("TRN2", target_bir_lowering=False)
    dram = {}
    for n, s, d in io["ins"]:
        dram[n] = nc.dram_tensor(n, list(s), _DT_NP2MY[np.dtype(d)],
                                 kind="ExternalInput").ap()
    for n, s, d in io["outs"]:
        dram[n] = nc.dram_tensor(n, list(s), _DT_NP2MY[np.dtype(d)],
                                 kind="ExternalOutput").ap()
    em = Em(nc)
    import contextlib
    with contextlib.ExitStack() as st:
        em.setup_consts(st, [EPS, 1.0, -1.0, -0.5, 0.5])
        for f in build_fns:
            f(em, dram, st)
        em.finalize_and_emit()
    return nc


def run_phase(io, build_fns, in_maps):
    nc = build_program(io, build_fns)
    res = run_bass_kernel_spmd(nc, in_maps, core_ids=list(range(len(in_maps))))
    return res.results


class _NW:

    def __init__(self, nc, pfx):
        object.__setattr__(self, "_nc", nc)
        object.__setattr__(self, "_p", pfx)

    def sbuf_tensor(self, name, *a, **k):
        return self._nc.sbuf_tensor(self._p + name, *a, **k)

    def psum_tensor(self, name, *a, **k):
        return self._nc.psum_tensor(self._p + name, *a, **k)

    def __getattr__(self, n):
        return getattr(self._nc, n)


def build_cc(em, dram, st):
    b = Buf("cc")
    groups = [[0, 1, 2, 3], [4, 5, 6, 7]]
    em.op("pool_d", lambda e: e.collective_compute(
        "AllGather", ALU.bypass, replica_groups=groups,
        ins=[dram["cc_S_in"].opt()], outs=[dram["cc_S_out"].opt()]),
        writes=[b], tag="ccS")
    em.op("pool_d", lambda e: e.collective_compute(
        "AllGather", ALU.bypass, replica_groups=groups,
        ins=[dram["cc_D_in"].opt()], outs=[dram["cc_D_out"].opt()]),
        writes=[b], tag="ccD")


FUSED_INS = [
    ("xt", (HALO + TPC, H), np.float32),
    ("W_in", (H, PROJ_DIM), np.float32),
    ("conv_w", (CONV_DIM, KCONV), np.float32),
    ("dt_bias_col", (NH, 1), np.float32),
    ("A_log_col", (NH, 1), np.float32),
    ("ident_bf16", (128, 128), BF),
    ("ident_f32", (128, 128), np.float32),
    ("eye16rep", (NCH, TPC), np.float32),
    ("maskW", (128, 128), np.float32),
    ("D_rep", (128, NH), np.float32),
    ("nw_col", (128, NH), np.float32),
    ("W_out", (DSSM, H), np.float32),
    ("ones_col_bf16", (128, 1), BF),
    ("w12T", (H, DSSM), np.float32),
    ("b12_col", (128, NH), np.float32),
    ("w3T", (H, H), np.float32),
    ("b3_col", (128, 32), np.float32),
    ("relay_cols", (128, 8), np.float32),
]
FUSED_INTERNAL = [
    ("z_hbm", (DSSM, TPC), BF),
    ("xssm_hbm", (CONV_DIM, TPC), BF),
    ("dt_soft_hbm", (NH, TPC), np.float32),
    ("y_hbm", (TPC, DSSM), BF),
    ("attn_hbm", (H, TPC), BF),
    ("xmid_hbm", (TPC, H), np.float32),
    ("x2T_hbm", (H, TPC), BF),
    ("ffn_hbm", (H, TPC), BF),
    ("cc_S_in", (128, DSSM), np.float32),
    ("cc_S_out", (512, DSSM), np.float32),
    ("cc_D_in", (1, NH), np.float32),
    ("cc_D_out", (4, NH), np.float32),
]

FUSED_PHASES = [build_phase_a, build_scan_b1, build_cc, build_scan_b2,
                build_c1, build_c2, build_c34, build_c5]


def build_multi(phases, ins, outs, internals):
    nc = bass.Bass("TRN2", target_bir_lowering=False)
    dram = {}
    for n, s, d in ins:
        dram[n] = nc.dram_tensor(n, list(s), _DT_NP2MY[np.dtype(d)],
                                 kind="ExternalInput").ap()
    for n, s, d in outs:
        dram[n] = nc.dram_tensor(n, list(s), _DT_NP2MY[np.dtype(d)],
                                 kind="ExternalOutput").ap()
    for n, s, d in internals:
        dram[n] = nc.dram_tensor(n, list(s), _DT_NP2MY[np.dtype(d)],
                                 kind="Internal").ap()
    em = Em(nc)
    import contextlib
    with contextlib.ExitStack() as stc:
        em.setup_consts(stc, [EPS, 1.0, -1.0, -0.5, 0.5])
        for i, f in enumerate(phases):
            em.nc = _NW(nc, f"p{i}_")
            with contextlib.ExitStack() as stf:
                f(em, dram, stf)
                em.fence()
        em.nc = nc
        em.finalize_and_emit()
    return nc



L1_INS = [
    ("xt", (HALO + TPC, H), np.float32),
    ("W_in", (H, PROJ_DIM), np.float32),
    ("conv_w", (CONV_DIM, KCONV), np.float32),
    ("dt_bias_col", (NH, 1), np.float32),
    ("A_log_col", (NH, 1), np.float32),
    ("ident_bf16", (128, 128), BF),
    ("ident_f32", (128, 128), np.float32),
    ("eye16rep", (NCH, TPC), np.float32),
    ("maskW", (128, 128), np.float32),
]
L1_OUTS = [
    ("z_hbm", (DSSM, TPC), BF),
    ("xssm_hbm", (CONV_DIM, TPC), BF),
    ("dt_soft_hbm", (NH, TPC), np.float32),
    ("S_own", (128, DSSM), np.float32),
    ("Dt_col", (NH, 1), np.float32),
]
L2_INS = [
    ("xt", (HALO + TPC, H), np.float32),
    ("z_hbm", (DSSM, TPC), BF),
    ("xssm_hbm", (CONV_DIM, TPC), BF),
    ("dt_soft_hbm", (NH, TPC), np.float32),
    ("A_log_col", (NH, 1), np.float32),
    ("ident_bf16", (128, 128), BF),
    ("ident_f32", (128, 128), np.float32),
    ("eye16rep", (NCH, TPC), np.float32),
    ("maskW", (128, 128), np.float32),
    ("D_rep", (128, NH), np.float32),
    ("nw_col", (128, NH), np.float32),
    ("W_out", (DSSM, H), np.float32),
    ("ones_col_bf16", (128, 1), BF),
    ("w12T", (H, DSSM), np.float32),
    ("b12_col", (128, NH), np.float32),
    ("w3T", (H, H), np.float32),
    ("b3_col", (128, 32), np.float32),
    ("S_in0", (128, DSSM), np.float32),
    ("S_in1", (128, DSSM), np.float32),
    ("S_in2", (128, DSSM), np.float32),
    ("Dt_row1", (1, NH), np.float32),
    ("Dt_row2", (1, NH), np.float32),
]
L2_OUTS = [("out_hbm", (TPC, H), np.float32)]
L2_INTERNAL = [
    ("y_hbm", (TPC, DSSM), BF),
    ("attn_hbm", (H, TPC), BF),
    ("xmid_hbm", (TPC, H), np.float32),
    ("x2T_hbm", (H, TPC), BF),
    ("ffn_hbm", (H, TPC), BF),
]


def build_l1():
    return build_multi([build_phase_a, build_scan_b1], L1_INS, L1_OUTS, [])


def build_l2():
    return build_multi([build_scan_b2, build_c1, build_c2, build_c34, build_c5],
                       L2_INS, L2_OUTS, L2_INTERNAL)


def _maps(io, shared, percore, extra=None):
    out = []
    for core in range(8):
        m = {}
        for k, _, _ in io["ins"]:
            if k in shared:
                m[k] = shared[k]
            elif k in percore[core]:
                m[k] = percore[core][k]
            else:
                m[k] = extra[core][k]
        out.append(m)
    return out


def kernel(**inputs):
    shared, percore = host_prep(inputs)
    nc1 = build_l1()
    maps1 = []
    for core in range(8):
        m = {k: (shared[k] if k in shared else percore[core][k])
             for k, _, _ in L1_INS}
        maps1.append(m)
    r1 = run_bass_kernel_spmd(nc1, maps1, core_ids=list(range(8))).results
    nc2 = build_l2()
    zS = np.zeros((128, DSSM), np.float32)
    oD = np.ones((1, NH), np.float32)
    maps2 = []
    for core in range(8):
        b, q = core // 4, core % 4
        m = {}
        for k, _, _ in L2_INS:
            if k in shared:
                m[k] = shared[k]
            elif k in percore[core]:
                m[k] = percore[core][k]
            elif k in r1[core]:
                m[k] = r1[core][k]
        for slot, j in enumerate((q - 3, q - 2, q - 1)):
            m[f"S_in{slot}"] = r1[4 * b + j]["S_own"] if j >= 0 else zS
        for slot, j in zip((1, 2), (q - 2, q - 1)):
            m[f"Dt_row{slot}"] = (np.ascontiguousarray(
                r1[4 * b + j]["Dt_col"].reshape(1, NH)) if j >= 0 else oD)
        maps2.append(m)
    r2 = run_bass_kernel_spmd(nc2, maps2, core_ids=list(range(8))).results
    out = np.empty((BATCH, SEQ, H), np.float32)
    for core in range(8):
        b, q = core // 4, core % 4
        out[b, q * TPC:(q + 1) * TPC] = r2[core]["out_hbm"]
    return out


def kernel_multilaunch(**inputs):
    shared, percore = host_prep(inputs)
    r_a = run_phase(IO_A, [build_phase_a], _maps(IO_A, shared, percore))
    r_b1 = run_phase(IO_B1, [build_scan_b1],
                     _maps(IO_B1, shared, percore, r_a))
    extra = []
    zS = np.zeros((128, DSSM), np.float32)
    oD = np.ones((1, NH), np.float32)
    for core in range(8):
        b, q = core // 4, core % 4
        m = dict(r_a[core])
        for slot, j in enumerate((q - 3, q - 2, q - 1)):
            m[f"S_in{slot}"] = (r_b1[4 * b + j]["S_own"] if j >= 0 else zS)
        for slot, j in zip((1, 2), (q - 2, q - 1)):
            m[f"Dt_row{slot}"] = (np.ascontiguousarray(
                r_b1[4 * b + j]["Dt_col"].reshape(1, NH)) if j >= 0 else oD)
        extra.append(m)
    r_b2 = run_phase(IO_B2, [build_scan_b2], _maps(IO_B2, shared, percore, extra))
    for core in range(8):
        extra[core]["y_hbm"] = r_b2[core]["y_hbm"]
    r_c1 = run_phase(IO_C1, [build_c1], _maps(IO_C1, shared, percore, extra))
    for core in range(8):
        extra[core]["attn_hbm"] = r_c1[core]["attn_hbm"]
    r_c2 = run_phase(IO_C2, [build_c2], _maps(IO_C2, shared, percore, extra))
    for core in range(8):
        extra[core]["x2T_hbm"] = r_c2[core]["x2T_hbm"]
        extra[core]["xmid_hbm"] = r_c2[core]["xmid_hbm"]
    r_c34 = run_phase(IO_C34, [build_c34], _maps(IO_C34, shared, percore, extra))
    for core in range(8):
        extra[core]["ffn_hbm"] = r_c34[core]["ffn_hbm"]
    r_c5 = run_phase(IO_C5, [build_c5], _maps(IO_C5, shared, percore, extra))
    out = np.empty((BATCH, SEQ, H), np.float32)
    for core in range(8):
        b, q = core // 4, core % 4
        out[b, q * TPC:(q + 1) * TPC] = r_c5[core]["out_hbm"]
    return out


# version 10
# speedup vs baseline: 1.0215x; 1.0215x over previous
import numpy as np
import ml_dtypes
import concourse.bass as bass
import concourse.mybir as mybir
from concourse.bass_utils import run_bass_kernel_spmd

F32 = mybir.dt.float32
BF16 = mybir.dt.bfloat16
AF = mybir.ActivationFunctionType
ALU = mybir.AluOpType
AX = mybir.AxisListType

H = 4096
DSSM = 8192
NH = 64
P = 128
NST = 128
KCONV = 4
EPS = 1e-5
CONV_DIM = DSSM + 2 * NST
PROJ_DIM = DSSM + CONV_DIM + NH
SEQ = 8192
BATCH = 2
TPC = 2048
Q = 128
NCH = TPC // Q
HALO = KCONV - 1
NEG = -1e30

ZT = 64
XBCT = 66



class Buf:
    __slots__ = ("last_w", "readers", "name")

    def __init__(self, name=""):
        self.last_w = None
        self.readers = []
        self.name = name


class _Op:
    __slots__ = ("fn", "stream", "ordinal", "deps", "inc", "tag")

    def __init__(self, fn, stream, ordinal, deps, tag):
        self.fn = fn
        self.stream = stream
        self.ordinal = ordinal
        self.deps = deps
        self.inc = False
        self.tag = tag


_STREAMS = {
    "pe": ("tensor", False),
    "act": ("scalar", False),
    "dve": ("vector", False),
    "pool": ("gpsimd", False),
    "sp_d": ("sync", True),
    "act_d": ("scalar", True),
    "pool_d": ("gpsimd", True),
}
_BODIES = ("tensor", "scalar", "vector", "gpsimd", "sync")


class Em:
    def __init__(self, nc):
        self.nc = nc
        self.ops = {s: [] for s in _STREAMS}
        self.body_ops = {b: [] for b in _BODIES}
        self.n = {s: 0 for s in _STREAMS}
        self.floor = {}

    def fence(self):
        self.floor = {s: self.n[s] - 1 for s in _STREAMS if self.n[s] > 0}

    def setup_consts(self, st, values):
        pool = st.enter_context(self.nc.sbuf_tensor("constpool", [128, len(values)], F32))
        b = Buf("consts")
        for i, v in enumerate(values):
            ap = pool.ap()[:, i:i + 1]
            self.dve(lambda e, ap=ap, v=v: e.memset(ap, v), writes=[b])
            self.nc.const_aps.register(F32, float(v), ap)
        self.fence()

    def op(self, stream, fn, reads=(), writes=(), tag=""):
        deps = dict(self.floor)
        for b in reads:
            if b.last_w is not None:
                s, o = b.last_w
                if deps.get(s, -1) < o:
                    deps[s] = o
        for b in writes:
            if b.last_w is not None:
                s, o = b.last_w
                if deps.get(s, -1) < o:
                    deps[s] = o
            for s, o in b.readers:
                if deps.get(s, -1) < o:
                    deps[s] = o
        ordn = self.n[stream]
        self.n[stream] += 1
        dl = []
        for s, o in deps.items():
            if s == stream and o >= ordn:
                continue
            if s == stream and o == ordn - 1 and False:
                continue
            if s == "pe" and stream == "pe":
                continue
            dl.append((s, o))
        o_ = _Op(fn, stream, ordn, dl, tag)
        self.ops[stream].append(o_)
        self.body_ops[_STREAMS[stream][0]].append(o_)
        for b in reads:
            b.readers.append((stream, ordn))
        for b in writes:
            b.last_w = (stream, ordn)
            b.readers = []
        return o_

    def pe(self, fn, reads=(), writes=(), tag=""):
        return self.op("pe", fn, reads, writes, tag)

    def act(self, fn, reads=(), writes=(), tag=""):
        return self.op("act", fn, reads, writes, tag)

    def dve(self, fn, reads=(), writes=(), tag=""):
        return self.op("dve", fn, reads, writes, tag)

    def pool(self, fn, reads=(), writes=(), tag=""):
        return self.op("pool", fn, reads, writes, tag)

    def dma(self, queue, out_ap, in_ap, reads=(), writes=(), tag="", **kw):
        return self.op(queue, lambda e: e.dma_start(out_ap, in_ap, **kw),
                       reads, writes, tag)

    def finalize_and_emit(self, final_body="sync"):
        for s, lst in self.ops.items():
            if not lst:
                continue
            if _STREAMS[s][1]:
                for o in lst:
                    o.inc = True
            lst[-1].inc = True
        for s, lst in self.ops.items():
            for o in lst:
                for ds, do in o.deps:
                    self.ops[ds][do].inc = True
        self.val_after = {}
        self.final_val = {}
        for s, lst in self.ops.items():
            step = 16 if _STREAMS[s][1] else 1
            c = 0
            va = []
            for o in lst:
                if o.inc:
                    c += step
                va.append(c)
            self.val_after[s] = va
            self.final_val[s] = c
        nc = self.nc
        import contextlib
        stack = contextlib.ExitStack()
        sems = {s: stack.enter_context(nc.semaphore(f"sem_{s}")) for s in _STREAMS}
        block = stack.enter_context(nc.Block())

        def mk_body(body):
            ops = self.body_ops[body]

            def run(eng):
                watermark = {}
                for o in ops:
                    for ds, do in o.deps:
                        v = self.val_after[ds][do]
                        if v <= 0 or watermark.get(ds, 0) >= v:
                            continue
                        eng.wait_ge(sems[ds], v)
                        watermark[ds] = v
                    inst = o.fn(eng)
                    if o.inc:
                        inst.then_inc(sems[o.stream],
                                      16 if _STREAMS[o.stream][1] else 1)
                if body == final_body:
                    for s in _STREAMS:
                        if s != "sp_d" and _STREAMS[s][0] == body:
                            continue
                        v = self.final_val[s]
                        if v > 0 and watermark.get(s, 0) < v:
                            eng.wait_ge(sems[s], v)
            return run

        for body in _BODIES:
            if not self.body_ops[body] and body != final_body:
                continue
            getattr(block, body)(mk_body(body))
        stack.close()



def host_constants():
    c = {}
    c["ident_bf16"] = np.eye(128, dtype=ml_dtypes.bfloat16)
    c["ident_f32"] = np.eye(128, dtype=np.float32)
    c["ones_f32"] = np.ones((128, 128), dtype=np.float32)
    c["ones_col_bf16"] = np.ones((128, 1), dtype=ml_dtypes.bfloat16)
    t = np.arange(128)
    c["maskW"] = np.where(t[None, :] <= t[:, None], 0.0, NEG).astype(np.float32)
    e = np.zeros((NCH, TPC), dtype=np.float32)
    for k in range(NCH):
        e[k, k * Q:(k + 1) * Q] = 1.0
    c["eye16rep"] = e
    return c


def host_prep(inputs):
    x = np.asarray(inputs["L0_attn_hidden_states"], np.float32)
    shared = dict(host_constants())
    shared["W_in"] = np.ascontiguousarray(inputs["L0_attn_W_in"], dtype=np.float32)
    shared["conv_w"] = np.ascontiguousarray(inputs["L0_attn_conv_w"], dtype=np.float32)
    shared["dt_bias_col"] = np.ascontiguousarray(
        np.asarray(inputs["L0_attn_dt_bias"], np.float32).reshape(NH, 1))
    shared["A_log_col"] = np.ascontiguousarray(
        np.asarray(inputs["L0_attn_A_log"], np.float32).reshape(NH, 1))
    shared["D_rep"] = np.ascontiguousarray(
        np.broadcast_to(np.asarray(inputs["L0_attn_D_param"], np.float32), (128, NH)))
    shared["nw_col"] = np.ascontiguousarray(
        np.asarray(inputs["L0_attn_norm_weight"], np.float32).reshape(DSSM // 128, 128).T)
    shared["W_out"] = np.ascontiguousarray(inputs["L0_attn_W_out"], dtype=np.float32)
    shared["w12T"] = np.ascontiguousarray(np.asarray(inputs["L0_ffn_w12"], np.float32).T)
    shared["b12_col"] = np.ascontiguousarray(
        np.asarray(inputs["L0_ffn_b12"], np.float32).reshape(DSSM // 128, 128).T)
    shared["w3T"] = np.ascontiguousarray(np.asarray(inputs["L0_ffn_w3"], np.float32).T)
    shared["b3_col"] = np.ascontiguousarray(
        np.asarray(inputs["L0_ffn_b3"], np.float32).reshape(H // 128, 128).T)
    percore = []
    for core in range(8):
        b, q = core // 4, core % 4
        d = {}
        xt = np.zeros((HALO + TPC, H), np.float32)
        s0 = q * TPC
        xt[HALO:] = x[b, s0:s0 + TPC]
        if q > 0:
            xt[:HALO] = x[b, s0 - HALO:s0]
        d["xt"] = np.ascontiguousarray(xt)
        rc = np.zeros((128, 8), np.float32)
        for r in range(4):
            if r < q:
                rc[:, r] = 1.0
                rc[:, 4 + r] = 1.0
        d["relay_cols"] = rc
        percore.append(d)
    return shared, percore



def build_phase_a(em, dram, st):
    nc = em.nc
    sb = lambda name, shape, dt: st.enter_context(nc.sbuf_tensor(name, shape, dt))

    ident = sb("identA", [128, 128], BF16)
    b_ident = Buf("ident")
    em.dma("sp_d", ident.ap(), dram["ident_bf16"], writes=[b_ident])
    identf = sb("identfA", [128, 128], F32)
    b_identf = Buf("identf")
    em.dma("sp_d", identf.ap(), dram["ident_f32"], writes=[b_identf])

    x1T = sb("x1T", [128, 32, HALO + TPC], BF16)
    b_x1T = [Buf(f"x1T{i}") for i in range(17)]

    ntile = 17
    import contextlib
    with contextlib.ExitStack() as st1:
        sb1 = lambda name, shape, dt: st1.enter_context(nc.sbuf_tensor(name, shape, dt))
        ps1 = lambda name, shape, dt: st1.enter_context(nc.psum_tensor(name, shape, dt))
        xa = [sb1(f"xa{i}", [128, H], F32) for i in range(2)]
        b_xa = [Buf(f"xa{i}") for i in range(2)]
        x1t = [sb1(f"x1t{i}", [128, H], BF16) for i in range(2)]
        b_x1t = [Buf(f"x1t{i}") for i in range(2)]
        sq = sb1("sq", [128, H], BF16)
        b_sq = Buf("sq")
        stats = sb1("statsA", [128, 8], F32)
        b_st = Buf("stats")
        tps = ps1("tpsA", [128, 2048], BF16)
        b_tps = Buf("tps")
        for it in range(ntile):
            r0 = it * 128
            nr = min(128, HALO + TPC - r0)
            pa = it % 2
            xa_t, b_xa_t = xa[pa], b_xa[pa]
            x1_t, b_x1_t = x1t[pa], b_x1t[pa]
            em.dma("sp_d", xa_t.ap()[0:nr, :], dram["xt"][r0:r0 + nr, :],
                   writes=[b_xa_t], tag=f"ldxa{it}")
            em.dve(lambda e, xa_t=xa_t, nr=nr: e.tensor_reduce(
                stats.ap()[0:nr, 0:1], xa_t.ap()[0:nr, :], AX.X, ALU.add),
                reads=[b_xa_t], writes=[b_st])
            em.act(lambda e, xa_t=xa_t, nr=nr: e.activation(
                sq.ap()[0:nr, :], xa_t.ap()[0:nr, :], AF.Square,
                accum_out=stats.ap()[0:nr, 1:2]),
                reads=[b_xa_t], writes=[b_sq, b_st])
            em.dve(lambda e, nr=nr: e.tensor_scalar(
                stats.ap()[0:nr, 2:3], stats.ap()[0:nr, 0:1], 1.0 / H, None, ALU.mult),
                reads=[b_st], writes=[b_st])
            em.dve(lambda e, nr=nr: e.tensor_scalar(
                stats.ap()[0:nr, 3:4], stats.ap()[0:nr, 1:2], 1.0 / H, None, ALU.mult),
                reads=[b_st], writes=[b_st])
            em.dve(lambda e, nr=nr: e.tensor_mul(
                stats.ap()[0:nr, 5:6], stats.ap()[0:nr, 2:3], stats.ap()[0:nr, 2:3]),
                reads=[b_st], writes=[b_st])
            em.dve(lambda e, nr=nr: e.tensor_sub(
                stats.ap()[0:nr, 3:4], stats.ap()[0:nr, 3:4], stats.ap()[0:nr, 5:6]),
                reads=[b_st], writes=[b_st])
            em.act(lambda e, nr=nr: e.activation(
                stats.ap()[0:nr, 5:6], stats.ap()[0:nr, 3:4], AF.Ln, bias=EPS),
                reads=[b_st], writes=[b_st])
            em.act(lambda e, nr=nr: e.activation(
                stats.ap()[0:nr, 4:5], stats.ap()[0:nr, 5:6], AF.Exp, scale=-0.5),
                reads=[b_st], writes=[b_st])
            em.dve(lambda e, xa_t=xa_t, x1_t=x1_t, nr=nr: e.tensor_scalar(
                x1_t.ap()[0:nr, :], xa_t.ap()[0:nr, :],
                stats.ap()[0:nr, 2:3], stats.ap()[0:nr, 4:5],
                ALU.subtract, ALU.mult),
                reads=[b_xa_t, b_st], writes=[b_x1_t])
            for g in range(2):
                for j in range(16):
                    dblk = g * 16 + j
                    em.pe(lambda e, x1_t=x1_t, dblk=dblk, j=j, nr=nr: e.transpose(
                        tps.ap()[:, j * 128:j * 128 + nr],
                        x1_t.ap()[0:nr, dblk * 128:(dblk + 1) * 128],
                        ident.ap()[0:nr, 0:nr]),
                        reads=[b_x1_t, b_ident], writes=[b_tps])
                em.dve(lambda e, g=g, r0=r0, nr=nr: e.tensor_copy(
                    x1T.ap()[:, g * 16:(g + 1) * 16, r0:r0 + nr],
                    tps.ap().rearrange("p (j t) -> p j t", j=16)[:, :, 0:nr]),
                    reads=[b_tps], writes=[b_x1T[it]], tag=f"x1Tev{it}_{g}")
        em.fence()

    dt_scope = contextlib.ExitStack()
    with dt_scope as st2:
        sb2 = lambda name, shape, dt: st2.enter_context(nc.sbuf_tensor(name, shape, dt))
        ps2 = lambda name, shape, dt: st2.enter_context(nc.psum_tensor(name, shape, dt))
        wdt = sb2("wdt", [128, 32, NH], BF16)
        b_wdt = Buf("wdt")
        src = dram["W_in"].rearrange("(i p) c -> p i c", p=128)[:, :, PROJ_DIM - NH:]
        for piece in range(4):
            em.dma("pool_d", wdt.ap()[:, piece * 8:(piece + 1) * 8, :],
                   src[:, piece * 8:(piece + 1) * 8, :], writes=[b_wdt])
        dtps = ps2("dtps", [128, NH], F32)
        b_dtps = Buf("dtps")
        dtsb = sb2("dtsb", [128, NH], F32)
        b_dtsb = Buf("dtsb")
        dtp2 = ps2("dtp2", [NH, 128], F32)
        b_dtp2 = Buf("dtp2")
        dtraw = sb2("dtraw", [NH, TPC], F32)
        b_dtraw = Buf("dtraw")
        for ch in range(NCH):
            t0 = HALO + ch * Q
            for d in range(32):
                em.pe(lambda e, d=d, t0=t0: e.matmul(
                    dtps.ap(), x1T.ap()[:, d, t0:t0 + Q], wdt.ap()[:, d, :],
                    start=(d == 0), stop=(d == 31)),
                    reads=[b_x1T[t0 // 128], b_x1T[t0 // 128 + 1], b_wdt],
                    writes=[b_dtps])
            em.dve(lambda e: e.tensor_copy(dtsb.ap(), dtps.ap()),
                   reads=[b_dtps], writes=[b_dtsb])
            em.pe(lambda e: e.transpose(dtp2.ap(), dtsb.ap(), identf.ap()),
                  reads=[b_dtsb, b_identf], writes=[b_dtp2])
            em.dve(lambda e, ch=ch: e.tensor_copy(
                dtraw.ap()[:, ch * Q:(ch + 1) * Q], dtp2.ap()),
                reads=[b_dtp2], writes=[b_dtraw])
        dtb = sb2("dtb", [NH, 1], F32)
        b_dtb = Buf("dtb")
        em.dma("sp_d", dtb.ap(), dram["dt_bias_col"], writes=[b_dtb])
        t_abs = sb2("t_abs", [NH, TPC], F32)
        b_abs = Buf("t_abs")
        dt_soft = sb2("dt_softA", [NH, TPC], F32)
        b_dts = Buf("dt_soft")
        em.act(lambda e: e.activation(t_abs.ap(), dtraw.ap(), AF.Abs, bias=dtb.ap()),
               reads=[b_dtraw, b_dtb], writes=[b_abs])
        em.act(lambda e: e.activation(t_abs.ap(), t_abs.ap(), AF.Exp, scale=-1.0),
               reads=[b_abs], writes=[b_abs])
        em.act(lambda e: e.activation(t_abs.ap(), t_abs.ap(), AF.Ln, bias=1.0),
               reads=[b_abs], writes=[b_abs])
        em.act(lambda e: e.activation(dt_soft.ap(), dtraw.ap(), AF.Relu, bias=dtb.ap()),
               reads=[b_dtraw, b_dtb], writes=[b_dts])
        em.dve(lambda e: e.tensor_add(dt_soft.ap(), dt_soft.ap(), t_abs.ap()),
               reads=[b_abs, b_dts], writes=[b_dts])
        em.dma("sp_d", dram["dt_soft_hbm"], dt_soft.ap(), reads=[b_dts],
               tag="st_dtsoft")
        em.fence()

    cw = sb("cwA", [128, XBCT, KCONV], F32)
    b_cw = Buf("cw")
    em.dma("sp_d", cw.ap(), dram["conv_w"].rearrange("(i p) k -> p i k", p=128),
           writes=[b_cw])

    wpan = [sb(f"wpanA{i}", [128, 32, 128], BF16) for i in range(2)]
    b_wpan = [Buf(f"wpan{i}") for i in range(2)]

    def load_wpanel(c, pa):
        src = dram["W_in"].rearrange("(i p) c -> p i c", p=128)[:, :, c * 128:(c + 1) * 128]
        for piece in range(4):
            em.dma("pool_d", wpan[pa].ap()[:, piece * 8:(piece + 1) * 8, :],
                   src[:, piece * 8:(piece + 1) * 8, :],
                   writes=[b_wpan[pa]], tag=f"ldw{c}_{piece}")

    with contextlib.ExitStack() as st4:
        sb4 = lambda name, shape, dt: st4.enter_context(nc.sbuf_tensor(name, shape, dt))
        ps4 = lambda name, shape, dt: st4.enter_context(nc.psum_tensor(name, shape, dt))
        xps = ps4("xps", [128, 2052], F32)
        b_xps = Buf("xps")
        xbc = [sb4(f"xbc{i}", [128, HALO + 2048], BF16) for i in range(2)]
        b_xbc_lo = [Buf(f"xbcl{i}") for i in range(2)]
        b_xbc_hi = [Buf(f"xbch{i}") for i in range(2)]
        cv = [sb4(f"cv{i}", [128, 2048], BF16) for i in range(4)]
        b_cv = [Buf(f"cv{i}") for i in range(4)]
        xout = [sb4(f"xout{i}", [128, 2048], BF16) for i in range(2)]
        b_xout = [Buf(f"xout{i}") for i in range(2)]
        offs = [0, 512, 1024, 1536, 2048, 2051]
        for ci in range(XBCT):
            c = ZT + ci
            pa = c % 2
            load_wpanel(c, pa)
            for d in range(32):
                for m in range(5):
                    em.pe(lambda e, pa=pa, d=d, m=m: e.matmul(
                        xps.ap()[:, offs[m]:offs[m + 1]],
                        wpan[pa].ap()[:, d, :],
                        x1T.ap()[:, d, offs[m]:offs[m + 1]],
                        start=(d == 0), stop=(d == 31)),
                        reads=[b_wpan[pa]] + b_x1T, writes=[b_xps])
            em.dve(lambda e, pa=pa: e.tensor_copy(
                xbc[pa].ap()[:, 0:1024], xps.ap()[:, 0:1024]),
                reads=[b_xps], writes=[b_xbc_lo[pa]])
            em.act(lambda e, pa=pa: e.activation(
                xbc[pa].ap()[:, 1024:2051], xps.ap()[:, 1024:2051], AF.Copy),
                reads=[b_xps], writes=[b_xbc_hi[pa]])
            if "xbcdbg_hbm" in dram:
                em.dma("act_d", dram["xbcdbg_hbm"][ci * 128:(ci + 1) * 128, :],
                       xbc[pa].ap(), reads=[b_xbc_lo[pa], b_xbc_hi[pa]])
            rbc = [b_xbc_lo[pa], b_xbc_hi[pa], b_cw]
            em.act(lambda e, pa=pa, ci=ci: e.activation(
                cv[0].ap(), xbc[pa].ap()[:, 0:2048], AF.Identity,
                scale=cw.ap()[:, ci, 0:1]),
                reads=rbc, writes=[b_cv[0]])
            em.act(lambda e, pa=pa, ci=ci: e.activation(
                cv[1].ap(), xbc[pa].ap()[:, 1:2049], AF.Identity,
                scale=cw.ap()[:, ci, 1:2]),
                reads=rbc, writes=[b_cv[1]])
            em.dve(lambda e, pa=pa, ci=ci: e.tensor_scalar(
                cv[2].ap(), xbc[pa].ap()[:, 2:2050], cw.ap()[:, ci, 2:3],
                None, ALU.mult),
                reads=rbc, writes=[b_cv[2]])
            em.dve(lambda e, pa=pa, ci=ci: e.tensor_scalar(
                cv[3].ap(), xbc[pa].ap()[:, 3:2051], cw.ap()[:, ci, 3:4],
                None, ALU.mult),
                reads=rbc, writes=[b_cv[3]])
            em.dve(lambda e: e.tensor_add(cv[0].ap(), cv[0].ap(), cv[1].ap()),
                   reads=[b_cv[0], b_cv[1]], writes=[b_cv[0]])
            em.dve(lambda e: e.tensor_add(cv[2].ap(), cv[2].ap(), cv[3].ap()),
                   reads=[b_cv[2], b_cv[3]], writes=[b_cv[2]])
            em.dve(lambda e: e.tensor_add(cv[0].ap(), cv[0].ap(), cv[2].ap()),
                   reads=[b_cv[0], b_cv[2]], writes=[b_cv[0]])
            em.act(lambda e, pa=pa: e.activation(xout[pa].ap(), cv[0].ap(), AF.Silu),
                   reads=[b_cv[0]], writes=[b_xout[pa]])
            em.dma("sp_d", dram["xssm_hbm"][ci * 128:(ci + 1) * 128, :],
                   xout[pa].ap(), reads=[b_xout[pa]], tag=f"stxs{ci}")
        em.fence()


    with contextlib.ExitStack() as st3:
        sb3 = lambda name, shape, dt: st3.enter_context(nc.sbuf_tensor(name, shape, dt))
        ps3 = lambda name, shape, dt: st3.enter_context(nc.psum_tensor(name, shape, dt))
        zps = [ps3(f"zps{i}", [128, 2048], F32) for i in range(2)]
        b_zps = [Buf(f"zps{i}") for i in range(2)]
        zsb = [sb3(f"zsb{i}", [128, 2048], BF16) for i in range(2)]
        b_zsb_lo = [Buf(f"zsbl{i}") for i in range(2)]
        b_zsb_hi = [Buf(f"zsbh{i}") for i in range(2)]
        for c in range(ZT):
            pa = c % 2
            load_wpanel(c, pa)
            for d in range(32):
                for m in range(4):
                    em.pe(lambda e, pa=pa, d=d, m=m: e.matmul(
                        zps[pa].ap()[:, m * 512:(m + 1) * 512],
                        wpan[pa].ap()[:, d, :],
                        x1T.ap()[:, d, HALO + m * 512:HALO + (m + 1) * 512],
                        start=(d == 0), stop=(d == 31)),
                        reads=[b_wpan[pa]] + b_x1T, writes=[b_zps[pa]])
            em.dve(lambda e, pa=pa: e.tensor_copy(
                zsb[pa].ap()[:, 0:1024], zps[pa].ap()[:, 0:1024]),
                reads=[b_zps[pa]], writes=[b_zsb_lo[pa]])
            em.act(lambda e, pa=pa: e.activation(
                zsb[pa].ap()[:, 1024:2048], zps[pa].ap()[:, 1024:2048], AF.Copy),
                reads=[b_zps[pa]], writes=[b_zsb_hi[pa]])
            em.dma("sp_d", dram["z_hbm"][c * 128:(c + 1) * 128, :], zsb[pa].ap(),
                   reads=[b_zsb_lo[pa], b_zsb_hi[pa]], tag=f"stz{c}")
        em.fence()


_DT_NP2MY = {np.dtype(np.float32): F32, np.dtype(ml_dtypes.bfloat16): BF16}
BF = ml_dtypes.bfloat16

IO_A = dict(
    ins=[("xt", (HALO + TPC, H), np.float32),
         ("W_in", (H, PROJ_DIM), np.float32),
         ("conv_w", (CONV_DIM, KCONV), np.float32),
         ("dt_bias_col", (NH, 1), np.float32),
         ("ident_bf16", (128, 128), BF),
         ("ident_f32", (128, 128), np.float32)],
    outs=[("z_hbm", (DSSM, TPC), BF),
          ("xssm_hbm", (CONV_DIM, TPC), BF),
          ("dt_soft_hbm", (NH, TPC), np.float32)])


def build_scan(em, dram, st, summary_only):
    nc = em.nc
    import contextlib
    sb = lambda name, shape, dt: st.enter_context(nc.sbuf_tensor(name, shape, dt))
    ps = lambda name, shape, dt: st.enter_context(nc.psum_tensor(name, shape, dt))
    tag = "B1" if summary_only else "B2"

    identf = sb("identfB", [128, 128], F32)
    b_identf = Buf("identf")
    em.dma("sp_d", identf.ap(), dram["ident_f32"], writes=[b_identf])
    maskW = sb("maskWB", [128, 128], F32)
    b_mask = Buf("maskW")
    em.dma("sp_d", maskW.ap(), dram["maskW"], writes=[b_mask])

    negall = sb("negallB", [128, NCH, NH], F32)
    b_negall = Buf("negall")
    biasall = sb("biasallB", [128, NCH, NH], F32)
    b_biasall = Buf("biasall")
    dwall = sb("dwallB", [128, NCH, NH], BF16)
    b_dwall = Buf("dwall")
    decCall = sb("decCallB", [128, NCH, NH], F32)
    b_decCall = Buf("decCall")
    eTall = sb("eTallB", [128, NCH, NH], F32)
    b_eTall = Buf("eTall")

    with contextlib.ExitStack() as stp:
        sbp = lambda name, shape, dt: stp.enter_context(nc.sbuf_tensor(name, shape, dt))
        psP = stp.enter_context(nc.psum_tensor("psPB", [128, 128], F32))
        b_psP = Buf("psP")
        dts = sbp("dtsB", [NH, TPC], F32)
        b_dts = Buf("dts")
        em.dma("sp_d", dts.ap(), dram["dt_soft_hbm"], writes=[b_dts])
        acol = sbp("acolB", [NH, 2], F32)
        b_acol = Buf("acol")
        em.dma("sp_d", acol.ap()[:, 0:1], dram["A_log_col"], writes=[b_acol])
        em.act(lambda e: e.activation(acol.ap()[:, 1:2], acol.ap()[:, 0:1], AF.Exp,
                                      scale=1.0), reads=[b_acol], writes=[b_acol])
        em.dve(lambda e: e.tensor_scalar(acol.ap()[:, 1:2], acol.ap()[:, 1:2], -1.0,
                                         None, ALU.mult), reads=[b_acol],
               writes=[b_acol])
        av = sbp("avB", [NH, TPC], F32)
        b_av = Buf("av")
        em.dve(lambda e: e.tensor_scalar(av.ap(), dts.ap(), acol.ap()[:, 1:2], None,
                                         ALU.mult), reads=[b_dts, b_acol],
               writes=[b_av])
        ones64 = sbp("ones64B", [NH, Q], F32)
        b_ones64 = Buf("ones64")
        em.dve(lambda e: e.memset(ones64.ap(), 1.0), writes=[b_ones64])
        clocal = sbp("clocalB", [NH, TPC], F32)
        b_cl = Buf("clocal")
        for ch in range(NCH):
            em.dve(lambda e, ch=ch: e.tensor_tensor_scan(
                clocal.ap()[:, ch * Q:(ch + 1) * Q], ones64.ap(),
                av.ap()[:, ch * Q:(ch + 1) * Q], 0.0, ALU.mult, ALU.add),
                reads=[b_av, b_ones64], writes=[b_cl])
        totsb = sbp("totsbB", [NH, NCH], F32)
        b_tot = Buf("totsb")
        em.dve(lambda e: e.tensor_copy(totsb.ap(), clocal.ap()[:, Q - 1::Q]),
               reads=[b_cl], writes=[b_tot])
        totT = sbp("totTB", [NCH, NH], F32)
        b_totT = Buf("totT")
        decT = sbp("decTB", [NCH, NH], F32)
        b_decT = Buf("decT")
        em.pe(lambda e: e.transpose(psP.ap()[0:NCH, 0:NH], totsb.ap(),
                                    identf.ap()[0:NH, 0:NH]),
              reads=[b_tot, b_identf], writes=[b_psP])
        em.dve(lambda e: e.tensor_copy(totT.ap(), psP.ap()[0:NCH, 0:NH]),
               reads=[b_psP], writes=[b_totT])
        em.act(lambda e: e.activation(decT.ap(), totT.ap(), AF.Exp),
               reads=[b_totT], writes=[b_decT])
        eye16 = sbp("eye16B", [NCH, TPC], F32)
        b_eye16 = Buf("eye16")
        em.dma("sp_d", eye16.ap(), dram["eye16rep"], writes=[b_eye16])
        if summary_only:
            dtot = sbp("dtotB", [NH, 2], F32)
            b_dtot = Buf("dtot")
            em.dve(lambda e: e.tensor_reduce(dtot.ap()[:, 0:1], totsb.ap(), AX.X,
                                             ALU.add), reads=[b_tot],
                   writes=[b_dtot])
            em.act(lambda e: e.activation(dtot.ap()[:, 1:2], dtot.ap()[:, 0:1],
                                          AF.Exp), reads=[b_dtot], writes=[b_dtot])
            if "cc_D_in" in dram:
                drow_o = sbp("drowoB", [1, NH], F32)
                b_drow_o = Buf("drow_o")
                em.pe(lambda e: e.transpose(psP.ap()[0:1, 0:NH], dtot.ap()[:, 1:2],
                                            identf.ap()[0:NH, 0:NH]),
                      reads=[b_dtot, b_identf], writes=[b_psP])
                em.dve(lambda e: e.tensor_copy(drow_o.ap(), psP.ap()[0:1, 0:NH]),
                       reads=[b_psP], writes=[b_drow_o])
                em.dma("sp_d", dram["cc_D_in"], drow_o.ap(), reads=[b_drow_o])
            else:
                em.dma("sp_d", dram["Dt_col"], dtot.ap()[:, 1:2], reads=[b_dtot])
        wtmp = sbp("wtmpB", [128, 2 * NH], F32)
        b_wtmp = Buf("wtmp")
        dtf = sbp("dtfB", [128, NH], F32)
        b_dtf = Buf("dtf")
        lnt = sbp("lntB", [128, NH], F32)
        b_lnt = Buf("lnt")
        for ch in range(NCH):
            csl = slice(ch * Q, (ch + 1) * Q)
            em.pe(lambda e, csl=csl: e.transpose(
                psP.ap()[:, 0:NH], clocal.ap()[:, csl], identf.ap()[0:NH, 0:NH]),
                reads=[b_cl, b_identf], writes=[b_psP])
            em.dve(lambda e, ch=ch: e.tensor_scalar(
                negall.ap()[:, ch, :], psP.ap()[:, 0:NH], -1.0, None, ALU.mult),
                reads=[b_psP], writes=[b_negall])
            em.pe(lambda e, csl=csl: e.transpose(
                psP.ap()[:, 0:NH], dts.ap()[:, csl], identf.ap()[0:NH, 0:NH]),
                reads=[b_dts, b_identf, b_negall], writes=[b_psP])
            em.dve(lambda e: e.tensor_copy(dtf.ap(), psP.ap()[:, 0:NH]),
                   reads=[b_psP], writes=[b_dtf])
            em.act(lambda e: e.activation(lnt.ap(), dtf.ap(), AF.Ln),
                   reads=[b_dtf], writes=[b_lnt])
            em.dve(lambda e, ch=ch: e.tensor_add(
                biasall.ap()[:, ch, :], negall.ap()[:, ch, :], lnt.ap()),
                reads=[b_negall, b_lnt], writes=[b_biasall])
            em.act(lambda e, ch=ch: e.activation(
                eTall.ap()[:, ch, :], negall.ap()[:, ch, :], AF.Exp, scale=-1.0),
                reads=[b_negall], writes=[b_eTall])
            em.pe(lambda e, csl=csl: e.matmul(
                psP.ap()[:, 0:NH], eye16.ap()[:, csl], totT.ap(),
                start=True, stop=True),
                reads=[b_eye16, b_totT, b_lnt], writes=[b_psP])
            em.dve(lambda e, ch=ch: e.tensor_add(
                wtmp.ap()[:, 0:NH], psP.ap()[:, 0:NH], negall.ap()[:, ch, :]),
                reads=[b_psP, b_negall], writes=[b_wtmp])
            em.act(lambda e: e.activation(wtmp.ap()[:, NH:2 * NH],
                                          wtmp.ap()[:, 0:NH], AF.Exp),
                   reads=[b_wtmp], writes=[b_wtmp])
            em.pool(lambda e, ch=ch: e.tensor_tensor(
                dwall.ap()[:, ch, :], dtf.ap(),
                wtmp.ap()[:, NH:2 * NH], ALU.mult),
                reads=[b_wtmp, b_dtf], writes=[b_dwall])
            em.pe(lambda e, csl=csl: e.matmul(
                psP.ap()[:, 0:NH], eye16.ap()[:, csl], decT.ap(),
                start=True, stop=True),
                reads=[b_eye16, b_decT, b_wtmp], writes=[b_psP])
            em.dve(lambda e, ch=ch: e.tensor_copy(
                decCall.ap()[:, ch, :], psP.ap()[:, 0:NH]),
                reads=[b_psP], writes=[b_decCall])
        em.fence()

    Bt = sb("BtB", [128, TPC], BF16)
    b_Bt = Buf("Bt")
    em.dma("sp_d", Bt.ap(), dram["xssm_hbm"][DSSM:DSSM + NST, :], writes=[b_Bt])
    Ct = sb("CtB", [128, TPC], BF16)
    b_Ct = Buf("Ct")
    em.dma("sp_d", Ct.ap(), dram["xssm_hbm"][DSSM + NST:, :], writes=[b_Ct])

    hst = sb("hstB", [128, DSSM], F32)
    b_h = [Buf(f"h{g}") for g in range(8)]
    if summary_only:
        psS_ = [ps(f"psSB{i}", [128, 1024], F32) for i in range(2)]
        b_psS_ = [Buf(f"psS{i}") for i in range(2)]
    else:
        psS_ = [ps("psSB", [128, 1024], F32)] * 2
        b_psS_ = [Buf("psS")] * 2
    psS, b_psS = psS_[0], b_psS_[0]
    if summary_only:
        em.dve(lambda e: e.memset(hst.ap(), 0.0), writes=b_h)
        hbf, b_hbf = None, None
    else:
        hbf = sb("hbfB", [128, DSSM], BF16)
        b_hbf = [Buf(f"hbf{g}") for g in range(8)]
        onesf = sb("onesfB", [1, 128], F32)
        b_onesf = Buf("onesf")
        em.dve(lambda e: e.memset(onesf.ap(), 1.0), writes=[b_onesf])
        drow = sb("drowB", [1, 3 * NH], F32)
        b_drow = Buf("drow")
        with contextlib.ExitStack() as sti:
            stage = sti.enter_context(nc.sbuf_tensor("stageB", [128, DSSM], F32))
            b_stage = Buf("stage")
            em.dma("sp_d", stage.ap(), dram["S_in0"], writes=[b_stage])
            em.dve(lambda e: e.tensor_copy(hst.ap(), stage.ap()),
                   reads=[b_stage], writes=b_h)
            for j in (1, 2):
                em.dma("sp_d", drow.ap()[:, (j - 1) * NH:j * NH],
                       dram[f"Dt_row{j}"], writes=[b_drow])
                em.pe(lambda e, j=j: e.matmul(
                    psS.ap()[:, 0:NH], onesf.ap(), drow.ap()[:, (j - 1) * NH:j * NH],
                    start=True, stop=True),
                    reads=[b_drow, b_onesf], writes=[b_psS])
                em.dve(lambda e: e.tensor_tensor(
                    hst.ap().rearrange("p (h q) -> p h q", q=P),
                    hst.ap().rearrange("p (h q) -> p h q", q=P),
                    psS.ap()[:, 0:NH].unsqueeze(2).broadcast_to([128, NH, P]),
                    ALU.mult),
                    reads=[b_psS] + b_h, writes=b_h)
                em.dma("sp_d", stage.ap(), dram[f"S_in{j}"], writes=[b_stage])
                em.dve(lambda e: e.tensor_add(hst.ap(), hst.ap(), stage.ap()),
                       reads=[b_stage] + b_h, writes=b_h)
            for g in range(8):
                em.act(lambda e, g=g: e.activation(
                    hbf.ap()[:, g * 1024:(g + 1) * 1024],
                    hst.ap()[:, g * 1024:(g + 1) * 1024], AF.Copy),
                    reads=[b_h[g]], writes=[b_hbf[g]])
            em.fence()

    xT = [sb(f"xTB{i}", [128, CONV_DIM], BF16) for i in range(2)]
    b_xT = [Buf(f"xT{i}") for i in range(2)]
    dtx2 = sb("dtx2B", [128, DSSM], BF16)
    b_dtx2 = Buf("dtx2")
    htmp = sb("htmpB", [128, 1024], F32)
    b_htmp = Buf("htmp")

    if not summary_only:
        ddiag = sb("ddiagB", [128, NH, 128], BF16)
        b_ddiag = Buf("ddiag")
        identb2 = sb("identb2B", [128, 128], BF16)
        b_identb2 = Buf("identb2")
        em.dma("sp_d", identb2.ap(), dram["ident_bf16"], writes=[b_identb2])
        drepf = sb("drepfB", [128, NH], F32)
        b_drepf = Buf("drepf")
        em.dma("sp_d", drepf.ap(), dram["D_rep"], writes=[b_drepf])
        for hh in range(NH):
            em.dve(lambda e, hh=hh: e.tensor_scalar(
                ddiag.ap()[:, hh, :], identb2.ap(), drepf.ap()[:, hh:hh + 1],
                None, ALU.mult),
                reads=[b_identb2, b_drepf], writes=[b_ddiag])
        Wsb = [sb(f"WsbB{i}", [128, 1024], F32) for i in range(2)]
        b_Wsb = [Buf(f"Wsb{i}") for i in range(2)]
        MtA = [sb(f"MtAB{i}", [128, 8, 1024], BF16) for i in range(2)]
        b_MtA = [[Buf(f"MtA{i}_{g}") for g in range(8)] for i in range(2)]
        Gtb2 = [sb(f"GtbB{i}", [128, 128], BF16) for i in range(2)]
        b_Gtb2 = [Buf(f"Gtb{i}") for i in range(2)]
        ysb = sb("ysbB", [128, 1024], F32)
        b_ysb = Buf("ysb")
        yo2 = [sb(f"yo2B{i}", [128, 1024], BF16) for i in range(2)]
        b_yo2 = [Buf(f"yo2{i}") for i in range(2)]
        psE = ps("psEB", [128, 1024], F32)
        b_psE = Buf("psE")
        psI = ps("psIB", [128, 1024], F32)
        b_psI = Buf("psI")
        psN = ps("psNB", [128, 1024], F32)
        b_psN = Buf("psN")

    def b2_gt(ch):
        cpp = ch % 2
        tc = slice(ch * Q, (ch + 1) * Q)
        em.pe(lambda e, tc=tc: e.matmul(
            psE.ap()[:, 0:128], Bt.ap()[:, tc], Ct.ap()[:, tc],
            start=True, stop=True),
            reads=[b_Bt, b_Ct], writes=[b_psE])
        em.dve(lambda e, cpp=cpp: e.tensor_copy(Gtb2[cpp].ap(), psE.ap()[:, 0:128]),
               reads=[b_psE], writes=[b_Gtb2[cpp]])

    def b2_prod_W(ch, g):
        pp = g % 2
        em.pool(lambda e, g=g, ch=ch, pp=pp: e.tensor_tensor(
            Wsb[pp].ap().rearrange("p (h s) -> p h s", s=Q),
            maskW.ap().unsqueeze(1).broadcast_to([128, 8, Q]),
            negall.ap()[:, ch, g * 8:(g + 1) * 8].unsqueeze(2)
                .broadcast_to([128, 8, Q]),
            ALU.subtract),
            reads=[b_negall, b_mask], writes=[b_Wsb[pp]])

    def b2_prod_tp(ch, g):
        pp = g % 2
        for j in range(8):
            em.pe(lambda e, j=j, pp=pp: e.transpose(
                psE.ap()[:, j * Q:(j + 1) * Q],
                Wsb[pp].ap()[:, j * Q:(j + 1) * Q], identf.ap()),
                reads=[b_Wsb[pp], b_identf], writes=[b_psE])

    def b2_prod_exp(ch, g):
        cpp = ch % 2
        for j in range(8):
            em.act(lambda e, g=g, j=j, ch=ch, cpp=cpp: e.activation(
                MtA[cpp].ap()[:, g, j * Q:(j + 1) * Q],
                psE.ap()[:, j * Q:(j + 1) * Q], AF.Exp,
                bias=biasall.ap()[:, ch, g * 8 + j:g * 8 + j + 1]),
                reads=[b_psE, b_biasall], writes=[b_MtA[cpp][g]])
        em.pool(lambda e, g=g, cpp=cpp: e.tensor_tensor(
            MtA[cpp].ap()[:, g, :].rearrange("p (h s) -> p h s", s=Q),
            MtA[cpp].ap()[:, g, :].rearrange("p (h s) -> p h s", s=Q),
            Gtb2[cpp].ap().unsqueeze(1).broadcast_to([128, 8, Q]),
            ALU.mult),
            reads=[b_Gtb2[cpp]], writes=[b_MtA[cpp][g]])

    def b2_dtx(ch):
        pa = ch % 2
        em.dve(lambda e, pa=pa, ch=ch: e.tensor_tensor(
            dtx2.ap().rearrange("p (h q) -> p h q", q=P),
            xT[pa].ap()[:, 0:DSSM].rearrange("p (h q) -> p h q", q=P),
            dwall.ap()[:, ch, :].unsqueeze(2).broadcast_to([128, NH, P]),
            ALU.mult),
            reads=[b_xT[pa], b_dwall], writes=[b_dtx2])

    def b2_cons_mm(ch, g):
        cpp = ch % 2
        pa = ch % 2
        tc = slice(ch * Q, (ch + 1) * Q)
        for m in range(2):
            em.pe(lambda e, g=g, m=m, tc=tc: e.matmul(
                psN.ap()[:, m * 512:(m + 1) * 512],
                Ct.ap()[:, tc],
                hbf.ap()[:, g * 1024 + m * 512:g * 1024 + (m + 1) * 512],
                start=True, stop=True),
                reads=[b_Ct, b_hbf[g]], writes=[b_psN])
        for j in range(8):
            em.pe(lambda e, g=g, j=j, cpp=cpp, pa=pa: e.matmul(
                psI.ap()[:, j * Q:(j + 1) * Q],
                MtA[cpp].ap()[:, g, j * Q:(j + 1) * Q],
                xT[pa].ap()[:, (g * 8 + j) * P:(g * 8 + j + 1) * P],
                start=True, stop=False),
                reads=[b_MtA[cpp][g], b_xT[pa]], writes=[b_psI])
            em.pe(lambda e, g=g, j=j, pa=pa: e.matmul(
                psI.ap()[:, j * Q:(j + 1) * Q],
                ddiag.ap()[:, g * 8 + j, :],
                xT[pa].ap()[:, (g * 8 + j) * P:(g * 8 + j + 1) * P],
                start=False, stop=True),
                reads=[b_ddiag, b_xT[pa]], writes=[b_psI])
        for m in range(2):
            em.pe(lambda e, g=g, m=m, pa=pa: e.matmul(
                psS_[0].ap()[:, m * 512:(m + 1) * 512],
                xT[pa].ap()[:, DSSM:DSSM + NST],
                dtx2.ap()[:, g * 1024 + m * 512:g * 1024 + (m + 1) * 512],
                start=True, stop=True),
                reads=[b_xT[pa], b_dtx2], writes=[b_psS_[0]])

    def b2_cons_ev(ch, g):
        hsl = slice(g * 1024, (g + 1) * 1024)
        em.dve(lambda e, g=g, ch=ch: e.tensor_tensor(
            ysb.ap().rearrange("p (h q) -> p h q", q=P),
            psN.ap().rearrange("p (h q) -> p h q", q=P),
            eTall.ap()[:, ch, g * 8:(g + 1) * 8].unsqueeze(2)
                 .broadcast_to([128, 8, P]),
            ALU.mult),
            reads=[b_psN, b_eTall], writes=[b_ysb])
        gy = g % 2
        em.dve(lambda e, gy=gy: e.tensor_add(
            yo2[gy].ap(), ysb.ap(), psI.ap()),
            reads=[b_ysb, b_psI], writes=[b_yo2[gy]])
        em.dma("sp_d", dram["y_hbm"][ch * Q:(ch + 1) * Q, hsl], yo2[gy].ap(),
               reads=[b_yo2[gy]], tag=f"sty{ch}_{g}")
        for j in range(8):
            em.dve(lambda e, g=g, j=j, ch=ch: e.scalar_tensor_tensor(
                hst.ap()[:, (g * 8 + j) * P:(g * 8 + j + 1) * P],
                hst.ap()[:, (g * 8 + j) * P:(g * 8 + j + 1) * P],
                decCall.ap()[:, ch, g * 8 + j:g * 8 + j + 1],
                psS_[0].ap()[:, j * P:(j + 1) * P],
                ALU.mult, ALU.add),
                reads=[b_h[g], b_decCall, b_psS_[0]], writes=[b_h[g]])
        em.act(lambda e, hsl=hsl: e.activation(
            hbf.ap()[:, hsl], hst.ap()[:, hsl], AF.Copy),
            reads=[b_h[g]], writes=[b_hbf[g]])

    if summary_only:
        for ch in range(NCH):
            tc = slice(ch * Q, (ch + 1) * Q)
            pa = ch % 2
            em.dma("sp_d", xT[pa].ap(), dram["xssm_hbm"][:, tc],
                   writes=[b_xT[pa]], tag=f"{tag}xT{ch}", transpose=True)
            em.dve(lambda e, pa=pa, ch=ch: e.tensor_tensor(
                dtx2.ap().rearrange("p (h q) -> p h q", q=P),
                xT[pa].ap()[:, 0:DSSM].rearrange("p (h q) -> p h q", q=P),
                dwall.ap()[:, ch, :].unsqueeze(2).broadcast_to([128, NH, P]),
                ALU.mult),
                reads=[b_xT[pa], b_dwall], writes=[b_dtx2])
            for g in range(8):
                hsl = slice(g * 1024, (g + 1) * 1024)
                ppS = g % 2
                for m in range(2):
                    em.pe(lambda e, g=g, m=m, pa=pa, ppS=ppS: e.matmul(
                        psS_[ppS].ap()[:, m * 512:(m + 1) * 512],
                        xT[pa].ap()[:, DSSM:DSSM + NST],
                        dtx2.ap()[:, g * 1024 + m * 512:g * 1024 + (m + 1) * 512],
                        start=True, stop=True),
                        reads=[b_xT[pa], b_dtx2], writes=[b_psS_[ppS]])
                em.dve(lambda e, g=g, ch=ch, hsl=hsl: e.tensor_tensor(
                    htmp.ap().rearrange("p (h q) -> p h q", q=P),
                    hst.ap()[:, hsl].rearrange("p (h q) -> p h q", q=P),
                    decCall.ap()[:, ch, g * 8:(g + 1) * 8].unsqueeze(2)
                           .broadcast_to([128, 8, P]),
                    ALU.mult),
                    reads=[b_h[g], b_decCall], writes=[b_htmp])
                em.dve(lambda e, hsl=hsl, ppS=ppS: e.tensor_add(
                    hst.ap()[:, hsl], htmp.ap(), psS_[ppS].ap()),
                    reads=[b_htmp, b_psS_[ppS]], writes=[b_h[g]])
    else:
        for ch in range(NCH + 1):
            if ch < NCH:
                tc = slice(ch * Q, (ch + 1) * Q)
                pa = ch % 2
                em.dma("sp_d", xT[pa].ap(), dram["xssm_hbm"][:, tc],
                       writes=[b_xT[pa]], tag=f"{tag}xT{ch}", transpose=True)
                b2_gt(ch)
            if ch >= 1:
                b2_dtx(ch - 1)
            for g in range(8):
                if ch < NCH:
                    b2_prod_W(ch, g)
                if ch >= 1:
                    b2_cons_mm(ch - 1, g)
                if ch < NCH:
                    b2_prod_tp(ch, g)
                    b2_prod_exp(ch, g)
                if ch >= 1:
                    b2_cons_ev(ch - 1, g)
    if summary_only:
        tgt = dram["cc_S_in"] if "cc_S_in" in dram else dram["S_own"]
        em.dma("sp_d", tgt, hst.ap(), reads=b_h, tag="stS")
    em.fence()



def build_c1(em, dram, st):
    nc = em.nc
    import contextlib
    sb = lambda name, shape, dt: st.enter_context(nc.sbuf_tensor(name, shape, dt))
    ps = lambda name, shape, dt: st.enter_context(nc.psum_tensor(name, shape, dt))

    nw = sb("nwC", [128, NH], F32)
    b_nw = Buf("nw")
    em.dma("sp_d", nw.ap(), dram["nw_col"], writes=[b_nw])
    onesb = sb("onesbC", [128, 1], BF16)
    b_onesb = Buf("onesb")
    em.dma("sp_d", onesb.ap(), dram["ones_col_bf16"], writes=[b_onesb])
    onesf = sb("onesfC", [1, 128], F32)
    b_onesf = Buf("onesf")
    em.dve(lambda e: e.memset(onesf.ap(), 1.0), writes=[b_onesf])

    ygnw = sb("ygnwC", [128, NH, 1024], BF16)
    b_yg = [Buf(f"yg{j}") for j in range(NH)]
    zt = [sb(f"ztC{i}", [128, 1024], BF16) for i in range(2)]
    b_zt = [Buf(f"zt{i}") for i in range(2)]
    zs = [sb(f"zsC{i}", [128, 1024], BF16) for i in range(2)]
    b_zs = [Buf(f"zs{i}") for i in range(2)]
    sq = [sb(f"sqC{i}", [128, 1024], BF16) for i in range(2)]
    b_sq = [Buf(f"sq{i}") for i in range(2)]
    psV = ps("psVC", [1, 1024], F32)
    b_psV = Buf("psV")
    psR = ps("psRC", [128, 1024], F32)
    b_psR = Buf("psR")
    rmsrow = sb("rmsrowC", [1, 1024], F32)
    b_rms = Buf("rmsrow")
    invrep = sb("invrepC", [128, 1024], F32)
    b_invrep = Buf("invrep")
    wpan = [sb(f"wpanC{i}", [128, NH, 128], BF16) for i in range(2)]
    b_wpan = [Buf(f"wpanC{i}") for i in range(2)]
    psA = [ps(f"psAC{i}", [128, 1024], F32) for i in range(2)]
    b_psA = [Buf(f"psA{i}") for i in range(2)]
    att = [sb(f"attC{i}", [128, 1024], BF16) for i in range(2)]
    b_att = [Buf(f"att{i}") for i in range(2)]

    for hf in range(2):
        tsl = slice(hf * 1024, (hf + 1) * 1024)
        for j in range(NH):
            pa = j % 2
            em.dma("sp_d", ygnw.ap()[:, j, :], dram["y_hbm"][tsl, j * 128:(j + 1) * 128],
                   writes=[b_yg[j]], transpose=True, tag=f"ldy{hf}_{j}")
            em.dma("act_d", zt[pa].ap(), dram["z_hbm"][j * 128:(j + 1) * 128, tsl],
                   writes=[b_zt[pa]], tag=f"ldz{hf}_{j}")
            em.act(lambda e, pa=pa: e.activation(zs[pa].ap(), zt[pa].ap(), AF.Silu),
                   reads=[b_zt[pa]], writes=[b_zs[pa]])
            em.dve(lambda e, j=j, pa=pa: e.tensor_tensor(
                ygnw.ap()[:, j, :], ygnw.ap()[:, j, :], zs[pa].ap(), ALU.mult),
                reads=[b_zs[pa], b_yg[j]], writes=[b_yg[j]])
            em.act(lambda e, j=j, pa=pa: e.activation(
                sq[pa].ap(), ygnw.ap()[:, j, :], AF.Square),
                reads=[b_yg[j]], writes=[b_sq[pa]])
            for m in range(2):
                em.pe(lambda e, pa=pa, m=m, j=j: e.matmul(
                    psV.ap()[:, m * 512:(m + 1) * 512], onesb.ap(),
                    sq[pa].ap()[:, m * 512:(m + 1) * 512],
                    start=(j == 0), stop=(j == NH - 1)),
                    reads=[b_sq[pa], b_onesb], writes=[b_psV])
            em.dve(lambda e, j=j: e.tensor_scalar(
                ygnw.ap()[:, j, :], ygnw.ap()[:, j, :], nw.ap()[:, j:j + 1],
                None, ALU.mult),
                reads=[b_nw, b_yg[j]], writes=[b_yg[j]])
        em.act(lambda e: e.activation(rmsrow.ap(), psV.ap(), AF.Ln,
                                      bias=EPS, scale=1.0 / DSSM),
               reads=[b_psV], writes=[b_rms])
        em.act(lambda e: e.activation(rmsrow.ap(), rmsrow.ap(), AF.Exp, scale=-0.5),
               reads=[b_rms], writes=[b_rms])
        for m in range(2):
            em.pe(lambda e, m=m: e.matmul(
                psR.ap()[:, m * 512:(m + 1) * 512], onesf.ap(),
                rmsrow.ap()[:, m * 512:(m + 1) * 512], start=True, stop=True),
                reads=[b_rms, b_onesf], writes=[b_psR])
        em.dve(lambda e: e.tensor_copy(invrep.ap(), psR.ap()),
               reads=[b_psR], writes=[b_invrep])
        for d in range(32):
            pa = d % 2
            src = dram["W_out"].rearrange("(i p) c -> p i c", p=128)[
                :, :, d * 128:(d + 1) * 128]
            for piece in range(8):
                em.dma("pool_d", wpan[pa].ap()[:, piece * 8:(piece + 1) * 8, :],
                       src[:, piece * 8:(piece + 1) * 8, :],
                       writes=[b_wpan[pa]], tag=f"ldWo{hf}_{d}_{piece}")
            for j in range(NH):
                for m in range(2):
                    em.pe(lambda e, pa=pa, j=j, m=m: e.matmul(
                        psA[pa].ap()[:, m * 512:(m + 1) * 512],
                        wpan[pa].ap()[:, j, :],
                        ygnw.ap()[:, j, m * 512:(m + 1) * 512],
                        start=(j == 0), stop=(j == NH - 1)),
                        reads=[b_wpan[pa], b_yg[j]], writes=[b_psA[pa]])
            em.dve(lambda e, pa=pa: e.tensor_tensor(
                att[pa].ap(), psA[pa].ap(), invrep.ap(), ALU.mult),
                reads=[b_psA[pa], b_invrep], writes=[b_att[pa]])
            em.dma("sp_d", dram["attn_hbm"][d * 128:(d + 1) * 128, tsl],
                   att[pa].ap(), reads=[b_att[pa]], tag=f"stat{hf}_{d}")
        em.fence()



def build_res_ln(em, dram, st, which):
    nc = em.nc
    sb = lambda name, shape, dt: st.enter_context(nc.sbuf_tensor(name + which, shape, dt))
    ps = lambda name, shape, dt: st.enter_context(nc.psum_tensor(name + which, shape, dt))
    src_ch = dram["attn_hbm"] if which == "c2" else dram["ffn_hbm"]

    ident = sb("identR", [128, 128], BF16)
    b_ident = Buf("ident")
    em.dma("sp_d", ident.ap(), dram["ident_bf16"], writes=[b_ident])

    at = [sb(f"atR{i}", [128, H], BF16) for i in range(2)]
    b_at = [Buf(f"at{i}") for i in range(2)]
    af = [sb(f"afR{i}", [128, H], F32) for i in range(2)]
    b_af = [Buf(f"af{i}") for i in range(2)]
    xm = [sb(f"xmR{i}", [128, H], F32) for i in range(2)]
    b_xm = [Buf(f"xm{i}") for i in range(2)]
    res = [sb(f"resR{i}", [128, H], F32) for i in range(2)]
    b_res = [Buf(f"res{i}") for i in range(2)]
    sq = [sb(f"sqR{i}", [128, H], BF16) for i in range(2)]
    b_sq = [Buf(f"sq{i}") for i in range(2)]
    stats = [sb(f"statsR{i}", [128, 8], F32) for i in range(2)]
    b_st_ = [Buf(f"stats{i}") for i in range(2)]
    if which == "c2":
        x2t = [sb(f"x2tR{i}", [128, H], BF16) for i in range(2)]
        b_x2t = [Buf(f"x2t{i}") for i in range(2)]
        tps = ps("tpsR", [128, 2048], BF16)
        b_tps = Buf("tps")
        evb = sb("evbR", [128, 16, 128], BF16)
        b_evb = Buf("evb")

    for it in range(16):
        r0 = it * 128
        pa = it % 2
        col = lambda i, pa=pa: stats[pa].ap()[:, i:i + 1]
        b_st = b_st_[pa]
        sq_t, b_sq_t = sq[pa], b_sq[pa]
        em.dma("sp_d", at[pa].ap(), src_ch[:, r0:r0 + 128], writes=[b_at[pa]],
               transpose=True, tag=f"ldat{it}")
        if which == "c2":
            em.dma("act_d", res[pa].ap(), dram["xt"][HALO + r0:HALO + r0 + 128, :],
                   writes=[b_res[pa]], tag=f"ldx{it}")
        else:
            em.dma("act_d", res[pa].ap(), dram["xmid_hbm"][r0:r0 + 128, :],
                   writes=[b_res[pa]], tag=f"ldxm{it}")
        em.dve(lambda e, pa=pa: e.tensor_copy(af[pa].ap(), at[pa].ap()),
               reads=[b_at[pa]], writes=[b_af[pa]])
        em.dve(lambda e, pa=pa, col=col: e.tensor_reduce(
            col(0), af[pa].ap(), AX.X, ALU.add),
            reads=[b_af[pa]], writes=[b_st])
        em.act(lambda e, pa=pa, sq_t=sq_t, col=col: e.activation(
            sq_t.ap(), at[pa].ap(), AF.Square, accum_out=col(1)),
            reads=[b_at[pa]], writes=[b_sq_t, b_st])
        em.dve(lambda e, col=col: e.tensor_scalar(col(2), col(0), 1.0 / H, None, ALU.mult),
               reads=[b_st], writes=[b_st])
        em.dve(lambda e, col=col: e.tensor_scalar(col(3), col(1), 1.0 / H, None, ALU.mult),
               reads=[b_st], writes=[b_st])
        em.dve(lambda e, col=col: e.tensor_mul(col(5), col(2), col(2)),
               reads=[b_st], writes=[b_st])
        em.dve(lambda e, col=col: e.tensor_sub(col(3), col(3), col(5)),
               reads=[b_st], writes=[b_st])
        em.act(lambda e, col=col: e.activation(col(5), col(3), AF.Ln, bias=EPS),
               reads=[b_st], writes=[b_st])
        em.act(lambda e, col=col: e.activation(col(4), col(5), AF.Exp, scale=-0.5),
               reads=[b_st], writes=[b_st])
        em.dve(lambda e, pa=pa, col=col: e.tensor_scalar(
            af[pa].ap(), af[pa].ap(), col(2), col(4), ALU.subtract, ALU.mult),
            reads=[b_st, b_af[pa]], writes=[b_af[pa]])
        if which == "c5":
            em.dve(lambda e, pa=pa: e.tensor_add(
                xm[pa].ap(), af[pa].ap(), res[pa].ap()),
                reads=[b_af[pa], b_res[pa]], writes=[b_xm[pa]])
            em.dma("sp_d", dram["out_hbm"][r0:r0 + 128, :], xm[pa].ap(),
                   reads=[b_xm[pa]], tag=f"stout{it}")
            continue
        em.op("dve", lambda e, pa=pa, col=col: e.scalar_tensor_tensor(
            xm[pa].ap(), af[pa].ap(), 0.0, res[pa].ap(), ALU.add, ALU.add,
            accum_out=col(6)),
            reads=[b_af[pa], b_res[pa]], writes=[b_xm[pa], b_st])
        em.dma("sp_d", dram["xmid_hbm"][r0:r0 + 128, :], xm[pa].ap(),
               reads=[b_xm[pa]], tag=f"stxm{it}")
        em.act(lambda e, pa=pa, sq_t=sq_t, col=col: e.activation(
            sq_t.ap(), xm[pa].ap(), AF.Square, accum_out=col(7)),
            reads=[b_xm[pa]], writes=[b_sq_t, b_st])
        em.dve(lambda e, col=col: e.tensor_scalar(col(2), col(6), 1.0 / H, None, ALU.mult),
               reads=[b_st], writes=[b_st])
        em.dve(lambda e, col=col: e.tensor_scalar(col(3), col(7), 1.0 / H, None, ALU.mult),
               reads=[b_st], writes=[b_st])
        em.dve(lambda e, col=col: e.tensor_mul(col(5), col(2), col(2)),
               reads=[b_st], writes=[b_st])
        em.dve(lambda e, col=col: e.tensor_sub(col(3), col(3), col(5)),
               reads=[b_st], writes=[b_st])
        em.act(lambda e, col=col: e.activation(col(5), col(3), AF.Ln, bias=EPS),
               reads=[b_st], writes=[b_st])
        em.act(lambda e, col=col: e.activation(col(4), col(5), AF.Exp, scale=-0.5),
               reads=[b_st], writes=[b_st])
        em.dve(lambda e, pa=pa, col=col: e.tensor_scalar(
            x2t[pa].ap(), xm[pa].ap(), col(2), col(4), ALU.subtract, ALU.mult),
            reads=[b_st, b_xm[pa]], writes=[b_x2t[pa]])
        for g in range(2):
            for jj in range(16):
                dblk = g * 16 + jj
                em.pe(lambda e, pa=pa, dblk=dblk, jj=jj: e.transpose(
                    tps.ap()[:, jj * 128:(jj + 1) * 128],
                    x2t[pa].ap()[:, dblk * 128:(dblk + 1) * 128], ident.ap()),
                    reads=[b_x2t[pa], b_ident], writes=[b_tps])
            em.dve(lambda e: e.tensor_copy(
                evb.ap(), tps.ap().rearrange("p (j t) -> p j t", j=16)),
                reads=[b_tps], writes=[b_evb])
            em.dma("sp_d",
                   dram["x2T_hbm"].rearrange("(i p) t -> p i t", p=128)[
                       :, g * 16:(g + 1) * 16, r0:r0 + 128],
                   evb.ap(), reads=[b_evb], tag=f"stx2T{it}_{g}")
    em.fence()


def build_c2(em, dram, st):
    build_res_ln(em, dram, st, "c2")


def build_c5(em, dram, st):
    build_res_ln(em, dram, st, "c5")



def build_c34(em, dram, st):
    nc = em.nc
    import contextlib
    sb = lambda name, shape, dt: st.enter_context(nc.sbuf_tensor(name, shape, dt))

    b12 = sb("b12F", [128, NH], F32)
    b_b12 = Buf("b12")
    em.dma("sp_d", b12.ap(), dram["b12_col"], writes=[b_b12])
    b3 = sb("b3F", [128, 32], F32)
    b_b3 = Buf("b3")
    em.dma("sp_d", b3.ap(), dram["b3_col"], writes=[b_b3])

    x2h = sb("x2hF", [128, 32, 1024], BF16)
    b_x2h = Buf("x2h")
    x4h = sb("x4hF", [128, 32, 1024], BF16)
    b_x4 = [Buf(f"x4_{k}") for k in range(32)]
    wp = [sb(f"wpF{i}", [128, 32, 128], BF16) for i in range(4)]
    b_wp = [Buf(f"wpF{i}") for i in range(4)]
    s1t = [sb(f"s1tF{i}", [128, 1024], BF16) for i in range(2)]
    b_s1 = [Buf(f"s1t{i}") for i in range(2)]
    v2t = [sb(f"v2tF{i}", [128, 1024], BF16) for i in range(2)]
    b_v2 = [Buf(f"v2t{i}") for i in range(2)]
    fout = [sb(f"foutF{i}", [128, 1024], BF16) for i in range(2)]
    b_fout = [Buf(f"fout{i}") for i in range(2)]

    for hf in range(2):
        tsl = slice(hf * 1024, (hf + 1) * 1024)
        em.dma("sp_d", x2h.ap(),
               dram["x2T_hbm"].rearrange("(i p) t -> p i t", p=128)[:, :, tsl],
               writes=[b_x2h], tag=f"ldx2h{hf}")
        import contextlib as _ct
        with _ct.ExitStack() as st1:
            ps1 = [st1.enter_context(nc.psum_tensor(f"ps1F{hf}{i}", [128, 1024], F32))
                   for i in range(2)]
            b_ps1_ = [Buf(f"ps1{i}") for i in range(2)]
            ps2 = [st1.enter_context(nc.psum_tensor(f"ps2F{hf}{i}", [128, 1024], F32))
                   for i in range(2)]
            b_ps2_ = [Buf(f"ps2{i}") for i in range(2)]
            for mi in range(32):
                pa = mi % 2
                ps1_, ps2_ = ps1[pa], ps2[pa]
                b_ps1, b_ps2 = b_ps1_[pa], b_ps2_[pa]
                src1 = dram["w12T"].rearrange("(i p) c -> p i c", p=128)[
                    :, :, mi * 128:(mi + 1) * 128]
                src2 = dram["w12T"].rearrange("(i p) c -> p i c", p=128)[
                    :, :, (mi + 32) * 128:(mi + 33) * 128]
                for piece in range(4):
                    pp = slice(piece * 8, (piece + 1) * 8)
                    em.dma("pool_d", wp[pa].ap()[:, pp, :], src1[:, pp, :],
                           writes=[b_wp[pa]], tag=f"ldw12a{hf}_{mi}_{piece}")
                    em.dma("pool_d", wp[2 + pa].ap()[:, pp, :], src2[:, pp, :],
                           writes=[b_wp[2 + pa]], tag=f"ldw12b{hf}_{mi}_{piece}")
                for k in range(32):
                    for m in range(2):
                        em.pe(lambda e, pa=pa, k=k, m=m, ps1_=ps1_: e.matmul(
                            ps1_.ap()[:, m * 512:(m + 1) * 512],
                            wp[pa].ap()[:, k, :],
                            x2h.ap()[:, k, m * 512:(m + 1) * 512],
                            start=(k == 0), stop=(k == 31)),
                            reads=[b_wp[pa], b_x2h], writes=[b_ps1])
                        em.pe(lambda e, pa=pa, k=k, m=m, ps2_=ps2_: e.matmul(
                            ps2_.ap()[:, m * 512:(m + 1) * 512],
                            wp[2 + pa].ap()[:, k, :],
                            x2h.ap()[:, k, m * 512:(m + 1) * 512],
                            start=(k == 0), stop=(k == 31)),
                            reads=[b_wp[2 + pa], b_x2h], writes=[b_ps2])
                em.act(lambda e, mi=mi, ps1_=ps1_, pa=pa: e.activation(
                    s1t[pa].ap(), ps1_.ap(), AF.Silu, bias=b12.ap()[:, mi:mi + 1]),
                    reads=[b_ps1, b_b12], writes=[b_s1[pa]])
                em.act(lambda e, mi=mi, ps2_=ps2_, pa=pa: e.activation(
                    v2t[pa].ap(), ps2_.ap(), AF.Identity,
                    bias=b12.ap()[:, mi + 32:mi + 33]),
                    reads=[b_ps2, b_b12], writes=[b_v2[pa]])
                em.dve(lambda e, mi=mi, pa=pa: e.tensor_tensor(
                    x4h.ap()[:, mi, :], s1t[pa].ap(), v2t[pa].ap(), ALU.mult),
                    reads=[b_s1[pa], b_v2[pa]], writes=[b_x4[mi]])
            em.fence()
        with _ct.ExitStack() as st2:
            psF = [st2.enter_context(nc.psum_tensor(f"psF{hf}_{i}", [128, 1024], F32))
                   for i in range(2)]
            b_psF = [Buf(f"psF{i}") for i in range(2)]
            for ni in range(32):
                pa = ni % 2
                src = dram["w3T"].rearrange("(i p) c -> p i c", p=128)[
                    :, :, ni * 128:(ni + 1) * 128]
                for piece in range(4):
                    pp = slice(piece * 8, (piece + 1) * 8)
                    em.dma("pool_d", wp[pa].ap()[:, pp, :], src[:, pp, :],
                           writes=[b_wp[pa]], tag=f"ldw3{hf}_{ni}_{piece}")
                for k in range(32):
                    for m in range(2):
                        em.pe(lambda e, pa=pa, k=k, m=m: e.matmul(
                            psF[pa].ap()[:, m * 512:(m + 1) * 512],
                            wp[pa].ap()[:, k, :],
                            x4h.ap()[:, k, m * 512:(m + 1) * 512],
                            start=(k == 0), stop=(k == 31)),
                            reads=[b_wp[pa], b_x4[k]], writes=[b_psF[pa]])
                em.act(lambda e, pa=pa, ni=ni: e.activation(
                    fout[pa].ap(), psF[pa].ap(), AF.Identity,
                    bias=b3.ap()[:, ni:ni + 1]),
                    reads=[b_psF[pa], b_b3], writes=[b_fout[pa]])
                em.dma("sp_d", dram["ffn_hbm"][ni * 128:(ni + 1) * 128, tsl],
                       fout[pa].ap(), reads=[b_fout[pa]], tag=f"stf{hf}_{ni}")
            em.fence()


def build_scan_b1(em, dram, st):
    build_scan(em, dram, st, True)


def build_scan_b2(em, dram, st):
    build_scan(em, dram, st, False)


IO_B1 = dict(
    ins=[("xssm_hbm", (CONV_DIM, TPC), BF),
         ("dt_soft_hbm", (NH, TPC), np.float32),
         ("A_log_col", (NH, 1), np.float32),
         ("ident_f32", (128, 128), np.float32),
         ("eye16rep", (NCH, TPC), np.float32),
         ("maskW", (128, 128), np.float32)],
    outs=[("S_own", (128, DSSM), np.float32),
          ("Dt_col", (NH, 1), np.float32)])

IO_B2 = dict(
    ins=[("xssm_hbm", (CONV_DIM, TPC), BF),
         ("dt_soft_hbm", (NH, TPC), np.float32),
         ("A_log_col", (NH, 1), np.float32),
         ("ident_f32", (128, 128), np.float32),
         ("eye16rep", (NCH, TPC), np.float32),
         ("maskW", (128, 128), np.float32),
         ("D_rep", (128, NH), np.float32),
         ("S_in0", (128, DSSM), np.float32),
         ("S_in1", (128, DSSM), np.float32),
         ("S_in2", (128, DSSM), np.float32),
         ("Dt_row1", (1, NH), np.float32),
         ("Dt_row2", (1, NH), np.float32)],
    outs=[("y_hbm", (TPC, DSSM), BF)])


IO_C1 = dict(
    ins=[("y_hbm", (TPC, DSSM), BF),
         ("z_hbm", (DSSM, TPC), BF),
         ("nw_col", (128, NH), np.float32),
         ("W_out", (DSSM, H), np.float32),
         ("ones_col_bf16", (128, 1), BF)],
    outs=[("attn_hbm", (H, TPC), BF)])

IO_C2 = dict(
    ins=[("attn_hbm", (H, TPC), BF),
         ("xt", (HALO + TPC, H), np.float32),
         ("ident_bf16", (128, 128), BF)],
    outs=[("xmid_hbm", (TPC, H), np.float32),
          ("x2T_hbm", (H, TPC), BF)])

IO_C34 = dict(
    ins=[("x2T_hbm", (H, TPC), BF),
         ("w12T", (H, DSSM), np.float32),
         ("b12_col", (128, NH), np.float32),
         ("w3T", (H, H), np.float32),
         ("b3_col", (128, 32), np.float32)],
    outs=[("ffn_hbm", (H, TPC), BF)])

IO_C5 = dict(
    ins=[("ffn_hbm", (H, TPC), BF),
         ("xmid_hbm", (TPC, H), np.float32),
         ("ident_bf16", (128, 128), BF)],
    outs=[("out_hbm", (TPC, H), np.float32)])


def build_program(io, build_fns):
    nc = bass.Bass("TRN2", target_bir_lowering=False)
    dram = {}
    for n, s, d in io["ins"]:
        dram[n] = nc.dram_tensor(n, list(s), _DT_NP2MY[np.dtype(d)],
                                 kind="ExternalInput").ap()
    for n, s, d in io["outs"]:
        dram[n] = nc.dram_tensor(n, list(s), _DT_NP2MY[np.dtype(d)],
                                 kind="ExternalOutput").ap()
    em = Em(nc)
    import contextlib
    with contextlib.ExitStack() as st:
        em.setup_consts(st, [EPS, 1.0, -1.0, -0.5, 0.5])
        for f in build_fns:
            f(em, dram, st)
        em.finalize_and_emit()
    return nc


def run_phase(io, build_fns, in_maps):
    nc = build_program(io, build_fns)
    res = run_bass_kernel_spmd(nc, in_maps, core_ids=list(range(len(in_maps))))
    return res.results


class _NW:

    def __init__(self, nc, pfx):
        object.__setattr__(self, "_nc", nc)
        object.__setattr__(self, "_p", pfx)

    def sbuf_tensor(self, name, *a, **k):
        return self._nc.sbuf_tensor(self._p + name, *a, **k)

    def psum_tensor(self, name, *a, **k):
        return self._nc.psum_tensor(self._p + name, *a, **k)

    def __getattr__(self, n):
        return getattr(self._nc, n)


def build_cc(em, dram, st):
    b = Buf("cc")
    groups = [[0, 1, 2, 3], [4, 5, 6, 7]]
    em.op("pool_d", lambda e: e.collective_compute(
        "AllGather", ALU.bypass, replica_groups=groups,
        ins=[dram["cc_S_in"].opt()], outs=[dram["cc_S_out"].opt()]),
        writes=[b], tag="ccS")
    em.op("pool_d", lambda e: e.collective_compute(
        "AllGather", ALU.bypass, replica_groups=groups,
        ins=[dram["cc_D_in"].opt()], outs=[dram["cc_D_out"].opt()]),
        writes=[b], tag="ccD")


FUSED_INS = [
    ("xt", (HALO + TPC, H), np.float32),
    ("W_in", (H, PROJ_DIM), np.float32),
    ("conv_w", (CONV_DIM, KCONV), np.float32),
    ("dt_bias_col", (NH, 1), np.float32),
    ("A_log_col", (NH, 1), np.float32),
    ("ident_bf16", (128, 128), BF),
    ("ident_f32", (128, 128), np.float32),
    ("eye16rep", (NCH, TPC), np.float32),
    ("maskW", (128, 128), np.float32),
    ("D_rep", (128, NH), np.float32),
    ("nw_col", (128, NH), np.float32),
    ("W_out", (DSSM, H), np.float32),
    ("ones_col_bf16", (128, 1), BF),
    ("w12T", (H, DSSM), np.float32),
    ("b12_col", (128, NH), np.float32),
    ("w3T", (H, H), np.float32),
    ("b3_col", (128, 32), np.float32),
    ("relay_cols", (128, 8), np.float32),
]
FUSED_INTERNAL = [
    ("z_hbm", (DSSM, TPC), BF),
    ("xssm_hbm", (CONV_DIM, TPC), BF),
    ("dt_soft_hbm", (NH, TPC), np.float32),
    ("y_hbm", (TPC, DSSM), BF),
    ("attn_hbm", (H, TPC), BF),
    ("xmid_hbm", (TPC, H), np.float32),
    ("x2T_hbm", (H, TPC), BF),
    ("ffn_hbm", (H, TPC), BF),
    ("cc_S_in", (128, DSSM), np.float32),
    ("cc_S_out", (512, DSSM), np.float32),
    ("cc_D_in", (1, NH), np.float32),
    ("cc_D_out", (4, NH), np.float32),
]

FUSED_PHASES = [build_phase_a, build_scan_b1, build_cc, build_scan_b2,
                build_c1, build_c2, build_c34, build_c5]


def build_multi(phases, ins, outs, internals):
    nc = bass.Bass("TRN2", target_bir_lowering=False)
    dram = {}
    for n, s, d in ins:
        dram[n] = nc.dram_tensor(n, list(s), _DT_NP2MY[np.dtype(d)],
                                 kind="ExternalInput").ap()
    for n, s, d in outs:
        dram[n] = nc.dram_tensor(n, list(s), _DT_NP2MY[np.dtype(d)],
                                 kind="ExternalOutput").ap()
    for n, s, d in internals:
        dram[n] = nc.dram_tensor(n, list(s), _DT_NP2MY[np.dtype(d)],
                                 kind="Internal").ap()
    em = Em(nc)
    import contextlib
    with contextlib.ExitStack() as stc:
        em.setup_consts(stc, [EPS, 1.0, -1.0, -0.5, 0.5])
        for i, f in enumerate(phases):
            em.nc = _NW(nc, f"p{i}_")
            with contextlib.ExitStack() as stf:
                f(em, dram, stf)
                em.fence()
        em.nc = nc
        em.finalize_and_emit()
    return nc



L1_INS = [
    ("xt", (HALO + TPC, H), np.float32),
    ("W_in", (H, PROJ_DIM), np.float32),
    ("conv_w", (CONV_DIM, KCONV), np.float32),
    ("dt_bias_col", (NH, 1), np.float32),
    ("A_log_col", (NH, 1), np.float32),
    ("ident_bf16", (128, 128), BF),
    ("ident_f32", (128, 128), np.float32),
    ("eye16rep", (NCH, TPC), np.float32),
    ("maskW", (128, 128), np.float32),
]
L1_OUTS = [
    ("z_hbm", (DSSM, TPC), BF),
    ("xssm_hbm", (CONV_DIM, TPC), BF),
    ("dt_soft_hbm", (NH, TPC), np.float32),
    ("S_own", (128, DSSM), np.float32),
    ("Dt_col", (NH, 1), np.float32),
]
L2_INS = [
    ("xt", (HALO + TPC, H), np.float32),
    ("z_hbm", (DSSM, TPC), BF),
    ("xssm_hbm", (CONV_DIM, TPC), BF),
    ("dt_soft_hbm", (NH, TPC), np.float32),
    ("A_log_col", (NH, 1), np.float32),
    ("ident_bf16", (128, 128), BF),
    ("ident_f32", (128, 128), np.float32),
    ("eye16rep", (NCH, TPC), np.float32),
    ("maskW", (128, 128), np.float32),
    ("D_rep", (128, NH), np.float32),
    ("nw_col", (128, NH), np.float32),
    ("W_out", (DSSM, H), np.float32),
    ("ones_col_bf16", (128, 1), BF),
    ("w12T", (H, DSSM), np.float32),
    ("b12_col", (128, NH), np.float32),
    ("w3T", (H, H), np.float32),
    ("b3_col", (128, 32), np.float32),
    ("S_in0", (128, DSSM), np.float32),
    ("S_in1", (128, DSSM), np.float32),
    ("S_in2", (128, DSSM), np.float32),
    ("Dt_row1", (1, NH), np.float32),
    ("Dt_row2", (1, NH), np.float32),
]
L2_OUTS = [("out_hbm", (TPC, H), np.float32)]
L2_INTERNAL = [
    ("y_hbm", (TPC, DSSM), BF),
    ("attn_hbm", (H, TPC), BF),
    ("xmid_hbm", (TPC, H), np.float32),
    ("x2T_hbm", (H, TPC), BF),
    ("ffn_hbm", (H, TPC), BF),
]


def build_l1():
    return build_multi([build_phase_a, build_scan_b1], L1_INS, L1_OUTS, [])


def build_l2():
    return build_multi([build_scan_b2, build_c1, build_c2, build_c34, build_c5],
                       L2_INS, L2_OUTS, L2_INTERNAL)


def _maps(io, shared, percore, extra=None):
    out = []
    for core in range(8):
        m = {}
        for k, _, _ in io["ins"]:
            if k in shared:
                m[k] = shared[k]
            elif k in percore[core]:
                m[k] = percore[core][k]
            else:
                m[k] = extra[core][k]
        out.append(m)
    return out


def kernel(**inputs):
    shared, percore = host_prep(inputs)
    nc1 = build_l1()
    maps1 = []
    for core in range(8):
        m = {k: (shared[k] if k in shared else percore[core][k])
             for k, _, _ in L1_INS}
        maps1.append(m)
    r1 = run_bass_kernel_spmd(nc1, maps1, core_ids=list(range(8))).results
    nc2 = build_l2()
    zS = np.zeros((128, DSSM), np.float32)
    oD = np.ones((1, NH), np.float32)
    maps2 = []
    for core in range(8):
        b, q = core // 4, core % 4
        m = {}
        for k, _, _ in L2_INS:
            if k in shared:
                m[k] = shared[k]
            elif k in percore[core]:
                m[k] = percore[core][k]
            elif k in r1[core]:
                m[k] = r1[core][k]
        for slot, j in enumerate((q - 3, q - 2, q - 1)):
            m[f"S_in{slot}"] = r1[4 * b + j]["S_own"] if j >= 0 else zS
        for slot, j in zip((1, 2), (q - 2, q - 1)):
            m[f"Dt_row{slot}"] = (np.ascontiguousarray(
                r1[4 * b + j]["Dt_col"].reshape(1, NH)) if j >= 0 else oD)
        maps2.append(m)
    r2 = run_bass_kernel_spmd(nc2, maps2, core_ids=list(range(8))).results
    out = np.empty((BATCH, SEQ, H), np.float32)
    for core in range(8):
        b, q = core // 4, core % 4
        out[b, q * TPC:(q + 1) * TPC] = r2[core]["out_hbm"]
    return out


def kernel_multilaunch(**inputs):
    shared, percore = host_prep(inputs)
    r_a = run_phase(IO_A, [build_phase_a], _maps(IO_A, shared, percore))
    r_b1 = run_phase(IO_B1, [build_scan_b1],
                     _maps(IO_B1, shared, percore, r_a))
    extra = []
    zS = np.zeros((128, DSSM), np.float32)
    oD = np.ones((1, NH), np.float32)
    for core in range(8):
        b, q = core // 4, core % 4
        m = dict(r_a[core])
        for slot, j in enumerate((q - 3, q - 2, q - 1)):
            m[f"S_in{slot}"] = (r_b1[4 * b + j]["S_own"] if j >= 0 else zS)
        for slot, j in zip((1, 2), (q - 2, q - 1)):
            m[f"Dt_row{slot}"] = (np.ascontiguousarray(
                r_b1[4 * b + j]["Dt_col"].reshape(1, NH)) if j >= 0 else oD)
        extra.append(m)
    r_b2 = run_phase(IO_B2, [build_scan_b2], _maps(IO_B2, shared, percore, extra))
    for core in range(8):
        extra[core]["y_hbm"] = r_b2[core]["y_hbm"]
    r_c1 = run_phase(IO_C1, [build_c1], _maps(IO_C1, shared, percore, extra))
    for core in range(8):
        extra[core]["attn_hbm"] = r_c1[core]["attn_hbm"]
    r_c2 = run_phase(IO_C2, [build_c2], _maps(IO_C2, shared, percore, extra))
    for core in range(8):
        extra[core]["x2T_hbm"] = r_c2[core]["x2T_hbm"]
        extra[core]["xmid_hbm"] = r_c2[core]["xmid_hbm"]
    r_c34 = run_phase(IO_C34, [build_c34], _maps(IO_C34, shared, percore, extra))
    for core in range(8):
        extra[core]["ffn_hbm"] = r_c34[core]["ffn_hbm"]
    r_c5 = run_phase(IO_C5, [build_c5], _maps(IO_C5, shared, percore, extra))
    out = np.empty((BATCH, SEQ, H), np.float32)
    for core in range(8):
        b, q = core // 4, core % 4
        out[b, q * TPC:(q + 1) * TPC] = r_c5[core]["out_hbm"]
    return out


# version 11 (final)
# speedup vs baseline: 1.1170x; 1.0935x over previous
import numpy as np
import ml_dtypes
import concourse.bass as bass
import concourse.mybir as mybir
from concourse.bass_utils import run_bass_kernel_spmd

F32 = mybir.dt.float32
BF16 = mybir.dt.bfloat16
AF = mybir.ActivationFunctionType
ALU = mybir.AluOpType
AX = mybir.AxisListType

H = 4096
DSSM = 8192
NH = 64
P = 128
NST = 128
KCONV = 4
EPS = 1e-5
CONV_DIM = DSSM + 2 * NST
PROJ_DIM = DSSM + CONV_DIM + NH
SEQ = 8192
BATCH = 2
TPC = 2048
Q = 128
NCH = TPC // Q
HALO = KCONV - 1
NEG = -1e30

ZT = 64
XBCT = 66



class Buf:
    __slots__ = ("last_w", "readers", "name")

    def __init__(self, name=""):
        self.last_w = None
        self.readers = []
        self.name = name


class _Op:
    __slots__ = ("fn", "stream", "ordinal", "deps", "inc", "tag")

    def __init__(self, fn, stream, ordinal, deps, tag):
        self.fn = fn
        self.stream = stream
        self.ordinal = ordinal
        self.deps = deps
        self.inc = False
        self.tag = tag


_STREAMS = {
    "pe": ("tensor", False),
    "act": ("scalar", False),
    "dve": ("vector", False),
    "pool": ("gpsimd", False),
    "sp_d": ("sync", True),
    "act_d": ("scalar", True),
    "pool_d": ("gpsimd", True),
}
_BODIES = ("tensor", "scalar", "vector", "gpsimd", "sync")


class Em:
    def __init__(self, nc):
        self.nc = nc
        self.ops = {s: [] for s in _STREAMS}
        self.body_ops = {b: [] for b in _BODIES}
        self.n = {s: 0 for s in _STREAMS}
        self.floor = {}

    def fence(self):
        self.floor = {s: self.n[s] - 1 for s in _STREAMS if self.n[s] > 0}

    def setup_consts(self, st, values):
        pool = st.enter_context(self.nc.sbuf_tensor("constpool", [128, len(values)], F32))
        b = Buf("consts")
        for i, v in enumerate(values):
            ap = pool.ap()[:, i:i + 1]
            self.dve(lambda e, ap=ap, v=v: e.memset(ap, v), writes=[b])
            self.nc.const_aps.register(F32, float(v), ap)
        self.fence()

    def op(self, stream, fn, reads=(), writes=(), tag=""):
        deps = dict(self.floor)
        for b in reads:
            if b.last_w is not None:
                s, o = b.last_w
                if deps.get(s, -1) < o:
                    deps[s] = o
        for b in writes:
            if b.last_w is not None:
                s, o = b.last_w
                if deps.get(s, -1) < o:
                    deps[s] = o
            for s, o in b.readers:
                if deps.get(s, -1) < o:
                    deps[s] = o
        ordn = self.n[stream]
        self.n[stream] += 1
        dl = []
        for s, o in deps.items():
            if s == stream and o >= ordn:
                continue
            if s == stream and o == ordn - 1 and False:
                continue
            if s == "pe" and stream == "pe":
                continue
            dl.append((s, o))
        o_ = _Op(fn, stream, ordn, dl, tag)
        self.ops[stream].append(o_)
        self.body_ops[_STREAMS[stream][0]].append(o_)
        for b in reads:
            b.readers.append((stream, ordn))
        for b in writes:
            b.last_w = (stream, ordn)
            b.readers = []
        return o_

    def pe(self, fn, reads=(), writes=(), tag=""):
        return self.op("pe", fn, reads, writes, tag)

    def act(self, fn, reads=(), writes=(), tag=""):
        return self.op("act", fn, reads, writes, tag)

    def dve(self, fn, reads=(), writes=(), tag=""):
        return self.op("dve", fn, reads, writes, tag)

    def pool(self, fn, reads=(), writes=(), tag=""):
        return self.op("pool", fn, reads, writes, tag)

    def dma(self, queue, out_ap, in_ap, reads=(), writes=(), tag="", **kw):
        return self.op(queue, lambda e: e.dma_start(out_ap, in_ap, **kw),
                       reads, writes, tag)

    def finalize_and_emit(self, final_body="sync"):
        for s, lst in self.ops.items():
            if not lst:
                continue
            if _STREAMS[s][1]:
                for o in lst:
                    o.inc = True
            lst[-1].inc = True
        for s, lst in self.ops.items():
            for o in lst:
                for ds, do in o.deps:
                    self.ops[ds][do].inc = True
        self.val_after = {}
        self.final_val = {}
        for s, lst in self.ops.items():
            step = 16 if _STREAMS[s][1] else 1
            c = 0
            va = []
            for o in lst:
                if o.inc:
                    c += step
                va.append(c)
            self.val_after[s] = va
            self.final_val[s] = c
        nc = self.nc
        import contextlib
        stack = contextlib.ExitStack()
        sems = {s: stack.enter_context(nc.semaphore(f"sem_{s}")) for s in _STREAMS}
        block = stack.enter_context(nc.Block())

        def mk_body(body):
            ops = self.body_ops[body]

            def run(eng):
                watermark = {}
                for o in ops:
                    for ds, do in o.deps:
                        v = self.val_after[ds][do]
                        if v <= 0 or watermark.get(ds, 0) >= v:
                            continue
                        eng.wait_ge(sems[ds], v)
                        watermark[ds] = v
                    inst = o.fn(eng)
                    if o.inc:
                        inst.then_inc(sems[o.stream],
                                      16 if _STREAMS[o.stream][1] else 1)
                if body == final_body:
                    for s in _STREAMS:
                        if s != "sp_d" and _STREAMS[s][0] == body:
                            continue
                        v = self.final_val[s]
                        if v > 0 and watermark.get(s, 0) < v:
                            eng.wait_ge(sems[s], v)
            return run

        for body in _BODIES:
            if not self.body_ops[body] and body != final_body:
                continue
            getattr(block, body)(mk_body(body))
        stack.close()



def host_constants():
    c = {}
    c["ident_bf16"] = np.eye(128, dtype=ml_dtypes.bfloat16)
    c["ident_f32"] = np.eye(128, dtype=np.float32)
    c["ones_f32"] = np.ones((128, 128), dtype=np.float32)
    c["ones_col_bf16"] = np.ones((128, 1), dtype=ml_dtypes.bfloat16)
    t = np.arange(128)
    c["maskW"] = np.where(t[None, :] <= t[:, None], 0.0, NEG).astype(np.float32)
    e = np.zeros((NCH, TPC), dtype=np.float32)
    for k in range(NCH):
        e[k, k * Q:(k + 1) * Q] = 1.0
    c["eye16rep"] = e
    return c


def host_prep(inputs):
    x = np.asarray(inputs["L0_attn_hidden_states"], np.float32)
    shared = dict(host_constants())
    shared["W_in"] = np.ascontiguousarray(inputs["L0_attn_W_in"], dtype=np.float32)
    shared["conv_w"] = np.ascontiguousarray(inputs["L0_attn_conv_w"], dtype=np.float32)
    shared["dt_bias_col"] = np.ascontiguousarray(
        np.asarray(inputs["L0_attn_dt_bias"], np.float32).reshape(NH, 1))
    shared["A_log_col"] = np.ascontiguousarray(
        np.asarray(inputs["L0_attn_A_log"], np.float32).reshape(NH, 1))
    shared["D_rep"] = np.ascontiguousarray(
        np.broadcast_to(np.asarray(inputs["L0_attn_D_param"], np.float32), (128, NH)))
    shared["nw_col"] = np.ascontiguousarray(
        np.asarray(inputs["L0_attn_norm_weight"], np.float32).reshape(DSSM // 128, 128).T)
    shared["W_out"] = np.ascontiguousarray(inputs["L0_attn_W_out"], dtype=np.float32)
    shared["w12T"] = np.ascontiguousarray(np.asarray(inputs["L0_ffn_w12"], np.float32).T)
    shared["b12_col"] = np.ascontiguousarray(
        np.asarray(inputs["L0_ffn_b12"], np.float32).reshape(DSSM // 128, 128).T)
    shared["w3T"] = np.ascontiguousarray(np.asarray(inputs["L0_ffn_w3"], np.float32).T)
    shared["b3_col"] = np.ascontiguousarray(
        np.asarray(inputs["L0_ffn_b3"], np.float32).reshape(H // 128, 128).T)
    percore = []
    for core in range(8):
        b, q = core // 4, core % 4
        d = {}
        xt = np.zeros((HALO + TPC, H), np.float32)
        s0 = q * TPC
        xt[HALO:] = x[b, s0:s0 + TPC]
        if q > 0:
            xt[:HALO] = x[b, s0 - HALO:s0]
        d["xt"] = np.ascontiguousarray(xt)
        rc = np.zeros((128, 8), np.float32)
        for r in range(4):
            if r < q:
                rc[:, r] = 1.0
                rc[:, 4 + r] = 1.0
        d["relay_cols"] = rc
        percore.append(d)
    return shared, percore



def build_phase_a(em, dram, st):
    nc = em.nc
    sb = lambda name, shape, dt: st.enter_context(nc.sbuf_tensor(name, shape, dt))

    ident = sb("identA", [128, 128], BF16)
    b_ident = Buf("ident")
    em.dma("sp_d", ident.ap(), dram["ident_bf16"], writes=[b_ident])
    identf = sb("identfA", [128, 128], F32)
    b_identf = Buf("identf")
    em.dma("sp_d", identf.ap(), dram["ident_f32"], writes=[b_identf])

    x1T = sb("x1T", [128, 32, HALO + TPC], BF16)
    b_x1T = [Buf(f"x1T{i}") for i in range(17)]

    ntile = 17
    import contextlib
    with contextlib.ExitStack() as st1:
        sb1 = lambda name, shape, dt: st1.enter_context(nc.sbuf_tensor(name, shape, dt))
        ps1 = lambda name, shape, dt: st1.enter_context(nc.psum_tensor(name, shape, dt))
        xa = [sb1(f"xa{i}", [128, H], F32) for i in range(2)]
        b_xa = [Buf(f"xa{i}") for i in range(2)]
        x1t = [sb1(f"x1t{i}", [128, H], BF16) for i in range(2)]
        b_x1t = [Buf(f"x1t{i}") for i in range(2)]
        sq = sb1("sq", [128, H], BF16)
        b_sq = Buf("sq")
        stats = sb1("statsA", [128, 8], F32)
        b_st = Buf("stats")
        tps = ps1("tpsA", [128, 2048], BF16)
        b_tps = Buf("tps")
        for it in range(ntile):
            r0 = it * 128
            nr = min(128, HALO + TPC - r0)
            pa = it % 2
            xa_t, b_xa_t = xa[pa], b_xa[pa]
            x1_t, b_x1_t = x1t[pa], b_x1t[pa]
            em.dma("sp_d", xa_t.ap()[0:nr, :], dram["xt"][r0:r0 + nr, :],
                   writes=[b_xa_t], tag=f"ldxa{it}")
            em.act(lambda e, xa_t=xa_t, nr=nr: e.activation(
                sq.ap()[0:nr, :], xa_t.ap()[0:nr, :], AF.Identity,
                accum_out=stats.ap()[0:nr, 0:1]),
                reads=[b_xa_t], writes=[b_sq, b_st])
            em.act(lambda e, xa_t=xa_t, nr=nr: e.activation(
                sq.ap()[0:nr, :], xa_t.ap()[0:nr, :], AF.Square,
                accum_out=stats.ap()[0:nr, 1:2]),
                reads=[b_xa_t], writes=[b_sq, b_st])
            em.dve(lambda e, nr=nr: e.tensor_scalar(
                stats.ap()[0:nr, 2:3], stats.ap()[0:nr, 0:1], 1.0 / H, None, ALU.mult),
                reads=[b_st], writes=[b_st])
            em.dve(lambda e, nr=nr: e.tensor_scalar(
                stats.ap()[0:nr, 3:4], stats.ap()[0:nr, 1:2], 1.0 / H, None, ALU.mult),
                reads=[b_st], writes=[b_st])
            em.dve(lambda e, nr=nr: e.tensor_mul(
                stats.ap()[0:nr, 5:6], stats.ap()[0:nr, 2:3], stats.ap()[0:nr, 2:3]),
                reads=[b_st], writes=[b_st])
            em.dve(lambda e, nr=nr: e.tensor_sub(
                stats.ap()[0:nr, 3:4], stats.ap()[0:nr, 3:4], stats.ap()[0:nr, 5:6]),
                reads=[b_st], writes=[b_st])
            em.act(lambda e, nr=nr: e.activation(
                stats.ap()[0:nr, 5:6], stats.ap()[0:nr, 3:4], AF.Ln, bias=EPS),
                reads=[b_st], writes=[b_st])
            em.act(lambda e, nr=nr: e.activation(
                stats.ap()[0:nr, 4:5], stats.ap()[0:nr, 5:6], AF.Exp, scale=-0.5),
                reads=[b_st], writes=[b_st])
            em.dve(lambda e, xa_t=xa_t, x1_t=x1_t, nr=nr: e.tensor_scalar(
                x1_t.ap()[0:nr, :], xa_t.ap()[0:nr, :],
                stats.ap()[0:nr, 2:3], stats.ap()[0:nr, 4:5],
                ALU.subtract, ALU.mult),
                reads=[b_xa_t, b_st], writes=[b_x1_t])
            for g in range(2):
                for j in range(16):
                    dblk = g * 16 + j
                    em.pe(lambda e, x1_t=x1_t, dblk=dblk, j=j, nr=nr: e.transpose(
                        tps.ap()[:, j * 128:j * 128 + nr],
                        x1_t.ap()[0:nr, dblk * 128:(dblk + 1) * 128],
                        ident.ap()[0:nr, 0:nr]),
                        reads=[b_x1_t, b_ident], writes=[b_tps])
                em.dve(lambda e, g=g, r0=r0, nr=nr: e.tensor_copy(
                    x1T.ap()[:, g * 16:(g + 1) * 16, r0:r0 + nr],
                    tps.ap().rearrange("p (j t) -> p j t", j=16)[:, :, 0:nr]),
                    reads=[b_tps], writes=[b_x1T[it]], tag=f"x1Tev{it}_{g}")
        em.fence()

    dt_scope = contextlib.ExitStack()
    with dt_scope as st2:
        sb2 = lambda name, shape, dt: st2.enter_context(nc.sbuf_tensor(name, shape, dt))
        ps2 = lambda name, shape, dt: st2.enter_context(nc.psum_tensor(name, shape, dt))
        wdt = sb2("wdt", [128, 32, NH], BF16)
        b_wdt = Buf("wdt")
        src = dram["W_in"].rearrange("(i p) c -> p i c", p=128)[:, :, PROJ_DIM - NH:]
        for piece in range(4):
            em.dma("pool_d", wdt.ap()[:, piece * 8:(piece + 1) * 8, :],
                   src[:, piece * 8:(piece + 1) * 8, :], writes=[b_wdt])
        dtps = ps2("dtps", [128, NH], F32)
        b_dtps = Buf("dtps")
        dtsb = sb2("dtsb", [128, NH], F32)
        b_dtsb = Buf("dtsb")
        dtp2 = ps2("dtp2", [NH, 128], F32)
        b_dtp2 = Buf("dtp2")
        dtraw = sb2("dtraw", [NH, TPC], F32)
        b_dtraw = Buf("dtraw")
        for ch in range(NCH):
            t0 = HALO + ch * Q
            for d in range(32):
                em.pe(lambda e, d=d, t0=t0: e.matmul(
                    dtps.ap(), x1T.ap()[:, d, t0:t0 + Q], wdt.ap()[:, d, :],
                    start=(d == 0), stop=(d == 31)),
                    reads=[b_x1T[t0 // 128], b_x1T[t0 // 128 + 1], b_wdt],
                    writes=[b_dtps])
            em.dve(lambda e: e.tensor_copy(dtsb.ap(), dtps.ap()),
                   reads=[b_dtps], writes=[b_dtsb])
            em.pe(lambda e: e.transpose(dtp2.ap(), dtsb.ap(), identf.ap()),
                  reads=[b_dtsb, b_identf], writes=[b_dtp2])
            em.dve(lambda e, ch=ch: e.tensor_copy(
                dtraw.ap()[:, ch * Q:(ch + 1) * Q], dtp2.ap()),
                reads=[b_dtp2], writes=[b_dtraw])
        dtb = sb2("dtb", [NH, 1], F32)
        b_dtb = Buf("dtb")
        em.dma("sp_d", dtb.ap(), dram["dt_bias_col"], writes=[b_dtb])
        t_abs = sb2("t_abs", [NH, TPC], F32)
        b_abs = Buf("t_abs")
        dt_soft = sb2("dt_softA", [NH, TPC], F32)
        b_dts = Buf("dt_soft")
        em.act(lambda e: e.activation(t_abs.ap(), dtraw.ap(), AF.Abs, bias=dtb.ap()),
               reads=[b_dtraw, b_dtb], writes=[b_abs])
        em.act(lambda e: e.activation(t_abs.ap(), t_abs.ap(), AF.Exp, scale=-1.0),
               reads=[b_abs], writes=[b_abs])
        em.act(lambda e: e.activation(t_abs.ap(), t_abs.ap(), AF.Ln, bias=1.0),
               reads=[b_abs], writes=[b_abs])
        em.act(lambda e: e.activation(dt_soft.ap(), dtraw.ap(), AF.Relu, bias=dtb.ap()),
               reads=[b_dtraw, b_dtb], writes=[b_dts])
        em.dve(lambda e: e.tensor_add(dt_soft.ap(), dt_soft.ap(), t_abs.ap()),
               reads=[b_abs, b_dts], writes=[b_dts])
        em.dma("sp_d", dram["dt_soft_hbm"], dt_soft.ap(), reads=[b_dts],
               tag="st_dtsoft")
        em.fence()

    cw = sb("cwA", [128, XBCT, KCONV], F32)
    b_cw = Buf("cw")
    em.dma("sp_d", cw.ap(), dram["conv_w"].rearrange("(i p) k -> p i k", p=128),
           writes=[b_cw])

    wpan = [sb(f"wpanA{i}", [128, 32, 128], BF16) for i in range(2)]
    b_wpan = [Buf(f"wpan{i}") for i in range(2)]

    def load_wpanel(c, pa):
        src = dram["W_in"].rearrange("(i p) c -> p i c", p=128)[:, :, c * 128:(c + 1) * 128]
        for piece in range(4):
            em.dma("pool_d", wpan[pa].ap()[:, piece * 8:(piece + 1) * 8, :],
                   src[:, piece * 8:(piece + 1) * 8, :],
                   writes=[b_wpan[pa]], tag=f"ldw{c}_{piece}")

    with contextlib.ExitStack() as st4:
        sb4 = lambda name, shape, dt: st4.enter_context(nc.sbuf_tensor(name, shape, dt))
        ps4 = lambda name, shape, dt: st4.enter_context(nc.psum_tensor(name, shape, dt))
        xps = ps4("xps", [128, 2052], F32)
        b_xps = Buf("xps")
        xbc = [sb4(f"xbc{i}", [128, HALO + 2048], BF16) for i in range(2)]
        b_xbc_lo = [Buf(f"xbcl{i}") for i in range(2)]
        b_xbc_hi = [Buf(f"xbch{i}") for i in range(2)]
        cv = [sb4(f"cv{i}", [128, 2048], BF16) for i in range(4)]
        b_cv = [Buf(f"cv{i}") for i in range(4)]
        xout = [sb4(f"xout{i}", [128, 2048], BF16) for i in range(2)]
        b_xout = [Buf(f"xout{i}") for i in range(2)]
        offs = [0, 512, 1024, 1536, 2048, 2051]
        for ci in range(XBCT):
            c = ZT + ci
            pa = c % 2
            load_wpanel(c, pa)
            for d in range(32):
                for m in range(5):
                    em.pe(lambda e, pa=pa, d=d, m=m: e.matmul(
                        xps.ap()[:, offs[m]:offs[m + 1]],
                        wpan[pa].ap()[:, d, :],
                        x1T.ap()[:, d, offs[m]:offs[m + 1]],
                        start=(d == 0), stop=(d == 31)),
                        reads=[b_wpan[pa]] + b_x1T, writes=[b_xps])
            em.dve(lambda e, pa=pa: e.tensor_copy(
                xbc[pa].ap()[:, 0:1024], xps.ap()[:, 0:1024]),
                reads=[b_xps], writes=[b_xbc_lo[pa]])
            em.act(lambda e, pa=pa: e.activation(
                xbc[pa].ap()[:, 1024:2051], xps.ap()[:, 1024:2051], AF.Copy),
                reads=[b_xps], writes=[b_xbc_hi[pa]])
            if "xbcdbg_hbm" in dram:
                em.dma("act_d", dram["xbcdbg_hbm"][ci * 128:(ci + 1) * 128, :],
                       xbc[pa].ap(), reads=[b_xbc_lo[pa], b_xbc_hi[pa]])
            rbc = [b_xbc_lo[pa], b_xbc_hi[pa], b_cw]
            em.act(lambda e, pa=pa, ci=ci: e.activation(
                cv[0].ap(), xbc[pa].ap()[:, 0:2048], AF.Identity,
                scale=cw.ap()[:, ci, 0:1]),
                reads=rbc, writes=[b_cv[0]])
            em.act(lambda e, pa=pa, ci=ci: e.activation(
                cv[1].ap(), xbc[pa].ap()[:, 1:2049], AF.Identity,
                scale=cw.ap()[:, ci, 1:2]),
                reads=rbc, writes=[b_cv[1]])
            em.dve(lambda e, pa=pa, ci=ci: e.tensor_scalar(
                cv[2].ap(), xbc[pa].ap()[:, 2:2050], cw.ap()[:, ci, 2:3],
                None, ALU.mult),
                reads=rbc, writes=[b_cv[2]])
            em.dve(lambda e, pa=pa, ci=ci: e.tensor_scalar(
                cv[3].ap(), xbc[pa].ap()[:, 3:2051], cw.ap()[:, ci, 3:4],
                None, ALU.mult),
                reads=rbc, writes=[b_cv[3]])
            em.dve(lambda e: e.tensor_add(cv[0].ap(), cv[0].ap(), cv[1].ap()),
                   reads=[b_cv[0], b_cv[1]], writes=[b_cv[0]])
            em.dve(lambda e: e.tensor_add(cv[2].ap(), cv[2].ap(), cv[3].ap()),
                   reads=[b_cv[2], b_cv[3]], writes=[b_cv[2]])
            em.dve(lambda e: e.tensor_add(cv[0].ap(), cv[0].ap(), cv[2].ap()),
                   reads=[b_cv[0], b_cv[2]], writes=[b_cv[0]])
            em.act(lambda e, pa=pa: e.activation(xout[pa].ap(), cv[0].ap(), AF.Silu),
                   reads=[b_cv[0]], writes=[b_xout[pa]])
            em.dma("sp_d", dram["xssm_hbm"][ci * 128:(ci + 1) * 128, :],
                   xout[pa].ap(), reads=[b_xout[pa]], tag=f"stxs{ci}")
        em.fence()


    with contextlib.ExitStack() as st3:
        sb3 = lambda name, shape, dt: st3.enter_context(nc.sbuf_tensor(name, shape, dt))
        ps3 = lambda name, shape, dt: st3.enter_context(nc.psum_tensor(name, shape, dt))
        zps = [ps3(f"zps{i}", [128, 2048], F32) for i in range(2)]
        b_zps = [Buf(f"zps{i}") for i in range(2)]
        zsb = [sb3(f"zsb{i}", [128, 2048], BF16) for i in range(2)]
        b_zsb_lo = [Buf(f"zsbl{i}") for i in range(2)]
        b_zsb_hi = [Buf(f"zsbh{i}") for i in range(2)]
        for c in range(ZT):
            pa = c % 2
            load_wpanel(c, pa)
            for d in range(32):
                for m in range(4):
                    em.pe(lambda e, pa=pa, d=d, m=m: e.matmul(
                        zps[pa].ap()[:, m * 512:(m + 1) * 512],
                        wpan[pa].ap()[:, d, :],
                        x1T.ap()[:, d, HALO + m * 512:HALO + (m + 1) * 512],
                        start=(d == 0), stop=(d == 31)),
                        reads=[b_wpan[pa]] + b_x1T, writes=[b_zps[pa]])
            em.dve(lambda e, pa=pa: e.tensor_copy(
                zsb[pa].ap()[:, 0:1024], zps[pa].ap()[:, 0:1024]),
                reads=[b_zps[pa]], writes=[b_zsb_lo[pa]])
            em.act(lambda e, pa=pa: e.activation(
                zsb[pa].ap()[:, 1024:2048], zps[pa].ap()[:, 1024:2048], AF.Copy),
                reads=[b_zps[pa]], writes=[b_zsb_hi[pa]])
            em.dma("sp_d", dram["z_hbm"][c * 128:(c + 1) * 128, :], zsb[pa].ap(),
                   reads=[b_zsb_lo[pa], b_zsb_hi[pa]], tag=f"stz{c}")
        em.fence()


_DT_NP2MY = {np.dtype(np.float32): F32, np.dtype(ml_dtypes.bfloat16): BF16}
BF = ml_dtypes.bfloat16

IO_A = dict(
    ins=[("xt", (HALO + TPC, H), np.float32),
         ("W_in", (H, PROJ_DIM), np.float32),
         ("conv_w", (CONV_DIM, KCONV), np.float32),
         ("dt_bias_col", (NH, 1), np.float32),
         ("ident_bf16", (128, 128), BF),
         ("ident_f32", (128, 128), np.float32)],
    outs=[("z_hbm", (DSSM, TPC), BF),
          ("xssm_hbm", (CONV_DIM, TPC), BF),
          ("dt_soft_hbm", (NH, TPC), np.float32)])


def build_scan(em, dram, st, summary_only):
    nc = em.nc
    import contextlib
    sb = lambda name, shape, dt: st.enter_context(nc.sbuf_tensor(name, shape, dt))
    ps = lambda name, shape, dt: st.enter_context(nc.psum_tensor(name, shape, dt))
    tag = "B1" if summary_only else "B2"

    identf = sb("identfB", [128, 128], F32)
    b_identf = Buf("identf")
    em.dma("sp_d", identf.ap(), dram["ident_f32"], writes=[b_identf])
    maskW = sb("maskWB", [128, 128], F32)
    b_mask = Buf("maskW")
    em.dma("sp_d", maskW.ap(), dram["maskW"], writes=[b_mask])

    negall = sb("negallB", [128, NCH, NH], F32)
    b_negall = Buf("negall")
    biasall = sb("biasallB", [128, NCH, NH], F32)
    b_biasall = Buf("biasall")
    dwall = sb("dwallB", [128, NCH, NH], BF16)
    b_dwall = Buf("dwall")
    decCall = sb("decCallB", [128, NCH, NH], F32)
    b_decCall = Buf("decCall")
    eTall = sb("eTallB", [128, NCH, NH], F32)
    b_eTall = Buf("eTall")

    with contextlib.ExitStack() as stp:
        sbp = lambda name, shape, dt: stp.enter_context(nc.sbuf_tensor(name, shape, dt))
        psP = stp.enter_context(nc.psum_tensor("psPB", [128, 128], F32))
        b_psP = Buf("psP")
        dts = sbp("dtsB", [NH, TPC], F32)
        b_dts = Buf("dts")
        em.dma("sp_d", dts.ap(), dram["dt_soft_hbm"], writes=[b_dts])
        acol = sbp("acolB", [NH, 2], F32)
        b_acol = Buf("acol")
        em.dma("sp_d", acol.ap()[:, 0:1], dram["A_log_col"], writes=[b_acol])
        em.act(lambda e: e.activation(acol.ap()[:, 1:2], acol.ap()[:, 0:1], AF.Exp,
                                      scale=1.0), reads=[b_acol], writes=[b_acol])
        em.dve(lambda e: e.tensor_scalar(acol.ap()[:, 1:2], acol.ap()[:, 1:2], -1.0,
                                         None, ALU.mult), reads=[b_acol],
               writes=[b_acol])
        av = sbp("avB", [NH, TPC], F32)
        b_av = Buf("av")
        em.dve(lambda e: e.tensor_scalar(av.ap(), dts.ap(), acol.ap()[:, 1:2], None,
                                         ALU.mult), reads=[b_dts, b_acol],
               writes=[b_av])
        ones64 = sbp("ones64B", [NH, Q], F32)
        b_ones64 = Buf("ones64")
        em.dve(lambda e: e.memset(ones64.ap(), 1.0), writes=[b_ones64])
        clocal = sbp("clocalB", [NH, TPC], F32)
        b_cl = Buf("clocal")
        for ch in range(NCH):
            em.dve(lambda e, ch=ch: e.tensor_tensor_scan(
                clocal.ap()[:, ch * Q:(ch + 1) * Q], ones64.ap(),
                av.ap()[:, ch * Q:(ch + 1) * Q], 0.0, ALU.mult, ALU.add),
                reads=[b_av, b_ones64], writes=[b_cl])
        totsb = sbp("totsbB", [NH, NCH], F32)
        b_tot = Buf("totsb")
        em.dve(lambda e: e.tensor_copy(totsb.ap(), clocal.ap()[:, Q - 1::Q]),
               reads=[b_cl], writes=[b_tot])
        totT = sbp("totTB", [NCH, NH], F32)
        b_totT = Buf("totT")
        decT = sbp("decTB", [NCH, NH], F32)
        b_decT = Buf("decT")
        em.pe(lambda e: e.transpose(psP.ap()[0:NCH, 0:NH], totsb.ap(),
                                    identf.ap()[0:NH, 0:NH]),
              reads=[b_tot, b_identf], writes=[b_psP])
        em.dve(lambda e: e.tensor_copy(totT.ap(), psP.ap()[0:NCH, 0:NH]),
               reads=[b_psP], writes=[b_totT])
        em.act(lambda e: e.activation(decT.ap(), totT.ap(), AF.Exp),
               reads=[b_totT], writes=[b_decT])
        eye16 = sbp("eye16B", [NCH, TPC], F32)
        b_eye16 = Buf("eye16")
        em.dma("sp_d", eye16.ap(), dram["eye16rep"], writes=[b_eye16])
        if summary_only:
            dtot = sbp("dtotB", [NH, 2], F32)
            b_dtot = Buf("dtot")
            em.dve(lambda e: e.tensor_reduce(dtot.ap()[:, 0:1], totsb.ap(), AX.X,
                                             ALU.add), reads=[b_tot],
                   writes=[b_dtot])
            em.act(lambda e: e.activation(dtot.ap()[:, 1:2], dtot.ap()[:, 0:1],
                                          AF.Exp), reads=[b_dtot], writes=[b_dtot])
            if "cc_D_in" in dram:
                drow_o = sbp("drowoB", [1, NH], F32)
                b_drow_o = Buf("drow_o")
                em.pe(lambda e: e.transpose(psP.ap()[0:1, 0:NH], dtot.ap()[:, 1:2],
                                            identf.ap()[0:NH, 0:NH]),
                      reads=[b_dtot, b_identf], writes=[b_psP])
                em.dve(lambda e: e.tensor_copy(drow_o.ap(), psP.ap()[0:1, 0:NH]),
                       reads=[b_psP], writes=[b_drow_o])
                em.dma("sp_d", dram["cc_D_in"], drow_o.ap(), reads=[b_drow_o])
            else:
                em.dma("sp_d", dram["Dt_col"], dtot.ap()[:, 1:2], reads=[b_dtot])
        wtmp = sbp("wtmpB", [128, 2 * NH], F32)
        b_wtmp = Buf("wtmp")
        dtf = sbp("dtfB", [128, NH], F32)
        b_dtf = Buf("dtf")
        lnt = sbp("lntB", [128, NH], F32)
        b_lnt = Buf("lnt")
        for ch in range(NCH):
            csl = slice(ch * Q, (ch + 1) * Q)
            em.pe(lambda e, csl=csl: e.transpose(
                psP.ap()[:, 0:NH], clocal.ap()[:, csl], identf.ap()[0:NH, 0:NH]),
                reads=[b_cl, b_identf], writes=[b_psP])
            em.dve(lambda e, ch=ch: e.tensor_scalar(
                negall.ap()[:, ch, :], psP.ap()[:, 0:NH], -1.0, None, ALU.mult),
                reads=[b_psP], writes=[b_negall])
            em.pe(lambda e, csl=csl: e.transpose(
                psP.ap()[:, 0:NH], dts.ap()[:, csl], identf.ap()[0:NH, 0:NH]),
                reads=[b_dts, b_identf, b_negall], writes=[b_psP])
            em.dve(lambda e: e.tensor_copy(dtf.ap(), psP.ap()[:, 0:NH]),
                   reads=[b_psP], writes=[b_dtf])
            em.act(lambda e: e.activation(lnt.ap(), dtf.ap(), AF.Ln),
                   reads=[b_dtf], writes=[b_lnt])
            em.dve(lambda e, ch=ch: e.tensor_add(
                biasall.ap()[:, ch, :], negall.ap()[:, ch, :], lnt.ap()),
                reads=[b_negall, b_lnt], writes=[b_biasall])
            em.act(lambda e, ch=ch: e.activation(
                eTall.ap()[:, ch, :], negall.ap()[:, ch, :], AF.Exp, scale=-1.0),
                reads=[b_negall], writes=[b_eTall])
            em.pe(lambda e, csl=csl: e.matmul(
                psP.ap()[:, 0:NH], eye16.ap()[:, csl], totT.ap(),
                start=True, stop=True),
                reads=[b_eye16, b_totT, b_lnt], writes=[b_psP])
            em.dve(lambda e, ch=ch: e.tensor_add(
                wtmp.ap()[:, 0:NH], psP.ap()[:, 0:NH], negall.ap()[:, ch, :]),
                reads=[b_psP, b_negall], writes=[b_wtmp])
            em.act(lambda e: e.activation(wtmp.ap()[:, NH:2 * NH],
                                          wtmp.ap()[:, 0:NH], AF.Exp),
                   reads=[b_wtmp], writes=[b_wtmp])
            em.pool(lambda e, ch=ch: e.tensor_tensor(
                dwall.ap()[:, ch, :], dtf.ap(),
                wtmp.ap()[:, NH:2 * NH], ALU.mult),
                reads=[b_wtmp, b_dtf], writes=[b_dwall])
            em.pe(lambda e, csl=csl: e.matmul(
                psP.ap()[:, 0:NH], eye16.ap()[:, csl], decT.ap(),
                start=True, stop=True),
                reads=[b_eye16, b_decT, b_wtmp], writes=[b_psP])
            em.dve(lambda e, ch=ch: e.tensor_copy(
                decCall.ap()[:, ch, :], psP.ap()[:, 0:NH]),
                reads=[b_psP], writes=[b_decCall])
        em.fence()

    Bt = sb("BtB", [128, TPC], BF16)
    b_Bt = Buf("Bt")
    em.dma("sp_d", Bt.ap(), dram["xssm_hbm"][DSSM:DSSM + NST, :], writes=[b_Bt])
    Ct = sb("CtB", [128, TPC], BF16)
    b_Ct = Buf("Ct")
    em.dma("sp_d", Ct.ap(), dram["xssm_hbm"][DSSM + NST:, :], writes=[b_Ct])

    hst = sb("hstB", [128, DSSM], F32)
    b_h = [Buf(f"h{g}") for g in range(8)]
    if summary_only:
        psS_ = [ps(f"psSB{i}", [128, 1024], F32) for i in range(2)]
        b_psS_ = [Buf(f"psS{i}") for i in range(2)]
    else:
        psS_ = [ps("psSB", [128, 1024], F32)] * 2
        b_psS_ = [Buf("psS")] * 2
    psS, b_psS = psS_[0], b_psS_[0]
    if summary_only:
        em.dve(lambda e: e.memset(hst.ap(), 0.0), writes=b_h)
        hbf, b_hbf = None, None
    else:
        hbf = sb("hbfB", [128, DSSM], BF16)
        b_hbf = [Buf(f"hbf{g}") for g in range(8)]
        onesf = sb("onesfB", [1, 128], F32)
        b_onesf = Buf("onesf")
        em.dve(lambda e: e.memset(onesf.ap(), 1.0), writes=[b_onesf])
        drow = sb("drowB", [1, 3 * NH], F32)
        b_drow = Buf("drow")
        with contextlib.ExitStack() as sti:
            stage = sti.enter_context(nc.sbuf_tensor("stageB", [128, DSSM], F32))
            b_stage = Buf("stage")
            em.dma("sp_d", stage.ap(), dram["S_in0"], writes=[b_stage])
            em.dve(lambda e: e.tensor_copy(hst.ap(), stage.ap()),
                   reads=[b_stage], writes=b_h)
            for j in (1, 2):
                em.dma("sp_d", drow.ap()[:, (j - 1) * NH:j * NH],
                       dram[f"Dt_row{j}"], writes=[b_drow])
                em.pe(lambda e, j=j: e.matmul(
                    psS.ap()[:, 0:NH], onesf.ap(), drow.ap()[:, (j - 1) * NH:j * NH],
                    start=True, stop=True),
                    reads=[b_drow, b_onesf], writes=[b_psS])
                em.dve(lambda e: e.tensor_tensor(
                    hst.ap().rearrange("p (h q) -> p h q", q=P),
                    hst.ap().rearrange("p (h q) -> p h q", q=P),
                    psS.ap()[:, 0:NH].unsqueeze(2).broadcast_to([128, NH, P]),
                    ALU.mult),
                    reads=[b_psS] + b_h, writes=b_h)
                em.dma("sp_d", stage.ap(), dram[f"S_in{j}"], writes=[b_stage])
                em.dve(lambda e: e.tensor_add(hst.ap(), hst.ap(), stage.ap()),
                       reads=[b_stage] + b_h, writes=b_h)
            for g in range(8):
                em.act(lambda e, g=g: e.activation(
                    hbf.ap()[:, g * 1024:(g + 1) * 1024],
                    hst.ap()[:, g * 1024:(g + 1) * 1024], AF.Copy),
                    reads=[b_h[g]], writes=[b_hbf[g]])
            em.fence()

    xT = [sb(f"xTB{i}", [128, CONV_DIM], BF16) for i in range(2)]
    b_xT = [Buf(f"xT{i}") for i in range(2)]
    dtx2 = sb("dtx2B", [128, DSSM], BF16)
    b_dtx2 = Buf("dtx2")
    htmp = sb("htmpB", [128, 1024], F32)
    b_htmp = Buf("htmp")

    if not summary_only:
        ddiag = sb("ddiagB", [128, NH, 128], BF16)
        b_ddiag = Buf("ddiag")
        identb2 = sb("identb2B", [128, 128], BF16)
        b_identb2 = Buf("identb2")
        em.dma("sp_d", identb2.ap(), dram["ident_bf16"], writes=[b_identb2])
        drepf = sb("drepfB", [128, NH], F32)
        b_drepf = Buf("drepf")
        em.dma("sp_d", drepf.ap(), dram["D_rep"], writes=[b_drepf])
        for hh in range(NH):
            em.dve(lambda e, hh=hh: e.tensor_scalar(
                ddiag.ap()[:, hh, :], identb2.ap(), drepf.ap()[:, hh:hh + 1],
                None, ALU.mult),
                reads=[b_identb2, b_drepf], writes=[b_ddiag])
        Wsb = [sb(f"WsbB{i}", [128, 1024], F32) for i in range(2)]
        b_Wsb = [Buf(f"Wsb{i}") for i in range(2)]
        MtA = [sb(f"MtAB{i}", [128, 8, 1024], BF16) for i in range(2)]
        b_MtA = [[Buf(f"MtA{i}_{g}") for g in range(8)] for i in range(2)]
        Gtb2 = [sb(f"GtbB{i}", [128, 128], BF16) for i in range(2)]
        b_Gtb2 = [Buf(f"Gtb{i}") for i in range(2)]
        ysb = sb("ysbB", [128, 1024], F32)
        b_ysb = Buf("ysb")
        yo2 = [sb(f"yo2B{i}", [128, 1024], BF16) for i in range(2)]
        b_yo2 = [Buf(f"yo2{i}") for i in range(2)]
        psE = ps("psEB", [128, 1024], F32)
        b_psE = Buf("psE")
        psI = ps("psIB", [128, 1024], F32)
        b_psI = Buf("psI")
        psN = ps("psNB", [128, 1024], F32)
        b_psN = Buf("psN")

    def b2_gt(ch):
        cpp = ch % 2
        tc = slice(ch * Q, (ch + 1) * Q)
        em.pe(lambda e, tc=tc: e.matmul(
            psE.ap()[:, 0:128], Bt.ap()[:, tc], Ct.ap()[:, tc],
            start=True, stop=True),
            reads=[b_Bt, b_Ct], writes=[b_psE])
        em.dve(lambda e, cpp=cpp: e.tensor_copy(Gtb2[cpp].ap(), psE.ap()[:, 0:128]),
               reads=[b_psE], writes=[b_Gtb2[cpp]])

    def b2_prod_W(ch, g):
        pp = g % 2
        em.pool(lambda e, g=g, ch=ch, pp=pp: e.tensor_tensor(
            Wsb[pp].ap().rearrange("p (h s) -> p h s", s=Q),
            maskW.ap().unsqueeze(1).broadcast_to([128, 8, Q]),
            negall.ap()[:, ch, g * 8:(g + 1) * 8].unsqueeze(2)
                .broadcast_to([128, 8, Q]),
            ALU.subtract),
            reads=[b_negall, b_mask], writes=[b_Wsb[pp]])

    def b2_prod_tp(ch, g):
        pp = g % 2
        for j in range(8):
            em.pe(lambda e, j=j, pp=pp: e.transpose(
                psE.ap()[:, j * Q:(j + 1) * Q],
                Wsb[pp].ap()[:, j * Q:(j + 1) * Q], identf.ap()),
                reads=[b_Wsb[pp], b_identf], writes=[b_psE])

    def b2_prod_exp(ch, g):
        cpp = ch % 2
        for j in range(8):
            em.act(lambda e, g=g, j=j, ch=ch, cpp=cpp: e.activation(
                MtA[cpp].ap()[:, g, j * Q:(j + 1) * Q],
                psE.ap()[:, j * Q:(j + 1) * Q], AF.Exp,
                bias=biasall.ap()[:, ch, g * 8 + j:g * 8 + j + 1]),
                reads=[b_psE, b_biasall], writes=[b_MtA[cpp][g]])
        em.pool(lambda e, g=g, cpp=cpp: e.tensor_tensor(
            MtA[cpp].ap()[:, g, :].rearrange("p (h s) -> p h s", s=Q),
            MtA[cpp].ap()[:, g, :].rearrange("p (h s) -> p h s", s=Q),
            Gtb2[cpp].ap().unsqueeze(1).broadcast_to([128, 8, Q]),
            ALU.mult),
            reads=[b_Gtb2[cpp]], writes=[b_MtA[cpp][g]])

    def b2_dtx(ch):
        pa = ch % 2
        em.dve(lambda e, pa=pa, ch=ch: e.tensor_tensor(
            dtx2.ap().rearrange("p (h q) -> p h q", q=P),
            xT[pa].ap()[:, 0:DSSM].rearrange("p (h q) -> p h q", q=P),
            dwall.ap()[:, ch, :].unsqueeze(2).broadcast_to([128, NH, P]),
            ALU.mult),
            reads=[b_xT[pa], b_dwall], writes=[b_dtx2])

    def b2_cons_mm(ch, g):
        cpp = ch % 2
        pa = ch % 2
        tc = slice(ch * Q, (ch + 1) * Q)
        for m in range(2):
            em.pe(lambda e, g=g, m=m, tc=tc: e.matmul(
                psN.ap()[:, m * 512:(m + 1) * 512],
                Ct.ap()[:, tc],
                hbf.ap()[:, g * 1024 + m * 512:g * 1024 + (m + 1) * 512],
                start=True, stop=True),
                reads=[b_Ct, b_hbf[g]], writes=[b_psN])
        for j in range(8):
            em.pe(lambda e, g=g, j=j, cpp=cpp, pa=pa: e.matmul(
                psI.ap()[:, j * Q:(j + 1) * Q],
                MtA[cpp].ap()[:, g, j * Q:(j + 1) * Q],
                xT[pa].ap()[:, (g * 8 + j) * P:(g * 8 + j + 1) * P],
                start=True, stop=False),
                reads=[b_MtA[cpp][g], b_xT[pa]], writes=[b_psI])
            em.pe(lambda e, g=g, j=j, pa=pa: e.matmul(
                psI.ap()[:, j * Q:(j + 1) * Q],
                ddiag.ap()[:, g * 8 + j, :],
                xT[pa].ap()[:, (g * 8 + j) * P:(g * 8 + j + 1) * P],
                start=False, stop=True),
                reads=[b_ddiag, b_xT[pa]], writes=[b_psI])
        for m in range(2):
            em.pe(lambda e, g=g, m=m, pa=pa: e.matmul(
                psS_[0].ap()[:, m * 512:(m + 1) * 512],
                xT[pa].ap()[:, DSSM:DSSM + NST],
                dtx2.ap()[:, g * 1024 + m * 512:g * 1024 + (m + 1) * 512],
                start=True, stop=True),
                reads=[b_xT[pa], b_dtx2], writes=[b_psS_[0]])

    def b2_cons_ev(ch, g):
        hsl = slice(g * 1024, (g + 1) * 1024)
        em.dve(lambda e, g=g, ch=ch: e.tensor_tensor(
            ysb.ap().rearrange("p (h q) -> p h q", q=P),
            psN.ap().rearrange("p (h q) -> p h q", q=P),
            eTall.ap()[:, ch, g * 8:(g + 1) * 8].unsqueeze(2)
                 .broadcast_to([128, 8, P]),
            ALU.mult),
            reads=[b_psN, b_eTall], writes=[b_ysb])
        gy = g % 2
        em.dve(lambda e, gy=gy: e.tensor_add(
            yo2[gy].ap(), ysb.ap(), psI.ap()),
            reads=[b_ysb, b_psI], writes=[b_yo2[gy]])
        em.dma("sp_d", dram["y_hbm"][ch * Q:(ch + 1) * Q, hsl], yo2[gy].ap(),
               reads=[b_yo2[gy]], tag=f"sty{ch}_{g}")
        for j in range(8):
            em.dve(lambda e, g=g, j=j, ch=ch: e.scalar_tensor_tensor(
                hst.ap()[:, (g * 8 + j) * P:(g * 8 + j + 1) * P],
                hst.ap()[:, (g * 8 + j) * P:(g * 8 + j + 1) * P],
                decCall.ap()[:, ch, g * 8 + j:g * 8 + j + 1],
                psS_[0].ap()[:, j * P:(j + 1) * P],
                ALU.mult, ALU.add),
                reads=[b_h[g], b_decCall, b_psS_[0]], writes=[b_h[g]])
        em.act(lambda e, hsl=hsl: e.activation(
            hbf.ap()[:, hsl], hst.ap()[:, hsl], AF.Copy),
            reads=[b_h[g]], writes=[b_hbf[g]])

    if summary_only:
        for ch in range(NCH):
            tc = slice(ch * Q, (ch + 1) * Q)
            pa = ch % 2
            em.dma("sp_d", xT[pa].ap(), dram["xssm_hbm"][:, tc],
                   writes=[b_xT[pa]], tag=f"{tag}xT{ch}", transpose=True)
            em.dve(lambda e, pa=pa, ch=ch: e.tensor_tensor(
                dtx2.ap().rearrange("p (h q) -> p h q", q=P),
                xT[pa].ap()[:, 0:DSSM].rearrange("p (h q) -> p h q", q=P),
                dwall.ap()[:, ch, :].unsqueeze(2).broadcast_to([128, NH, P]),
                ALU.mult),
                reads=[b_xT[pa], b_dwall], writes=[b_dtx2])
            for g in range(8):
                hsl = slice(g * 1024, (g + 1) * 1024)
                ppS = g % 2
                for m in range(2):
                    em.pe(lambda e, g=g, m=m, pa=pa, ppS=ppS: e.matmul(
                        psS_[ppS].ap()[:, m * 512:(m + 1) * 512],
                        xT[pa].ap()[:, DSSM:DSSM + NST],
                        dtx2.ap()[:, g * 1024 + m * 512:g * 1024 + (m + 1) * 512],
                        start=True, stop=True),
                        reads=[b_xT[pa], b_dtx2], writes=[b_psS_[ppS]])
                em.dve(lambda e, g=g, ch=ch, hsl=hsl: e.tensor_tensor(
                    htmp.ap().rearrange("p (h q) -> p h q", q=P),
                    hst.ap()[:, hsl].rearrange("p (h q) -> p h q", q=P),
                    decCall.ap()[:, ch, g * 8:(g + 1) * 8].unsqueeze(2)
                           .broadcast_to([128, 8, P]),
                    ALU.mult),
                    reads=[b_h[g], b_decCall], writes=[b_htmp])
                em.dve(lambda e, hsl=hsl, ppS=ppS: e.tensor_add(
                    hst.ap()[:, hsl], htmp.ap(), psS_[ppS].ap()),
                    reads=[b_htmp, b_psS_[ppS]], writes=[b_h[g]])
    else:
        for ch in range(NCH + 1):
            if ch < NCH:
                tc = slice(ch * Q, (ch + 1) * Q)
                pa = ch % 2
                em.dma("sp_d", xT[pa].ap(), dram["xssm_hbm"][:, tc],
                       writes=[b_xT[pa]], tag=f"{tag}xT{ch}", transpose=True)
                b2_gt(ch)
            if ch >= 1:
                b2_dtx(ch - 1)
            for g in range(8):
                if ch < NCH:
                    b2_prod_W(ch, g)
                if ch >= 1:
                    b2_cons_mm(ch - 1, g)
                if ch < NCH:
                    b2_prod_tp(ch, g)
                    b2_prod_exp(ch, g)
                if ch >= 1:
                    b2_cons_ev(ch - 1, g)
    if summary_only:
        tgt = dram["cc_S_in"] if "cc_S_in" in dram else dram["S_own"]
        em.dma("sp_d", tgt, hst.ap(), reads=b_h, tag="stS")
    em.fence()



def build_c1(em, dram, st):
    nc = em.nc
    import contextlib
    sb = lambda name, shape, dt: st.enter_context(nc.sbuf_tensor(name, shape, dt))
    ps = lambda name, shape, dt: st.enter_context(nc.psum_tensor(name, shape, dt))

    nw = sb("nwC", [128, NH], F32)
    b_nw = Buf("nw")
    em.dma("sp_d", nw.ap(), dram["nw_col"], writes=[b_nw])
    onesb = sb("onesbC", [128, 1], BF16)
    b_onesb = Buf("onesb")
    em.dma("sp_d", onesb.ap(), dram["ones_col_bf16"], writes=[b_onesb])
    onesf = sb("onesfC", [1, 128], F32)
    b_onesf = Buf("onesf")
    em.dve(lambda e: e.memset(onesf.ap(), 1.0), writes=[b_onesf])

    ygnw = sb("ygnwC", [128, NH, 1024], BF16)
    b_yg = [Buf(f"yg{j}") for j in range(NH)]
    zt = [sb(f"ztC{i}", [128, 1024], BF16) for i in range(2)]
    b_zt = [Buf(f"zt{i}") for i in range(2)]
    zs = [sb(f"zsC{i}", [128, 1024], BF16) for i in range(2)]
    b_zs = [Buf(f"zs{i}") for i in range(2)]
    sq = [sb(f"sqC{i}", [128, 1024], BF16) for i in range(2)]
    b_sq = [Buf(f"sq{i}") for i in range(2)]
    psV = ps("psVC", [1, 1024], F32)
    b_psV = Buf("psV")
    psR = ps("psRC", [128, 1024], F32)
    b_psR = Buf("psR")
    rmsrow = sb("rmsrowC", [1, 1024], F32)
    b_rms = Buf("rmsrow")
    invrep = sb("invrepC", [128, 1024], F32)
    b_invrep = Buf("invrep")
    wpan = [sb(f"wpanC{i}", [128, NH, 128], BF16) for i in range(2)]
    b_wpan = [Buf(f"wpanC{i}") for i in range(2)]
    psA = [ps(f"psAC{i}", [128, 1024], F32) for i in range(2)]
    b_psA = [Buf(f"psA{i}") for i in range(2)]
    att = [sb(f"attC{i}", [128, 1024], BF16) for i in range(2)]
    b_att = [Buf(f"att{i}") for i in range(2)]

    for hf in range(2):
        tsl = slice(hf * 1024, (hf + 1) * 1024)
        for j in range(NH):
            pa = j % 2
            em.dma("sp_d", ygnw.ap()[:, j, :], dram["y_hbm"][tsl, j * 128:(j + 1) * 128],
                   writes=[b_yg[j]], transpose=True, tag=f"ldy{hf}_{j}")
            em.dma("sp_d", zt[pa].ap(), dram["z_hbm"][j * 128:(j + 1) * 128, tsl],
                   writes=[b_zt[pa]], tag=f"ldz{hf}_{j}")
            em.act(lambda e, pa=pa: e.activation(zs[pa].ap(), zt[pa].ap(), AF.Silu),
                   reads=[b_zt[pa]], writes=[b_zs[pa]])
            em.dve(lambda e, j=j, pa=pa: e.tensor_tensor(
                ygnw.ap()[:, j, :], ygnw.ap()[:, j, :], zs[pa].ap(), ALU.mult),
                reads=[b_zs[pa], b_yg[j]], writes=[b_yg[j]])
            em.act(lambda e, j=j, pa=pa: e.activation(
                sq[pa].ap(), ygnw.ap()[:, j, :], AF.Square),
                reads=[b_yg[j]], writes=[b_sq[pa]])
            for m in range(2):
                em.pe(lambda e, pa=pa, m=m, j=j: e.matmul(
                    psV.ap()[:, m * 512:(m + 1) * 512], onesb.ap(),
                    sq[pa].ap()[:, m * 512:(m + 1) * 512],
                    start=(j == 0), stop=(j == NH - 1)),
                    reads=[b_sq[pa], b_onesb], writes=[b_psV])
            em.dve(lambda e, j=j: e.tensor_scalar(
                ygnw.ap()[:, j, :], ygnw.ap()[:, j, :], nw.ap()[:, j:j + 1],
                None, ALU.mult),
                reads=[b_nw, b_yg[j]], writes=[b_yg[j]])
        em.act(lambda e: e.activation(rmsrow.ap(), psV.ap(), AF.Ln,
                                      bias=EPS, scale=1.0 / DSSM),
               reads=[b_psV], writes=[b_rms])
        em.act(lambda e: e.activation(rmsrow.ap(), rmsrow.ap(), AF.Exp, scale=-0.5),
               reads=[b_rms], writes=[b_rms])
        for m in range(2):
            em.pe(lambda e, m=m: e.matmul(
                psR.ap()[:, m * 512:(m + 1) * 512], onesf.ap(),
                rmsrow.ap()[:, m * 512:(m + 1) * 512], start=True, stop=True),
                reads=[b_rms, b_onesf], writes=[b_psR])
        em.dve(lambda e: e.tensor_copy(invrep.ap(), psR.ap()),
               reads=[b_psR], writes=[b_invrep])
        for d in range(32):
            pa = d % 2
            src = dram["W_out"].rearrange("(i p) c -> p i c", p=128)[
                :, :, d * 128:(d + 1) * 128]
            for piece in range(8):
                em.dma("pool_d", wpan[pa].ap()[:, piece * 8:(piece + 1) * 8, :],
                       src[:, piece * 8:(piece + 1) * 8, :],
                       writes=[b_wpan[pa]], tag=f"ldWo{hf}_{d}_{piece}")
            for j in range(NH):
                for m in range(2):
                    em.pe(lambda e, pa=pa, j=j, m=m: e.matmul(
                        psA[pa].ap()[:, m * 512:(m + 1) * 512],
                        wpan[pa].ap()[:, j, :],
                        ygnw.ap()[:, j, m * 512:(m + 1) * 512],
                        start=(j == 0), stop=(j == NH - 1)),
                        reads=[b_wpan[pa], b_yg[j]], writes=[b_psA[pa]])
            em.dve(lambda e, pa=pa: e.tensor_tensor(
                att[pa].ap(), psA[pa].ap(), invrep.ap(), ALU.mult),
                reads=[b_psA[pa], b_invrep], writes=[b_att[pa]])
            em.dma("sp_d", dram["attn_hbm"][d * 128:(d + 1) * 128, tsl],
                   att[pa].ap(), reads=[b_att[pa]], tag=f"stat{hf}_{d}")
        em.fence()



def build_res_ln(em, dram, st, which):
    nc = em.nc
    sb = lambda name, shape, dt: st.enter_context(nc.sbuf_tensor(name + which, shape, dt))
    ps = lambda name, shape, dt: st.enter_context(nc.psum_tensor(name + which, shape, dt))
    src_ch = dram["attn_hbm"] if which == "c2" else dram["ffn_hbm"]

    ident = sb("identR", [128, 128], BF16)
    b_ident = Buf("ident")
    em.dma("sp_d", ident.ap(), dram["ident_bf16"], writes=[b_ident])

    at = [sb(f"atR{i}", [128, H], BF16) for i in range(2)]
    b_at = [Buf(f"at{i}") for i in range(2)]
    af = [sb(f"afR{i}", [128, H], F32) for i in range(2)]
    b_af = [Buf(f"af{i}") for i in range(2)]
    xm = [sb(f"xmR{i}", [128, H], F32) for i in range(2)]
    b_xm = [Buf(f"xm{i}") for i in range(2)]
    res = [sb(f"resR{i}", [128, H], F32) for i in range(2)]
    b_res = [Buf(f"res{i}") for i in range(2)]
    sq = [sb(f"sqR{i}", [128, H], BF16) for i in range(2)]
    b_sq = [Buf(f"sq{i}") for i in range(2)]
    stats = [sb(f"statsR{i}", [128, 8], F32) for i in range(2)]
    b_st_ = [Buf(f"stats{i}") for i in range(2)]
    if which == "c2":
        x2t = [sb(f"x2tR{i}", [128, H], BF16) for i in range(2)]
        b_x2t = [Buf(f"x2t{i}") for i in range(2)]
        tps = ps("tpsR", [128, 2048], BF16)
        b_tps = Buf("tps")
        evb = sb("evbR", [128, 16, 128], BF16)
        b_evb = Buf("evb")

    for it in range(16):
        r0 = it * 128
        pa = it % 2
        col = lambda i, pa=pa: stats[pa].ap()[:, i:i + 1]
        b_st = b_st_[pa]
        sq_t, b_sq_t = sq[pa], b_sq[pa]
        em.dma("sp_d", at[pa].ap(), src_ch[:, r0:r0 + 128], writes=[b_at[pa]],
               transpose=True, tag=f"ldat{it}")
        if which == "c2":
            em.dma("act_d", res[pa].ap(), dram["xt"][HALO + r0:HALO + r0 + 128, :],
                   writes=[b_res[pa]], tag=f"ldx{it}")
        else:
            em.dma("act_d", res[pa].ap(), dram["xmid_hbm"][r0:r0 + 128, :],
                   writes=[b_res[pa]], tag=f"ldxm{it}")
        em.act(lambda e, pa=pa, col=col: e.activation(
            af[pa].ap(), at[pa].ap(), AF.Identity, accum_out=col(0)),
            reads=[b_at[pa]], writes=[b_af[pa], b_st])
        em.act(lambda e, pa=pa, sq_t=sq_t, col=col: e.activation(
            sq_t.ap(), at[pa].ap(), AF.Square, accum_out=col(1)),
            reads=[b_at[pa]], writes=[b_sq_t, b_st])
        em.dve(lambda e, col=col: e.tensor_scalar(col(2), col(0), 1.0 / H, None, ALU.mult),
               reads=[b_st], writes=[b_st])
        em.dve(lambda e, col=col: e.tensor_scalar(col(3), col(1), 1.0 / H, None, ALU.mult),
               reads=[b_st], writes=[b_st])
        em.dve(lambda e, col=col: e.tensor_mul(col(5), col(2), col(2)),
               reads=[b_st], writes=[b_st])
        em.dve(lambda e, col=col: e.tensor_sub(col(3), col(3), col(5)),
               reads=[b_st], writes=[b_st])
        em.act(lambda e, col=col: e.activation(col(5), col(3), AF.Ln, bias=EPS),
               reads=[b_st], writes=[b_st])
        em.act(lambda e, col=col: e.activation(col(4), col(5), AF.Exp, scale=-0.5),
               reads=[b_st], writes=[b_st])
        em.dve(lambda e, pa=pa, col=col: e.tensor_scalar(
            af[pa].ap(), af[pa].ap(), col(2), col(4), ALU.subtract, ALU.mult),
            reads=[b_st, b_af[pa]], writes=[b_af[pa]])
        if which == "c5":
            em.dve(lambda e, pa=pa: e.tensor_add(
                xm[pa].ap(), af[pa].ap(), res[pa].ap()),
                reads=[b_af[pa], b_res[pa]], writes=[b_xm[pa]])
            em.dma("sp_d", dram["out_hbm"][r0:r0 + 128, :], xm[pa].ap(),
                   reads=[b_xm[pa]], tag=f"stout{it}")
            continue
        em.op("dve", lambda e, pa=pa, col=col: e.scalar_tensor_tensor(
            xm[pa].ap(), af[pa].ap(), 0.0, res[pa].ap(), ALU.add, ALU.add,
            accum_out=col(6)),
            reads=[b_af[pa], b_res[pa]], writes=[b_xm[pa], b_st])
        em.dma("sp_d", dram["xmid_hbm"][r0:r0 + 128, :], xm[pa].ap(),
               reads=[b_xm[pa]], tag=f"stxm{it}")
        em.act(lambda e, pa=pa, sq_t=sq_t, col=col: e.activation(
            sq_t.ap(), xm[pa].ap(), AF.Square, accum_out=col(7)),
            reads=[b_xm[pa]], writes=[b_sq_t, b_st])
        em.dve(lambda e, col=col: e.tensor_scalar(col(2), col(6), 1.0 / H, None, ALU.mult),
               reads=[b_st], writes=[b_st])
        em.dve(lambda e, col=col: e.tensor_scalar(col(3), col(7), 1.0 / H, None, ALU.mult),
               reads=[b_st], writes=[b_st])
        em.dve(lambda e, col=col: e.tensor_mul(col(5), col(2), col(2)),
               reads=[b_st], writes=[b_st])
        em.dve(lambda e, col=col: e.tensor_sub(col(3), col(3), col(5)),
               reads=[b_st], writes=[b_st])
        em.act(lambda e, col=col: e.activation(col(5), col(3), AF.Ln, bias=EPS),
               reads=[b_st], writes=[b_st])
        em.act(lambda e, col=col: e.activation(col(4), col(5), AF.Exp, scale=-0.5),
               reads=[b_st], writes=[b_st])
        em.dve(lambda e, pa=pa, col=col: e.tensor_scalar(
            x2t[pa].ap(), xm[pa].ap(), col(2), col(4), ALU.subtract, ALU.mult),
            reads=[b_st, b_xm[pa]], writes=[b_x2t[pa]])
        for g in range(2):
            for jj in range(16):
                dblk = g * 16 + jj
                em.pe(lambda e, pa=pa, dblk=dblk, jj=jj: e.transpose(
                    tps.ap()[:, jj * 128:(jj + 1) * 128],
                    x2t[pa].ap()[:, dblk * 128:(dblk + 1) * 128], ident.ap()),
                    reads=[b_x2t[pa], b_ident], writes=[b_tps])
            em.dve(lambda e: e.tensor_copy(
                evb.ap(), tps.ap().rearrange("p (j t) -> p j t", j=16)),
                reads=[b_tps], writes=[b_evb])
            em.dma("sp_d",
                   dram["x2T_hbm"].rearrange("(i p) t -> p i t", p=128)[
                       :, g * 16:(g + 1) * 16, r0:r0 + 128],
                   evb.ap(), reads=[b_evb], tag=f"stx2T{it}_{g}")
    em.fence()


def build_c2(em, dram, st):
    build_res_ln(em, dram, st, "c2")


def build_c5(em, dram, st):
    build_res_ln(em, dram, st, "c5")



def build_c34(em, dram, st):
    nc = em.nc
    import contextlib
    sb = lambda name, shape, dt: st.enter_context(nc.sbuf_tensor(name, shape, dt))

    b12 = sb("b12F", [128, NH], F32)
    b_b12 = Buf("b12")
    em.dma("sp_d", b12.ap(), dram["b12_col"], writes=[b_b12])
    b3 = sb("b3F", [128, 32], F32)
    b_b3 = Buf("b3")
    em.dma("sp_d", b3.ap(), dram["b3_col"], writes=[b_b3])

    x2h = sb("x2hF", [128, 32, 1024], BF16)
    b_x2h = Buf("x2h")
    x4h = sb("x4hF", [128, 32, 1024], BF16)
    b_x4 = [Buf(f"x4_{k}") for k in range(32)]
    wp = [sb(f"wpF{i}", [128, 32, 128], BF16) for i in range(4)]
    b_wp = [Buf(f"wpF{i}") for i in range(4)]
    s1t = [sb(f"s1tF{i}", [128, 1024], BF16) for i in range(2)]
    b_s1 = [Buf(f"s1t{i}") for i in range(2)]
    v2t = [sb(f"v2tF{i}", [128, 1024], BF16) for i in range(2)]
    b_v2 = [Buf(f"v2t{i}") for i in range(2)]
    fout = [sb(f"foutF{i}", [128, 1024], BF16) for i in range(2)]
    b_fout = [Buf(f"fout{i}") for i in range(2)]

    for hf in range(2):
        tsl = slice(hf * 1024, (hf + 1) * 1024)
        em.dma("sp_d", x2h.ap(),
               dram["x2T_hbm"].rearrange("(i p) t -> p i t", p=128)[:, :, tsl],
               writes=[b_x2h], tag=f"ldx2h{hf}")
        import contextlib as _ct
        with _ct.ExitStack() as st1:
            ps1 = [st1.enter_context(nc.psum_tensor(f"ps1F{hf}{i}", [128, 1024], F32))
                   for i in range(2)]
            b_ps1_ = [Buf(f"ps1{i}") for i in range(2)]
            ps2 = [st1.enter_context(nc.psum_tensor(f"ps2F{hf}{i}", [128, 1024], F32))
                   for i in range(2)]
            b_ps2_ = [Buf(f"ps2{i}") for i in range(2)]
            for mi in range(32):
                pa = mi % 2
                ps1_, ps2_ = ps1[pa], ps2[pa]
                b_ps1, b_ps2 = b_ps1_[pa], b_ps2_[pa]
                src1 = dram["w12T"].rearrange("(i p) c -> p i c", p=128)[
                    :, :, mi * 128:(mi + 1) * 128]
                src2 = dram["w12T"].rearrange("(i p) c -> p i c", p=128)[
                    :, :, (mi + 32) * 128:(mi + 33) * 128]
                for piece in range(4):
                    pp = slice(piece * 8, (piece + 1) * 8)
                    em.dma("pool_d", wp[pa].ap()[:, pp, :], src1[:, pp, :],
                           writes=[b_wp[pa]], tag=f"ldw12a{hf}_{mi}_{piece}")
                    em.dma("pool_d", wp[2 + pa].ap()[:, pp, :], src2[:, pp, :],
                           writes=[b_wp[2 + pa]], tag=f"ldw12b{hf}_{mi}_{piece}")
                for k in range(32):
                    for m in range(2):
                        em.pe(lambda e, pa=pa, k=k, m=m, ps1_=ps1_: e.matmul(
                            ps1_.ap()[:, m * 512:(m + 1) * 512],
                            wp[pa].ap()[:, k, :],
                            x2h.ap()[:, k, m * 512:(m + 1) * 512],
                            start=(k == 0), stop=(k == 31)),
                            reads=[b_wp[pa], b_x2h], writes=[b_ps1])
                        em.pe(lambda e, pa=pa, k=k, m=m, ps2_=ps2_: e.matmul(
                            ps2_.ap()[:, m * 512:(m + 1) * 512],
                            wp[2 + pa].ap()[:, k, :],
                            x2h.ap()[:, k, m * 512:(m + 1) * 512],
                            start=(k == 0), stop=(k == 31)),
                            reads=[b_wp[2 + pa], b_x2h], writes=[b_ps2])
                em.act(lambda e, mi=mi, ps1_=ps1_, pa=pa: e.activation(
                    s1t[pa].ap(), ps1_.ap(), AF.Silu, bias=b12.ap()[:, mi:mi + 1]),
                    reads=[b_ps1, b_b12], writes=[b_s1[pa]])
                em.act(lambda e, mi=mi, ps2_=ps2_, pa=pa: e.activation(
                    v2t[pa].ap(), ps2_.ap(), AF.Identity,
                    bias=b12.ap()[:, mi + 32:mi + 33]),
                    reads=[b_ps2, b_b12], writes=[b_v2[pa]])
                em.dve(lambda e, mi=mi, pa=pa: e.tensor_tensor(
                    x4h.ap()[:, mi, :], s1t[pa].ap(), v2t[pa].ap(), ALU.mult),
                    reads=[b_s1[pa], b_v2[pa]], writes=[b_x4[mi]])
            em.fence()
        with _ct.ExitStack() as st2:
            psF = [st2.enter_context(nc.psum_tensor(f"psF{hf}_{i}", [128, 1024], F32))
                   for i in range(2)]
            b_psF = [Buf(f"psF{i}") for i in range(2)]
            for ni in range(32):
                pa = ni % 2
                src = dram["w3T"].rearrange("(i p) c -> p i c", p=128)[
                    :, :, ni * 128:(ni + 1) * 128]
                for piece in range(4):
                    pp = slice(piece * 8, (piece + 1) * 8)
                    em.dma("pool_d", wp[pa].ap()[:, pp, :], src[:, pp, :],
                           writes=[b_wp[pa]], tag=f"ldw3{hf}_{ni}_{piece}")
                for k in range(32):
                    for m in range(2):
                        em.pe(lambda e, pa=pa, k=k, m=m: e.matmul(
                            psF[pa].ap()[:, m * 512:(m + 1) * 512],
                            wp[pa].ap()[:, k, :],
                            x4h.ap()[:, k, m * 512:(m + 1) * 512],
                            start=(k == 0), stop=(k == 31)),
                            reads=[b_wp[pa], b_x4[k]], writes=[b_psF[pa]])
                em.act(lambda e, pa=pa, ni=ni: e.activation(
                    fout[pa].ap(), psF[pa].ap(), AF.Identity,
                    bias=b3.ap()[:, ni:ni + 1]),
                    reads=[b_psF[pa], b_b3], writes=[b_fout[pa]])
                em.dma("sp_d", dram["ffn_hbm"][ni * 128:(ni + 1) * 128, tsl],
                       fout[pa].ap(), reads=[b_fout[pa]], tag=f"stf{hf}_{ni}")
            em.fence()


def build_scan_b1(em, dram, st):
    build_scan(em, dram, st, True)


def build_scan_b2(em, dram, st):
    build_scan(em, dram, st, False)


IO_B1 = dict(
    ins=[("xssm_hbm", (CONV_DIM, TPC), BF),
         ("dt_soft_hbm", (NH, TPC), np.float32),
         ("A_log_col", (NH, 1), np.float32),
         ("ident_f32", (128, 128), np.float32),
         ("eye16rep", (NCH, TPC), np.float32),
         ("maskW", (128, 128), np.float32)],
    outs=[("S_own", (128, DSSM), np.float32),
          ("Dt_col", (NH, 1), np.float32)])

IO_B2 = dict(
    ins=[("xssm_hbm", (CONV_DIM, TPC), BF),
         ("dt_soft_hbm", (NH, TPC), np.float32),
         ("A_log_col", (NH, 1), np.float32),
         ("ident_f32", (128, 128), np.float32),
         ("eye16rep", (NCH, TPC), np.float32),
         ("maskW", (128, 128), np.float32),
         ("D_rep", (128, NH), np.float32),
         ("S_in0", (128, DSSM), np.float32),
         ("S_in1", (128, DSSM), np.float32),
         ("S_in2", (128, DSSM), np.float32),
         ("Dt_row1", (1, NH), np.float32),
         ("Dt_row2", (1, NH), np.float32)],
    outs=[("y_hbm", (TPC, DSSM), BF)])


IO_C1 = dict(
    ins=[("y_hbm", (TPC, DSSM), BF),
         ("z_hbm", (DSSM, TPC), BF),
         ("nw_col", (128, NH), np.float32),
         ("W_out", (DSSM, H), np.float32),
         ("ones_col_bf16", (128, 1), BF)],
    outs=[("attn_hbm", (H, TPC), BF)])

IO_C2 = dict(
    ins=[("attn_hbm", (H, TPC), BF),
         ("xt", (HALO + TPC, H), np.float32),
         ("ident_bf16", (128, 128), BF)],
    outs=[("xmid_hbm", (TPC, H), np.float32),
          ("x2T_hbm", (H, TPC), BF)])

IO_C34 = dict(
    ins=[("x2T_hbm", (H, TPC), BF),
         ("w12T", (H, DSSM), np.float32),
         ("b12_col", (128, NH), np.float32),
         ("w3T", (H, H), np.float32),
         ("b3_col", (128, 32), np.float32)],
    outs=[("ffn_hbm", (H, TPC), BF)])

IO_C5 = dict(
    ins=[("ffn_hbm", (H, TPC), BF),
         ("xmid_hbm", (TPC, H), np.float32),
         ("ident_bf16", (128, 128), BF)],
    outs=[("out_hbm", (TPC, H), np.float32)])


def build_program(io, build_fns):
    nc = bass.Bass("TRN2", target_bir_lowering=False)
    dram = {}
    for n, s, d in io["ins"]:
        dram[n] = nc.dram_tensor(n, list(s), _DT_NP2MY[np.dtype(d)],
                                 kind="ExternalInput").ap()
    for n, s, d in io["outs"]:
        dram[n] = nc.dram_tensor(n, list(s), _DT_NP2MY[np.dtype(d)],
                                 kind="ExternalOutput").ap()
    em = Em(nc)
    import contextlib
    with contextlib.ExitStack() as st:
        em.setup_consts(st, [EPS, 1.0, -1.0, -0.5, 0.5])
        for f in build_fns:
            f(em, dram, st)
        em.finalize_and_emit()
    return nc


def run_phase(io, build_fns, in_maps):
    nc = build_program(io, build_fns)
    res = run_bass_kernel_spmd(nc, in_maps, core_ids=list(range(len(in_maps))))
    return res.results


class _NW:

    def __init__(self, nc, pfx):
        object.__setattr__(self, "_nc", nc)
        object.__setattr__(self, "_p", pfx)

    def sbuf_tensor(self, name, *a, **k):
        return self._nc.sbuf_tensor(self._p + name, *a, **k)

    def psum_tensor(self, name, *a, **k):
        return self._nc.psum_tensor(self._p + name, *a, **k)

    def __getattr__(self, n):
        return getattr(self._nc, n)


def build_cc(em, dram, st):
    b = Buf("cc")
    groups = [[0, 1, 2, 3], [4, 5, 6, 7]]
    em.op("pool_d", lambda e: e.collective_compute(
        "AllGather", ALU.bypass, replica_groups=groups,
        ins=[dram["cc_S_in"].opt()], outs=[dram["cc_S_out"].opt()]),
        writes=[b], tag="ccS")
    em.op("pool_d", lambda e: e.collective_compute(
        "AllGather", ALU.bypass, replica_groups=groups,
        ins=[dram["cc_D_in"].opt()], outs=[dram["cc_D_out"].opt()]),
        writes=[b], tag="ccD")


FUSED_INS = [
    ("xt", (HALO + TPC, H), np.float32),
    ("W_in", (H, PROJ_DIM), np.float32),
    ("conv_w", (CONV_DIM, KCONV), np.float32),
    ("dt_bias_col", (NH, 1), np.float32),
    ("A_log_col", (NH, 1), np.float32),
    ("ident_bf16", (128, 128), BF),
    ("ident_f32", (128, 128), np.float32),
    ("eye16rep", (NCH, TPC), np.float32),
    ("maskW", (128, 128), np.float32),
    ("D_rep", (128, NH), np.float32),
    ("nw_col", (128, NH), np.float32),
    ("W_out", (DSSM, H), np.float32),
    ("ones_col_bf16", (128, 1), BF),
    ("w12T", (H, DSSM), np.float32),
    ("b12_col", (128, NH), np.float32),
    ("w3T", (H, H), np.float32),
    ("b3_col", (128, 32), np.float32),
    ("relay_cols", (128, 8), np.float32),
]
FUSED_INTERNAL = [
    ("z_hbm", (DSSM, TPC), BF),
    ("xssm_hbm", (CONV_DIM, TPC), BF),
    ("dt_soft_hbm", (NH, TPC), np.float32),
    ("y_hbm", (TPC, DSSM), BF),
    ("attn_hbm", (H, TPC), BF),
    ("xmid_hbm", (TPC, H), np.float32),
    ("x2T_hbm", (H, TPC), BF),
    ("ffn_hbm", (H, TPC), BF),
    ("cc_S_in", (128, DSSM), np.float32),
    ("cc_S_out", (512, DSSM), np.float32),
    ("cc_D_in", (1, NH), np.float32),
    ("cc_D_out", (4, NH), np.float32),
]

FUSED_PHASES = [build_phase_a, build_scan_b1, build_cc, build_scan_b2,
                build_c1, build_c2, build_c34, build_c5]


def build_multi(phases, ins, outs, internals):
    nc = bass.Bass("TRN2", target_bir_lowering=False)
    dram = {}
    for n, s, d in ins:
        dram[n] = nc.dram_tensor(n, list(s), _DT_NP2MY[np.dtype(d)],
                                 kind="ExternalInput").ap()
    for n, s, d in outs:
        dram[n] = nc.dram_tensor(n, list(s), _DT_NP2MY[np.dtype(d)],
                                 kind="ExternalOutput").ap()
    for n, s, d in internals:
        dram[n] = nc.dram_tensor(n, list(s), _DT_NP2MY[np.dtype(d)],
                                 kind="Internal").ap()
    em = Em(nc)
    import contextlib
    with contextlib.ExitStack() as stc:
        em.setup_consts(stc, [EPS, 1.0, -1.0, -0.5, 0.5])
        for i, f in enumerate(phases):
            em.nc = _NW(nc, f"p{i}_")
            with contextlib.ExitStack() as stf:
                f(em, dram, stf)
                em.fence()
        em.nc = nc
        em.finalize_and_emit()
    return nc



L1_INS = [
    ("xt", (HALO + TPC, H), np.float32),
    ("W_in", (H, PROJ_DIM), np.float32),
    ("conv_w", (CONV_DIM, KCONV), np.float32),
    ("dt_bias_col", (NH, 1), np.float32),
    ("A_log_col", (NH, 1), np.float32),
    ("ident_bf16", (128, 128), BF),
    ("ident_f32", (128, 128), np.float32),
    ("eye16rep", (NCH, TPC), np.float32),
    ("maskW", (128, 128), np.float32),
]
L1_OUTS = [
    ("z_hbm", (DSSM, TPC), BF),
    ("xssm_hbm", (CONV_DIM, TPC), BF),
    ("dt_soft_hbm", (NH, TPC), np.float32),
    ("S_own", (128, DSSM), np.float32),
    ("Dt_col", (NH, 1), np.float32),
]
L2_INS = [
    ("xt", (HALO + TPC, H), np.float32),
    ("z_hbm", (DSSM, TPC), BF),
    ("xssm_hbm", (CONV_DIM, TPC), BF),
    ("dt_soft_hbm", (NH, TPC), np.float32),
    ("A_log_col", (NH, 1), np.float32),
    ("ident_bf16", (128, 128), BF),
    ("ident_f32", (128, 128), np.float32),
    ("eye16rep", (NCH, TPC), np.float32),
    ("maskW", (128, 128), np.float32),
    ("D_rep", (128, NH), np.float32),
    ("nw_col", (128, NH), np.float32),
    ("W_out", (DSSM, H), np.float32),
    ("ones_col_bf16", (128, 1), BF),
    ("w12T", (H, DSSM), np.float32),
    ("b12_col", (128, NH), np.float32),
    ("w3T", (H, H), np.float32),
    ("b3_col", (128, 32), np.float32),
    ("S_in0", (128, DSSM), np.float32),
    ("S_in1", (128, DSSM), np.float32),
    ("S_in2", (128, DSSM), np.float32),
    ("Dt_row1", (1, NH), np.float32),
    ("Dt_row2", (1, NH), np.float32),
]
L2_OUTS = [("out_hbm", (TPC, H), np.float32)]
L2_INTERNAL = [
    ("y_hbm", (TPC, DSSM), BF),
    ("attn_hbm", (H, TPC), BF),
    ("xmid_hbm", (TPC, H), np.float32),
    ("x2T_hbm", (H, TPC), BF),
    ("ffn_hbm", (H, TPC), BF),
]


def build_l1():
    return build_multi([build_phase_a, build_scan_b1], L1_INS, L1_OUTS, [])


def build_l2():
    return build_multi([build_scan_b2, build_c1, build_c2, build_c34, build_c5],
                       L2_INS, L2_OUTS, L2_INTERNAL)


def _maps(io, shared, percore, extra=None):
    out = []
    for core in range(8):
        m = {}
        for k, _, _ in io["ins"]:
            if k in shared:
                m[k] = shared[k]
            elif k in percore[core]:
                m[k] = percore[core][k]
            else:
                m[k] = extra[core][k]
        out.append(m)
    return out


def kernel(**inputs):
    shared, percore = host_prep(inputs)
    nc1 = build_l1()
    maps1 = []
    for core in range(8):
        m = {k: (shared[k] if k in shared else percore[core][k])
             for k, _, _ in L1_INS}
        maps1.append(m)
    r1 = run_bass_kernel_spmd(nc1, maps1, core_ids=list(range(8))).results
    nc2 = build_l2()
    zS = np.zeros((128, DSSM), np.float32)
    oD = np.ones((1, NH), np.float32)
    maps2 = []
    for core in range(8):
        b, q = core // 4, core % 4
        m = {}
        for k, _, _ in L2_INS:
            if k in shared:
                m[k] = shared[k]
            elif k in percore[core]:
                m[k] = percore[core][k]
            elif k in r1[core]:
                m[k] = r1[core][k]
        for slot, j in enumerate((q - 3, q - 2, q - 1)):
            m[f"S_in{slot}"] = r1[4 * b + j]["S_own"] if j >= 0 else zS
        for slot, j in zip((1, 2), (q - 2, q - 1)):
            m[f"Dt_row{slot}"] = (np.ascontiguousarray(
                r1[4 * b + j]["Dt_col"].reshape(1, NH)) if j >= 0 else oD)
        maps2.append(m)
    r2 = run_bass_kernel_spmd(nc2, maps2, core_ids=list(range(8))).results
    out = np.empty((BATCH, SEQ, H), np.float32)
    for core in range(8):
        b, q = core // 4, core % 4
        out[b, q * TPC:(q + 1) * TPC] = r2[core]["out_hbm"]
    return out


def kernel_multilaunch(**inputs):
    shared, percore = host_prep(inputs)
    r_a = run_phase(IO_A, [build_phase_a], _maps(IO_A, shared, percore))
    r_b1 = run_phase(IO_B1, [build_scan_b1],
                     _maps(IO_B1, shared, percore, r_a))
    extra = []
    zS = np.zeros((128, DSSM), np.float32)
    oD = np.ones((1, NH), np.float32)
    for core in range(8):
        b, q = core // 4, core % 4
        m = dict(r_a[core])
        for slot, j in enumerate((q - 3, q - 2, q - 1)):
            m[f"S_in{slot}"] = (r_b1[4 * b + j]["S_own"] if j >= 0 else zS)
        for slot, j in zip((1, 2), (q - 2, q - 1)):
            m[f"Dt_row{slot}"] = (np.ascontiguousarray(
                r_b1[4 * b + j]["Dt_col"].reshape(1, NH)) if j >= 0 else oD)
        extra.append(m)
    r_b2 = run_phase(IO_B2, [build_scan_b2], _maps(IO_B2, shared, percore, extra))
    for core in range(8):
        extra[core]["y_hbm"] = r_b2[core]["y_hbm"]
    r_c1 = run_phase(IO_C1, [build_c1], _maps(IO_C1, shared, percore, extra))
    for core in range(8):
        extra[core]["attn_hbm"] = r_c1[core]["attn_hbm"]
    r_c2 = run_phase(IO_C2, [build_c2], _maps(IO_C2, shared, percore, extra))
    for core in range(8):
        extra[core]["x2T_hbm"] = r_c2[core]["x2T_hbm"]
        extra[core]["xmid_hbm"] = r_c2[core]["xmid_hbm"]
    r_c34 = run_phase(IO_C34, [build_c34], _maps(IO_C34, shared, percore, extra))
    for core in range(8):
        extra[core]["ffn_hbm"] = r_c34[core]["ffn_hbm"]
    r_c5 = run_phase(IO_C5, [build_c5], _maps(IO_C5, shared, percore, extra))
    out = np.empty((BATCH, SEQ, H), np.float32)
    for core in range(8):
        b, q = core // 4, core % 4
        out[b, q * TPC:(q + 1) * TPC] = r_c5[core]["out_hbm"]
    return out
